# version 32
# baseline (speedup 1.0000x reference)
import math
from contextlib import ExitStack

import numpy as np
import concourse.bass as bass
import concourse.mybir as mybir
from concourse.bass_utils import run_bass_kernel_spmd

F32 = mybir.dt.float32
BF16 = mybir.dt.bfloat16
AF = mybir.ActivationFunctionType
ALU = mybir.AluOpType
AX = mybir.AxisListType

D, T, NCH = 1024, 2048, 8
H, DH, MRF = 16, 64, 256
NE, FD = 8, 512
NCORES = 8

PI_F = float(np.float32(np.pi))
TWO_PI = float(np.float32(2 * np.pi))
INV2PI = float(np.float32(1.0 / (2 * np.pi)))
MAGIC = float(np.float32(12582912.0))
SCALE_2PI = float(np.float32(6.2831783))
BIG = 1.0e20
LNSM = float(np.float32(math.log(math.sqrt(MRF))))
EPS = 1.0e-5
f32 = np.float32

SEM_OWNER = {"M": "tensor", "V": "vector", "S": "scalar", "G": "gpsimd"}


class Emitter:
    ENGINES = ("tensor", "vector", "scalar", "gpsimd", "sync")

    def __init__(self, nc, ctx):
        self.nc, self.ctx = nc, ctx
        self.ops = {e: [] for e in self.ENGINES}
        self.sems, self.cnt, self.lastwait = {}, {}, {}

    def sem(self, name):
        self.sems[name] = self.ctx.enter_context(self.nc.semaphore(name))
        self.cnt[name] = 0

    def wait(self, eng, semname, val, force=False):
        if val is None or val <= 0 or (SEM_OWNER.get(semname) == eng and not force):
            return
        key = (eng, semname)
        if self.lastwait.get(key, -1) >= val:
            return
        self.lastwait[key] = val
        h = self.sems[semname]
        self.ops[eng].append(lambda en, h=h, v=val: en.wait_ge(h, v))

    def op(self, eng, fn, inc=None, incby=1):
        if inc is None:
            self.ops[eng].append(fn)
            return None
        h = self.sems[inc]
        self.cnt[inc] += incby
        v = self.cnt[inc]

        def run(en, fn=fn, h=h, incby=incby):
            fn(en).then_inc(h, incby)
        self.ops[eng].append(run)
        return v


class Buf:
    def __init__(self, em, name=""):
        self.em, self.name = em, name
        self.w = None
        self.r = []

    def wait_write(self, eng):
        if self.w:
            self.em.wait(eng, *self.w)

    def read_by(self, sem, cnt):
        if cnt is not None:
            self.r.append((sem, cnt))

    def wait_reads(self, eng):
        for (s, c) in self.r:
            self.em.wait(eng, s, c)
        if self.w:
            self.em.wait(eng, *self.w)

    def written(self, sem, cnt):
        if cnt is not None:
            self.w = (sem, cnt)
            self.r = []


def build_program(mode="full"):
    nc = bass.Bass("TRN2", target_bir_lowering=False, detect_race_conditions=False)

    x_in = nc.dram_tensor("x", [D, T], F32, kind="ExternalInput").ap()
    wq_d = nc.dram_tensor("wq", [D, D], F32, kind="ExternalInput").ap()
    wk_d = nc.dram_tensor("wk", [D, D], F32, kind="ExternalInput").ap()
    wv_d = nc.dram_tensor("wv", [D, D], F32, kind="ExternalInput").ap()
    wo_d = nc.dram_tensor("wo", [D, D], F32, kind="ExternalInput").ap()
    rf_d = nc.dram_tensor("rf", [MRF, DH], F32, kind="ExternalInput").ap()
    ws1_d = nc.dram_tensor("ws1", [D, FD], F32, kind="ExternalInput").ap()
    ws2_d = nc.dram_tensor("ws2", [FD, D], F32, kind="ExternalInput").ap()
    w1_d = nc.dram_tensor("w1", [NE, D, FD], F32, kind="ExternalInput").ap()
    w2_d = nc.dram_tensor("w2", [NE, FD, D], F32, kind="ExternalInput").ap()
    wr_d = nc.dram_tensor("wr", [D, NE], F32, kind="ExternalInput").ap()
    cst_d = nc.dram_tensor("cst", [128, 1152], F32, kind="ExternalInput").ap()
    out_d = nc.dram_tensor("out", [D, T], F32, kind="ExternalOutput").ap()

    ctx = ExitStack()
    SB = lambda name, shape, dt: ctx.enter_context(nc.sbuf_tensor(name, shape, dt))
    PSA = lambda name, shape: ctx.enter_context(nc.psum_tensor(name, shape, F32))

    em = Emitter(nc, ctx)
    for s in ("dW", "dX", "dG", "M", "V", "S", "G"):
        em.sem(s)

    x_sb = SB("x_sb", [128, NCH * T], F32)
    ident_bf = SB("ident_bf", [128, 128], BF16)
    rfT = SB("rfT", [128, 2 * MRF], BF16)
    rf_raw = SB("rf_raw", [128, 2 * DH], BF16)
    ones_row = SB("ones_row", [1, 512], BF16)
    ones_col = SB("ones_col", [128, 1], BF16)
    e0_col = SB("e0_col", [128, 2], BF16)
    e1_col = SB("e1_col", [128, 2], BF16)
    pairsel = SB("pairsel", [128, 2], BF16)
    maskmat = SB("maskmat", [8, 8 * 128], BF16)
    iota8 = SB("iota8", [128, 8], F32)
    wrb = SB("wrb", [128, NCH * 8], BF16)
    m_sl = SB("m_sl", [1, 512], F32)
    t_sl = SB("t_sl", [1, 512], F32)
    var_sl = SB("var_sl", [1, 512], F32)
    a_sl = SB("a_sl", [1, 512], F32)
    abf_sl = SB("abf_sl", [1, 512], BF16)
    bbf_sl = SB("bbf_sl", [1, 512], BF16)
    lt_sb = SB("lt_sb", [128, 128], F32)
    rs_sb = SB("rs_sb", [128, 128], F32)
    max1 = SB("max1", [128, 16], F32)
    idm = SB("idm", [128, 16], F32)
    max2 = SB("max2", [128, 16], F32)
    t1_sb = SB("t1_sb", [128, 16], F32)
    pe_tm = SB("pe_tm", [128, 128], F32)
    pe_bf = SB("pe_bf", [128, 128], BF16)
    sk_tm = SB("sk_tm", [128, 16 * H], F32)
    sq_tm = SB("sq_tm", [128, 16 * H], F32)
    skb_tm = SB("skb_tm", [128, 16 * H], BF16)
    w_tm = SB("w_tm", [128, 16 * H], BF16)
    wsc_f = SB("wsc_f", [128, 32], F32)
    NTRIG = 4
    nscr_t = SB("nscr", [128, 2048], BF16)
    dscr_t = SB("dscr", [128, 2048], BF16)
    nscr = [nscr_t[:, i * 512: min(i * 512 + 1024, 2048)] for i in range(NTRIG)]
    dscr = [dscr_t[:, i * 512: min(i * 512 + 1024, 2048)] for i in range(NTRIG)]
    iscr_s = SB("iscr", [128, 512], mybir.dt.int32)
    iscr = [iscr_s] * NTRIG
    vas_sb = [SB(f"vas{i}", [128, 416], BF16) for i in range(2)]

    ACOLS = 57472
    arena = SB("arena", [128, ACOLS], BF16)

    def ar(c0, cols):
        return arena[:, c0:c0 + cols]

    UB = ar(0, NCH * T)
    LN_HOME_A = ((ar(51200, 2048), ar(53248, 2048)),
                 [ar(55296, 512), ar(55808, 512)],
                 [ar(56320, 512), ar(56832, 512)])
    LN_HOME_B = ((ar(24576, 2048), ar(28736, 2048)),
                 [ar(26624, 512), ar(27136, 512)],
                 [ar(27648, 512), ar(28160, 512)])
    HS = [ar(16384 + i * 4096, 4096) for i in range(3)]
    W1S = [ar(28672 + i * 4096, 4096) for i in range(2)]
    W2S = [ar(36864 + i * 4096, 4096) for i in range(3)]
    RBUF = [ar(49152 + i * 1024, 1024) for i in range(2)]
    LOGB = ar(51200, T)
    INEGB = ar(53248, T)
    A_WL = ar(16384, 8192)
    A_WT = ar(24576, 8192)
    A_V = ar(32832, 16384)

    def A_K(c):
        if c < 4:
            return ar(16384 + c * T, T)
        return ar(49216 + (c - 4) * T, T)
    A_KV = ar(24576, H * 4 * 65)
    A_KF = [ar(28736, 2048), ar(30784, 2048)]
    A_NSQ = ar(28736, 512)
    A_VAS = vas_sb
    A_ZST = ar(28736, 2048)
    A_ZFM = ar(30784, 2048)
    A_Q = ar(32832, 16384)
    A_WL_Q = ar(16384, 8192)
    A_WT_Q = ar(49216, 8192)
    A_QFU = ar(16384, 8192)
    A_WO = ar(16384, 8192)
    A_WOT = ar(49216, 8192)

    PB = [PSA(f"pb{i}", [128, 1024]) for i in range(4)]
    pbuf = [Buf(em, f"pb{i // 2}.{i % 2}") for i in range(8)]

    def pb(i, half, cols=512):
        return PB[i][:, half * 512: half * 512 + cols]

    def pk(i, half):
        return pbuf[2 * i + half]

    class Two:
        def __init__(self, i):
            self.a, self.b = pbuf[2 * i], pbuf[2 * i + 1]

        def wait_write(self, eng):
            self.a.wait_write(eng); self.b.wait_write(eng)

        def read_by(self, sem, cnt):
            self.a.read_by(sem, cnt); self.b.read_by(sem, cnt)

        def wait_reads(self, eng):
            self.a.wait_reads(eng); self.b.wait_reads(eng)

        def written(self, sem, cnt):
            self.a.written(sem, cnt); self.b.written(sem, cnt)

    def pk2(i):
        return Two(i)

    def pb2(i):
        return PB[i][:, :]

    B_ub = Buf(em, "ub")
    B_x = [Buf(em, f"x{c}") for c in range(NCH)]
    B_sq = [Buf(em, "sq0"), Buf(em, "sq1")]
    B_ab = [Buf(em, "ab0"), Buf(em, "ab1")]
    B_lnsl = Buf(em, "lnsl")
    B_hs = [Buf(em, f"hs{i}") for i in range(3)]
    B_w1 = [Buf(em, "w1a"), Buf(em, "w1b")]
    B_w2 = [Buf(em, f"w2{i}") for i in range(3)]
    B_rb = [Buf(em, "rb0"), Buf(em, "rb1")]
    B_logb = Buf(em, "logb")
    B_inegb = Buf(em, "inegb")
    B_wrb = Buf(em, "wrb")
    B_rfT = Buf(em, "rfT")
    B_wl = Buf(em, "wl")
    B_wt = Buf(em, "wt")
    B_v = Buf(em, "v")
    B_k = Buf(em, "k")
    B_kv = Buf(em, "kv")
    B_kf = [Buf(em, "kf0"), Buf(em, "kf1")]
    B_nsq = Buf(em, "nsq")
    B_vas = [Buf(em, "vas0"), Buf(em, "vas1")]
    B_q = Buf(em, "q")
    B_qfu = Buf(em, "qfu")
    B_zst = None
    B_zfm = Buf(em, "zfm")
    B_sk = Buf(em, "sk")
    B_skb = Buf(em, "skb")
    B_sqt = Buf(em, "sqt")
    B_wtm = Buf(em, "wtm")
    B_trig = [Buf(em, f"trig{i}") for i in range(4)]
    trig_slot = [0]
    B_rt = Buf(em, "routing")
    B_zst = Buf(em, "zst")

    def barrier(tag=""):
        for eng in Emitter.ENGINES:
            for sname in ("dW", "dX", "dG", "M", "V", "S"):
                em.wait(eng, sname, em.cnt[sname])

    def vh(fn):
        c = em.op("vector", fn, inc="V")
        em.wait("vector", "V", c, force=True)
        return c

    def mm(out_ap, lhsT_ap, rhs_ap, start, stop, inc=None, tp=None):
        def fn(en, o=out_ap, l=lhsT_ap, r=rhs_ap, st=start, sp=stop, tp=tp):
            return en.matmul(o, l, r, start=st, stop=sp, tile_position=tp)
        return em.op("tensor", fn, inc=inc)

    def scopy(dst_ap, src_ap, inc="S", scale=None, func=AF.Copy, bias=None):
        kw = {}
        if scale is not None:
            kw["scale"] = f32(scale)
        if bias is not None:
            kw["bias"] = f32(bias)

        def fn(en, d=dst_ap, s=src_ap, func=func, kw=kw):
            return en.activation(d, s, func, **kw)
        return em.op("scalar", fn, inc=inc)

    em.op("gpsimd", lambda g: g.dma_start(ident_bf[:], cst_d[:, 0:128]), inc="dG", incby=16)
    em.op("gpsimd", lambda g: g.dma_start(maskmat[:], cst_d[0:8, 128:1152]), inc="dG", incby=16)
    rf_r = rf_d.rearrange("(c p) d -> c p d", p=128)
    em.op("gpsimd", lambda g: g.dma_start(rf_raw[:, 0:DH], rf_r[0]), inc="dG", incby=16)
    c_rf = em.op("gpsimd", lambda g: g.dma_start(rf_raw[:, DH:2 * DH], rf_r[1]),
                 inc="dG", incby=16)

    x_r = x_in.rearrange("(c p) t -> p c t", p=128)
    for hf in range(2):
        src = x_r[:, 4 * hf:4 * hf + 4, :]
        dst = x_sb[:, hf * 4 * T:(hf + 1) * 4 * T].rearrange("p (c t) -> p c t", c=4)
        cx = em.op("sync", lambda sy, d=dst, s=src: sy.dma_start(d, s),
                   inc="dX", incby=16)
        for c in range(4 * hf, 4 * hf + 4):
            B_x[c].written("dX", cx)

    for fn in (
        lambda v: v.memset(ones_row[:], 1.0),
        lambda v: v.memset(ones_col[:], 1.0),
        lambda v: v.memset(e0_col[:, 0:1], 1.0),
        lambda v: v.memset(e0_col[:, 1:2], 0.0),
        lambda v: v.memset(e1_col[:, 0:1], 0.0),
        lambda v: v.memset(e1_col[:, 1:2], 1.0),
        lambda v: v.memset(pairsel[:], 0.0),
        lambda v: v.memset(pairsel[0:64, 0:1], 1.0),
        lambda v: v.memset(pairsel[64:128, 1:2], 1.0),
    ):
        em.op("vector", fn)
    c_consts = None
    for e in range(8):
        c_consts = em.op("vector", lambda v, e=e: v.memset(iota8[:, e:e + 1], float(e)),
                         inc="V")

    vz = em.op("vector", lambda v: v.memset(rfT[:], 0.0), inc="V")
    em.wait("tensor", "dG", c_rf)
    pk2(0).wait_reads("tensor")
    c_t = None
    for mc in range(2):
        mm(PB[0][0:64, mc * 128:(mc + 1) * 128],
           rf_raw[:, mc * DH:(mc + 1) * DH], ident_bf[:], True, True)
        c_t = mm(PB[0][64:128, mc * 128:(mc + 1) * 128],
                 rf_raw[:, mc * DH:(mc + 1) * DH], ident_bf[:], True, True, inc="M")
    pk2(0).written("M", c_t)
    pk2(0).wait_write("scalar")
    em.wait("scalar", "V", vz)
    scopy(rfT[0:64, 0:256], PB[0][0:64, 0:256], scale=INV2PI)
    c_rfT = scopy(rfT[64:128, 256:512], PB[0][64:128, 0:256], scale=INV2PI)
    pk2(0).read_by("S", c_rfT)
    B_rfT.written("S", c_rfT)

    def emit_layernorm(home=None):
        sqs, abuf, bbuf = home if home else LN_HOME_A
        sq_scr = [sqs[0], sqs[1]]
        stA = [pb(0, 0), pb(0, 1), pb(1, 0), pb(1, 1)]
        stB = [pb(2, 0), pb(2, 1), pb(3, 0), pb(3, 1)]
        kA = [pk(0, 0), pk(0, 1), pk(1, 0), pk(1, 1)]
        kB = [pk(2, 0), pk(2, 1), pk(3, 0), pk(3, 1)]
        for b in kA + kB:
            b.wait_reads("tensor")
        for c in range(NCH):
            B_x[c].wait_write("vector")
            B_sq[c % 2].wait_reads("vector")
            if c < 2:
                B_ub.wait_reads("scalar")
                B_ub.wait_reads("vector")
                B_logb.wait_reads("vector")
                B_kv.wait_reads("vector")
            xb_ap = UB[:, c * T:(c + 1) * T]
            em.op("vector", lambda v, a=xb_ap, c=c: v.tensor_copy(
                a, x_sb[:, c * T:(c + 1) * T]))
            vs = em.op("vector", lambda v, a=xb_ap, c=c: v.tensor_mul(
                sq_scr[c % 2][:], a, a), inc="V")
            em.wait("tensor", "V", vs)
            last = None
            for tt in range(4):
                mm(stA[tt][0:1, :], ones_col[:, 0:1],
                   xb_ap[:, tt * 512:(tt + 1) * 512], c == 0, c == NCH - 1)
                last = mm(stB[tt][0:1, :], ones_col[:, 0:1],
                          sq_scr[c % 2][:, tt * 512:(tt + 1) * 512],
                          c == 0, c == NCH - 1,
                          inc="M" if tt == 3 else None)
            B_sq[c % 2].written("V", vs)
            B_sq[c % 2].read_by("M", last)
        for b in kA + kB:
            b.written("M", em.cnt["M"])
        uv = None
        for tt in range(4):
            kA[tt].wait_write("scalar")
            kB[tt].wait_write("vector")
            B_lnsl.wait_reads("scalar")
            em.op("scalar", lambda s, st=stA[tt]: s.activation(
                m_sl[:], st[0:1, :], AF.Identity, scale=f32(1.0 / D)))
            sm2 = em.op("scalar", lambda s: s.activation(
                t_sl[:], m_sl[:], AF.Square), inc="S")
            kA[tt].read_by("S", sm2)
            em.wait("vector", "S", sm2)
            B_lnsl.wait_reads("vector")
            vv = em.op("vector", lambda v, st=stB[tt]: v.scalar_tensor_tensor(
                var_sl[:], st[0:1, :], 1.0 / D, t_sl[:], ALU.mult,
                ALU.subtract), inc="V")
            kB[tt].read_by("V", vv)
            em.wait("scalar", "V", vv)
            em.op("scalar", lambda s: s.activation(
                t_sl[:], var_sl[:], AF.Ln, bias=f32(EPS)))
            sv = em.op("scalar", lambda s: s.activation(
                a_sl[:], t_sl[:], AF.Exp, scale=f32(-0.5)), inc="S")
            em.wait("vector", "S", sv)
            em.op("vector", lambda v: v.tensor_copy(abf_sl[:], a_sl[:]))
            vb = em.op("vector", lambda v: v.scalar_tensor_tensor(
                bbf_sl[:], m_sl[:], -1.0, a_sl[:], ALU.mult, ALU.mult),
                inc="V")
            B_lnsl.written("V", vb)
            B_lnsl.read_by("V", vb)
            em.wait("tensor", "V", vb)
            kA[tt].wait_reads("tensor")
            kB[tt].wait_reads("tensor")
            mm(stA[tt], ones_row[0:1, 0:128], abf_sl[0:1, :], True, True)
            mab = mm(stB[tt], ones_row[0:1, 0:128], bbf_sl[0:1, :], True, True,
                     inc="M")
            kA[tt].written("M", mab)
            kB[tt].written("M", mab)
            B_lnsl.read_by("M", mab)
            kA[tt].wait_write("scalar")
            kB[tt].wait_write("scalar")
            B_ab[tt % 2].wait_reads("scalar")
            em.op("scalar", lambda s, tt=tt: s.copy(abuf[tt % 2][:], stA[tt]))
            sab = em.op("scalar", lambda s, tt=tt: s.copy(bbuf[tt % 2][:], stB[tt]),
                        inc="S")
            kA[tt].read_by("S", sab)
            kB[tt].read_by("S", sab)
            B_ab[tt % 2].written("S", sab)
            em.wait("vector", "S", sab)
            for c in range(NCH):
                u_ap = UB[:, c * T + tt * 512: c * T + (tt + 1) * 512]
                em.op("vector", lambda v, u=u_ap, tt=tt: v.tensor_mul(
                    u, u, abuf[tt % 2][:]))
                uv = em.op("vector", lambda v, u=u_ap, tt=tt: v.tensor_add(
                    u, u, bbuf[tt % 2][:]), inc="V")
            B_ab[tt % 2].read_by("V", uv)
        B_ub.written("V", uv)

    def emit_routing():
        wr_r = wr_d.rearrange("(c p) e -> p c e", p=128)
        B_wrb.wait_reads("gpsimd")
        cw = em.op("gpsimd", lambda g: g.dma_start(
            wrb[:].rearrange("p (c e) -> p c e", c=NCH), wr_r), inc="dW", incby=16)
        B_wrb.written("dW", cw)
        B_ub.wait_write("tensor")
        B_wrb.wait_write("tensor")
        se = None
        for tt in range(4):
            lgb = pk(3, tt % 2)
            lga = pb(3, tt % 2)
            lgb.wait_reads("tensor")
            mq = None
            for c in range(NCH):
                mq = mm(lga[0:8, :], wrb[:, c * 8:(c + 1) * 8],
                        UB[:, c * T + tt * 512: c * T + (tt + 1) * 512],
                        c == 0, c == NCH - 1, inc="M" if c == NCH - 1 else None)
            lgb.written("M", mq)
            B_ub.read_by("M", mq)
            B_wrb.read_by("M", mq)
            lgb.wait_write("scalar")
            B_logb.wait_reads("scalar")
            se = em.op("scalar", lambda s, tt=tt, lga=lga: s.copy(
                LOGB[0:8, tt * 512:(tt + 1) * 512], lga[0:8, :]), inc="S")
            lgb.read_by("S", se)
        B_logb.written("S", se)
        B_logb.wait_write("tensor")
        pk(2, 0).wait_reads("tensor")
        mlt = None
        for tch in range(16):
            mlt = mm(pb(2, 0)[:, tch * 8:(tch + 1) * 8],
                     LOGB[0:8, tch * 128:(tch + 1) * 128], ident_bf[0:8, 0:8],
                     True, True, inc="M" if tch == 15 else None)
        pk(2, 0).written("M", mlt)
        B_logb.read_by("M", mlt)
        pk(2, 0).wait_write("vector")
        B_rt.wait_reads("vector")
        vl = em.op("vector", lambda v: v.tensor_copy(lt_sb[:], pb(2, 0)[:, 0:128]),
                   inc="V")
        pk(2, 0).read_by("V", vl)
        lt3 = lt_sb[:].rearrange("p (g e) -> p g e", e=8)
        rs3 = rs_sb[:].rearrange("p (g e) -> p g e", e=8)
        pe3 = pe_tm[:].rearrange("p (g e) -> p g e", e=8)
        m1b = max1[:].unsqueeze(2).to_broadcast([128, 16, 8])
        idb = idm[:].unsqueeze(2).to_broadcast([128, 16, 8])
        t1b = t1_sb[:].unsqueeze(2).to_broadcast([128, 16, 8])
        io3 = iota8[:].unsqueeze(1).to_broadcast([128, 16, 8])
        em.wait("vector", "V", c_consts)
        vh(lambda v: v.tensor_reduce(max1[:], lt3, AX.X, ALU.max))
        vh(lambda v: v.tensor_tensor(rs3, lt3, m1b, ALU.is_equal))
        vh(lambda v: v.scalar_tensor_tensor(
            rs3, rs3, -8.0, io3, ALU.mult, ALU.add))
        vh(lambda v: v.tensor_reduce(idm[:], rs3, AX.X, ALU.min))
        vh(lambda v: v.tensor_tensor(pe3, rs3, idb, ALU.is_equal))
        vh(lambda v: v.scalar_tensor_tensor(
            rs3, pe3, -1e30, lt3, ALU.mult, ALU.add))
        vh(lambda v: v.tensor_reduce(max2[:], rs3, AX.X, ALU.max))
        vg = vh(lambda v: v.tensor_sub(max2[:], max2[:], max1[:]))
        em.wait("scalar", "V", vg)
        sg = em.op("scalar", lambda s: s.activation(max2[:], max2[:], AF.Exp),
                   inc="S")
        em.wait("vector", "S", sg)
        vh(lambda v: v.tensor_scalar(
            max2[:], max2[:], 1.0, None, ALU.add))
        vh(lambda v: v.reciprocal(t1_sb[:], max2[:]))
        vh(lambda v: v.tensor_scalar(
            t1_sb[:], t1_sb[:], 1.0, None, ALU.is_ge))
        vh(lambda v: v.tensor_tensor(pe3, pe3, t1b, ALU.mult))
        vh(lambda v: v.tensor_scalar(
            pe_tm[:], pe_tm[:], -1.0, 1.0, ALU.mult, ALU.add))
        vp = vh(lambda v: v.tensor_copy(pe_bf[:], pe_tm[:]))
        B_rt.written("V", vp)
        em.wait("tensor", "V", vp)
        se2 = None
        for tch in range(16):
            pk(2, 1).wait_reads("tensor")
            mtb = mm(pb(2, 1)[0:8, 0:128], pe_bf[:, tch * 8:(tch + 1) * 8],
                     ident_bf[:, 0:128], True, True, inc="M")
            pk(2, 1).written("M", mtb)
            B_rt.read_by("M", mtb)
            pk(2, 1).wait_write("scalar")
            if tch == 0:
                B_inegb.wait_reads("scalar")
            se2 = em.op("scalar", lambda s, tch=tch: s.copy(
                INEGB[0:8, tch * 128:(tch + 1) * 128], pb(2, 1)[0:8, 0:128]),
                inc="S")
            pk(2, 1).read_by("S", se2)
        B_inegb.written("S", se2)


    def emit_ffn(ffn_idx, emit_out_tail=False, max_groups=3, max_th=2,
                 skip_routing=False):
        if not skip_routing:
            emit_routing()
        eseq = [8, 0, 1, 2, 3, 4, 5, 6, 7]
        w1_stream = [(th, e) for th in range(max_th)
                     for e in eseq[:3 * max_groups]]

        def load_w1(idx):
            th, e = w1_stream[idx]
            slot = idx % 2
            B_w1[slot].wait_reads("gpsimd")
            src = (ws1_d if e == 8 else w1_d[e]).rearrange("(c p) f -> p c f", p=128)
            dst = W1S[slot].rearrange("p (c f) -> p c f", c=NCH)
            cw = em.op("gpsimd", lambda g, s=src, d=dst: g.dma_start(d, s),
                       inc="dW", incby=16)
            B_w1[slot].written("dW", cw)

        def load_w2_group(th, gi):
            for k in range(3):
                e = eseq[3 * gi + k]
                B_w2[k].wait_reads("gpsimd")
                src = (ws2_d if e == 8 else w2_d[e]).rearrange(
                    "(c p) f -> p c f", p=128)
                dst = W2S[k].rearrange("p (c f) -> p c f", c=4)
                cw = em.op("gpsimd", lambda g, s=src, d=dst: g.dma_start(d, s),
                           inc="dW", incby=16)
                B_w2[k].written("dW", cw)

        load_w1(0)
        load_w1(1)
        load_w2_group(0, 0)
        for th in range(max_th):
            for gi in range(max_groups):
                grp = eseq[3 * gi:3 * gi + 3]
                for k, e in enumerate(grp):
                    idx = th * 3 * max_groups + 3 * gi + k
                    w1slot = idx % 2
                    hs_ap = HS[k]
                    B_w1[w1slot].wait_write("tensor")
                    B_ub.wait_write("tensor")
                    use_mask = (e != 8 and not skip_routing)
                    vh = None
                    for f in range(4):
                        pB = pk2(f % 2)
                        pA = pb2(f % 2)
                        pB.wait_reads("tensor")
                        for dc in range(NCH):
                            lhs = W1S[w1slot][:, dc * 512 + f * 128:
                                              dc * 512 + (f + 1) * 128]
                            for nn in range(2):
                                lastw1 = (dc == NCH - 1 and not use_mask)
                                cmz = mm(pA[:, nn * 512:(nn + 1) * 512], lhs, UB[:, dc * T + th * 1024 + nn * 512:
                                         dc * T + th * 1024 + (nn + 1) * 512],
                                   dc == 0, lastw1,
                                   inc="M" if (lastw1 and nn == 1) else None)
                        if use_mask:
                            B_inegb.wait_write("tensor")
                            for nn in range(2):
                                cmz = mm(pA[:, nn * 512:(nn + 1) * 512],
                                         maskmat[0:8, e * 128:(e + 1) * 128],
                                         INEGB[0:8, th * 1024 + nn * 512:
                                               th * 1024 + (nn + 1) * 512],
                                         False, True, inc="M" if nn == 1 else None)
                        pB.written("M", cmz)
                        if use_mask:
                            B_inegb.read_by("M", cmz)
                        B_w1[w1slot].read_by("M", cmz)
                        B_ub.read_by("M", cmz)
                        pB.wait_write("scalar")
                        B_rb[f % 2].wait_reads("scalar")
                        sr = em.op("scalar", lambda s, pA=pA, f=f: s.activation(
                            RBUF[f % 2][:], pA, AF.Relu), inc="S")
                        pB.read_by("S", sr)
                        B_rb[f % 2].written("S", sr)
                        em.wait("vector", "S", sr)
                        B_hs[k].wait_reads("vector")
                        vh = em.op("vector", lambda v, hs=hs_ap, f=f: v.tensor_mul(
                            hs[:, f * 1024:(f + 1) * 1024],
                            RBUF[f % 2][:], RBUF[f % 2][:]), inc="V")
                        B_rb[f % 2].read_by("V", vh)
                    B_hs[k].written("V", vh)
                    if idx + 2 < len(w1_stream):
                        load_w1(idx + 2)
                for k in range(3):
                    B_w2[k].wait_write("tensor")
                    B_hs[k].wait_write("tensor")
                for j in range(NCH):
                    for nn in range(2):
                        pB = pk(2, (2 * j + nn) % 2)
                        pA = pb(2, (2 * j + nn) % 2)
                        pB.wait_reads("tensor")
                        cmb = None
                        for k in range(3):
                            for f in range(4):
                                lhs = W2S[k][:, f * 1024 + j * 128:
                                             f * 1024 + (j + 1) * 128]
                                rhs = HS[k][:, f * 1024 + nn * 512:
                                            f * 1024 + (nn + 1) * 512]
                                cmb = mm(pA, lhs, rhs, (k == 0 and f == 0),
                                         (k == 2 and f == 3),
                                         inc="M" if (k == 2 and f == 3) else None)
                        pB.written("M", cmb)
                        for k in range(3):
                            B_w2[k].read_by("M", cmb)
                            B_hs[k].read_by("M", cmb)
                        pB.wait_write("vector")
                        B_x[j].wait_write("vector")
                        x_ap = x_sb[:, j * T + th * 1024 + nn * 512:
                                    j * T + th * 1024 + (nn + 1) * 512]
                        vx = em.op("vector", lambda v, pA=pA, x_ap=x_ap:
                                   v.scalar_tensor_tensor(
                                       x_ap, pA, 0.5, x_ap, ALU.mult, ALU.add),
                                   inc="V")
                        pB.read_by("V", vx)
                        B_x[j].written("V", vx)
                        if emit_out_tail and th == 1 and gi == 2:
                            if nn == 1:
                                emit_out_chunk(j)
                if not (th == 1 and gi == 2):
                    ngi, nth = (gi + 1, th) if gi < 2 else (0, th + 1)
                    load_w2_group(nth, ngi)

    out_r = out_d.rearrange("(c p) t -> c p t", p=128)
    odma = []

    def emit_out_chunk(c):
        B_x[c].wait_write("sync")
        co = em.op("sync", lambda sy, c=c: sy.dma_start(
            out_r[c], x_sb[:, c * T:(c + 1) * T]), inc="dX", incby=16)
        odma.append(co)

    def finish_output():
        em.wait("sync", "dX", em.cnt["dX"])

    def emit_output_from_ub():
        B_ub.wait_write("gpsimd")
        for c in range(NCH):
            co = em.op("gpsimd", lambda g, c=c: g.dma_start(
                out_r[c], UB[:, c * T:(c + 1) * T]), inc="dG", incby=16)
        em.wait("gpsimd", "dG", em.cnt["dG"])

    def load_and_transpose(w_dram, wl_ap, wt_ap):
        B_wl.wait_reads("gpsimd")
        src = w_dram.rearrange("(c p) d -> p c d", p=128)
        dst = wl_ap.rearrange("p (c d) -> p c d", c=NCH)
        cw = em.op("gpsimd", lambda g, s=src, d=dst: g.dma_start(d, s),
                   inc="dW", incby=16)
        B_wl.written("dW", cw)
        B_wl.wait_write("tensor")
        B_wt.wait_reads("tensor")
        B_wt.wait_reads("scalar")
        sq_ = None
        for dc in range(NCH):
            for icq in range(2):
                pB = pk(3, icq)
                pA = pb(3, icq)
                pB.wait_reads("tensor")
                mq = None
                for ici in range(4):
                    ic = icq * 4 + ici
                    lhs = wl_ap[:, ic * 1024 + dc * 128: ic * 1024 + (dc + 1) * 128]
                    mq = mm(pA[:, ici * 128:(ici + 1) * 128], lhs, ident_bf[:],
                            True, True, inc="M" if ici == 3 else None)
                pB.written("M", mq)
                B_wl.read_by("M", mq)
                pB.wait_write("vector")
                sq_ = em.op("vector", lambda v, pA=pA, dc=dc, icq=icq:
                            v.tensor_copy(
                    wt_ap[:, dc * 1024 + icq * 512: dc * 1024 + (icq + 1) * 512],
                    pA), inc="V")
                pB.read_by("V", sq_)
        B_wt.written("V", sq_)

    def proj_fm(wt_ap, dst_fn, bdst):
        B_wt.wait_write("tensor")
        B_ub.wait_write("tensor")
        bdst.wait_reads("tensor")
        bdst.wait_reads("scalar")
        last_s = None
        groups = [[(3, 0), (3, 1), (2, 0), (2, 1)], [(1, 0), (1, 1), (0, 0), (0, 1)]]
        for ic in range(NCH):
            tiles = groups[ic % 2]
            for (q, hq) in tiles:
                pk(q, hq).wait_reads("tensor")
            mq = None
            for dc in range(NCH):
                lhs = wt_ap[:, dc * 1024 + ic * 128: dc * 1024 + (ic + 1) * 128]
                for tt in range(4):
                    q, hq = tiles[tt]
                    rhs = UB[:, dc * T + tt * 512: dc * T + (tt + 1) * 512]
                    mq = mm(pb(q, hq), lhs, rhs, dc == 0, dc == NCH - 1,
                            inc="M" if (dc == NCH - 1 and tt == 3) else None)
            B_wt.read_by("M", mq)
            B_ub.read_by("M", mq)
            for tt in range(4):
                q, hq = tiles[tt]
                pk(q, hq).written("M", mq)
                pk(q, hq).wait_write("vector")
                last_s = em.op("vector", lambda v, q=q, hq=hq, ic=ic, tt=tt:
                               v.tensor_copy(
                    dst_fn(ic)[:, tt * 512:(tt + 1) * 512], pb(q, hq)), inc="V")
                pk(q, hq).read_by("V", last_s)
        bdst.written("V", last_s)

    def proj_tm(wt_ap, dst_ap, bdst):
        B_wt.wait_write("tensor")
        B_ub.wait_write("tensor")
        bdst.wait_reads("tensor")
        bdst.wait_reads("scalar")
        last_s = None
        for tch in range(16):
            q = 3 if tch % 2 == 0 else 2
            pB = pk2(q)
            pA = pb2(q)
            pB.wait_reads("tensor")
            mq = None
            for dc in range(NCH):
                lhs = UB[:, dc * T + tch * 128: dc * T + (tch + 1) * 128]
                for ih in range(2):
                    rhs = wt_ap[:, dc * 1024 + ih * 512: dc * 1024 + (ih + 1) * 512]
                    mq = mm(pA[:, ih * 512:(ih + 1) * 512], lhs, rhs,
                            dc == 0, dc == NCH - 1,
                            inc="M" if (dc == NCH - 1 and ih == 1) else None)
            pB.written("M", mq)
            B_wt.read_by("M", mq)
            B_ub.read_by("M", mq)
            pB.wait_write("vector")
            last_s = em.op("vector", lambda v, pA=pA, tch=tch: v.tensor_copy(
                dst_ap[:, tch * 1024:(tch + 1) * 1024], pA), inc="V")
            pB.read_by("V", last_s)
        bdst.written("V", last_s)

    def emit_norms(src_fn, bsrc, dst_sl, bdst):
        psn = pb(2, 0)[:, 0:256]
        pk(2, 0).wait_reads("tensor")
        bsrc.wait_write("vector")
        mq = None
        for c in range(NCH):
            for hb in range(4):
                B_nsq.wait_reads("vector")
                vn = em.op("vector", lambda v, c=c, hb=hb: v.tensor_mul(
                    A_NSQ[:], src_fn(c)[:, hb * 512:(hb + 1) * 512],
                    src_fn(c)[:, hb * 512:(hb + 1) * 512]), inc="V")
                B_nsq.written("V", vn)
                em.wait("tensor", "V", vn)
                for tq in range(4):
                    tch = hb * 4 + tq
                    mq = mm(psn[:, tch * 16 + 2 * c: tch * 16 + 2 * c + 2],
                            A_NSQ[:, tq * 128:(tq + 1) * 128], pairsel[:],
                            True, True,
                            inc="M" if tq == 3 else None)
                B_nsq.read_by("M", mq)
        pk(2, 0).written("M", mq)
        pk(2, 0).wait_write("scalar")
        bdst.wait_reads("scalar")
        last = None
        for tch in range(16):
            last = em.op("scalar", lambda s, tch=tch: s.activation(
                dst_sl[:, tch * H:(tch + 1) * H], psn[:, tch * 16:tch * 16 + 16],
                AF.Exp, scale=f32(-0.5), bias=f32(-LNSM)), inc="S")
        pk(2, 0).read_by("S", last)
        bdst.written("S", last)

    def trig_pipeline(pja, pjb, width, sin_dst, cos_dst):
        if width > 512:
            sl = 0 if trig_slot[0] in (0, 3) else 2
            trig_slot[0] = (sl + 2) % NTRIG
        else:
            sl = trig_slot[0]
            trig_slot[0] = (sl + 1) % NTRIG
        ns, ds, isc = nscr[sl], dscr[sl], iscr[sl]
        Bt = B_trig[sl]
        Bt.wait_reads("vector")
        pjb.wait_write("vector")
        for c0 in range(0, width, 512):
            em.op("vector", lambda v, pja=pja, c0=c0, isc=isc: v.tensor_scalar(
                isc[:, 0:512], pja[:, c0:c0 + 512], 8.5, None, ALU.add))
            em.op("vector", lambda v, pja=pja, c0=c0, ds=ds, isc=isc:
                  v.scalar_tensor_tensor(
                ds[:, c0:c0 + 512], pja[:, c0:c0 + 512], 8.5, isc[:, 0:512],
                ALU.add, ALU.subtract), inc="V")
        vd = em.op("vector", lambda v, w=width, ns=ns, ds=ds:
                   v.scalar_tensor_tensor(
            ns[:, 0:w], ds[:, 0:w], 0.5, ds[:, 0:w], ALU.is_ge,
            ALU.subtract), inc="V")
        pjb.read_by("V", vd)
        Bt.written("V", vd)
        em.wait("scalar", "V", vd)
        em.op("scalar", lambda s, sd=sin_dst, w=width, ns=ns: s.activation(
            sd, ns[:, 0:w], AF.Sin, scale=f32(-SCALE_2PI)))
        em.op("scalar", lambda s, w=width, ns=ns, ds=ds: s.activation(
            ds[:, 0:w], ns[:, 0:w], AF.Abs))
        st = em.op("scalar", lambda s, cd=cos_dst, w=width, ds=ds: s.activation(
            cd, ds[:, 0:w], AF.Sin, scale=f32(-SCALE_2PI), bias=f32(PI_F / 2)),
            inc="S")
        Bt.read_by("S", st)
        return st

    def emit_attention(stop_after=None):
        emit_layernorm()
        barrier("attn-start")
        load_and_transpose(wv_d, A_WL, A_WT)
        proj_tm(A_WT, A_V, B_v)
        if stop_after == "v":
            return
        barrier("pre-k")
        load_and_transpose(wk_d, A_WL, A_WT)
        proj_fm(A_WT, A_K, B_k)
        if stop_after == "k":
            return
        emit_norms(A_K, B_k, sk_tm, B_sk)
        if stop_after == "norms":
            return
        B_sk.wait_write("vector")
        B_skb.wait_reads("vector")
        vsk = em.op("vector", lambda v: v.tensor_copy(skb_tm[:], sk_tm[:]), inc="V")
        B_sk.read_by("V", vsk)
        B_skb.written("V", vsk)

        passes = [(0, 1), (2, 3), (4, 5), (6, 7)]
        pending_drain = [None, None]
        for pi, prs in enumerate(passes):
            npr = 2
            kvmap = [(0, 0), (0, 1), (1, 0), (1, 1)]
            kvb = [pk(*kvmap[k]) for k in range(4)]
            kva = [pb(*kvmap[k])[0:65, :] for k in range(4)]
            B_v.wait_write("tensor")
            B_skb.wait_write("vector")
            B_rfT.wait_write("tensor")
            h0 = 2 * prs[0]
            nh = 4
            mkv = [None] * 4

            def emit_vaug_proj_trig(tch):
                vsl = tch % 2
                B_vas[vsl].wait_reads("vector")
                B_v.wait_write("vector")
                vdst = A_VAS[vsl][:, 0:65 * nh]
                v3 = vdst.rearrange("p (h q) -> p h q", h=nh)[:, :, 0:64]
                s3 = skb_tm[:, tch * H + h0: tch * H + h0 + nh]\
                    .unsqueeze(2).to_broadcast([128, nh, 64])
                src3 = A_V[:, tch * 1024 + h0 * 64: tch * 1024 + (h0 + nh) * 64]\
                    .rearrange("p (h q) -> p h q", h=nh)
                em.op("vector", lambda v, a=v3, b=src3, c=s3:
                      v.tensor_tensor(a, b, c, ALU.mult))
                vcol = vdst.rearrange("p (h q) -> p h q", h=nh)[:, :, 64:65]
                vva = em.op("vector", lambda v, a=vcol, tch=tch:
                            v.tensor_copy(a, skb_tm[:, tch * H + h0:
                                                    tch * H + h0 + nh].unsqueeze(2)),
                            inc="V")
                B_vas[vsl].written("V", vva)
                em.wait("tensor", "V", vva)
                pjq = 3 if tch % 2 == 0 else 2
                pjb = pk2(pjq)
                pja = pb2(pjq)
                pjb.wait_reads("tensor")
                mpj = None
                for ip, hp in enumerate(prs):
                    lhsE = A_K(hp)[:, tch * 128:(tch + 1) * 128]
                    mpj = mm(pja[:, ip * 512:(ip + 1) * 512], lhsE, rfT[:, 0:512],
                             True, True, inc="M" if ip == 1 else None)
                B_k.read_by("M", mpj)
                pjb.written("M", mpj)
                kfsl = tch % 2
                kf = A_KF[kfsl]
                B_kf[kfsl].wait_reads("vector")
                B_kf[kfsl].wait_reads("scalar")
                st = None
                for ip in range(2):
                    kf2 = kf[:, ip * 1024:(ip + 1) * 1024]\
                        .rearrange("p (h q) -> p h q", h=2)
                    st = trig_pipeline(pb(pjq, ip), pk(pjq, ip), 512,
                                       kf2[:, :, 256:512], kf2[:, :, 0:256])
                B_kf[kfsl].written("S", st)

            def emit_kv(tch):
                vsl = tch % 2
                kfsl = tch % 2
                kf = A_KF[kfsl]
                B_kf[kfsl].wait_write("tensor")
                B_vas[vsl].wait_write("tensor")
                for k in range(4):
                    lhsV = A_VAS[vsl][:, k * 65: k * 65 + 65]
                    mkv[k] = mm(kva[k], lhsV, kf[:, k * 512:(k + 1) * 512],
                                tch == 0, tch == 15,
                                inc="M" if k == 3 else None)
                B_kf[kfsl].read_by("M", mkv[3])
                B_vas[vsl].read_by("M", mkv[3])

            drain_slots = []

            def drain_a(kvb=kvb, kva=kva, prs=prs, drain_slots=drain_slots):
                for ip2, hp2 in enumerate(prs):
                    for hh in range(2):
                        k = 2 * ip2 + hh
                        sl = trig_slot[0]
                        trig_slot[0] = (sl + 1) % NTRIG
                        nb = nscr[sl]
                        Bt = B_trig[sl]
                        kvb[k].wait_write("scalar")
                        Bt.wait_reads("scalar")
                        skv = em.op("scalar", lambda s, a=kva[k], nb=nb: s.copy(
                            nb[0:65, 0:512], a), inc="S")
                        kvb[k].read_by("S", skv)
                        Bt.written("S", skv)
                        drain_slots.append((sl, 2 * hp2 + hh))

            def drain_b(drain_slots=drain_slots):
                for ki, (sl, h) in enumerate(drain_slots):
                    nb = nscr[sl]
                    Bt = B_trig[sl]
                    psl = ki % 2
                    Bt.wait_write("tensor")
                    pk(3, psl).wait_reads("tensor")
                    mkt = None
                    for q in range(4):
                        mkt = mm(pb(3, psl)[:, q * 65:(q + 1) * 65],
                                 nb[0:65, q * 128:(q + 1) * 128],
                                 ident_bf[0:65, 0:65], True, True,
                                 inc="M" if q == 3 else None)
                    pk(3, psl).written("M", mkt)
                    Bt.read_by("M", mkt)
                    pk(3, psl).wait_write("scalar")
                    B_kv.wait_reads("scalar")
                    skv2 = em.op("scalar", lambda s, h=h, psl=psl: s.copy(
                        A_KV[:, h * 260:(h + 1) * 260], pb(3, psl)[:, 0:260]),
                        inc="S")
                    pk(3, psl).read_by("S", skv2)
                    B_kv.written("S", skv2)
                del drain_slots[:]

            if pending_drain[0] is not None:
                pending_drain[0]()
                pending_drain[1]()
            emit_vaug_proj_trig(0)
            emit_vaug_proj_trig(1)
            for b in kvb:
                b.wait_reads("tensor")
            for tch in range(16):
                if 1 <= tch < 15:
                    emit_vaug_proj_trig(tch + 1)
                emit_kv(tch)
            for k in range(4):
                kvb[k].written("M", em.cnt["M"])
            pending_drain = [drain_a, drain_b]
        pending_drain[0]()
        pending_drain[1]()

        if stop_after == "phaseA":
            return
        barrier("pre-q")
        load_and_transpose(wq_d, A_WL_Q, A_WT_Q)

        def qchunk(ic):
            return A_Q[:, ic * T:(ic + 1) * T]
        if stop_after == "q":
            proj_fm(A_WT_Q, qchunk, B_q)
            emit_norms(qchunk, B_q, sq_tm, B_sqt)
            return
        barrier("pre-sweep1")

        def qfu_ap(par, dc, lo, width):
            return A_QFU[:, par * 4096 + dc * 1024 + lo: par * 4096 + dc * 1024 + lo + width]
        B_qf2 = [Buf(em, "qf0"), Buf(em, "qf1")]
        steps = [(h, th2) for h in range(16) for th2 in range(2)]
        B_qc = [Buf(em, f"qc{ic}") for ic in range(NCH)]
        psn_q = pb(1, 1)[:, 0:256]

        def emit_qproj_tile(ic, tt):
            pB = pk(1, 0)
            pA = pb(1, 0)
            pB.wait_reads("tensor")
            B_wt.wait_write("tensor")
            B_ub.wait_write("tensor")
            mq = None
            for dc in range(NCH):
                lhs = A_WT_Q[:, dc * 1024 + ic * 128: dc * 1024 + (ic + 1) * 128]
                rhs = UB[:, dc * T + tt * 512: dc * T + (tt + 1) * 512]
                mq = mm(pA, lhs, rhs, dc == 0, dc == NCH - 1,
                        inc="M" if dc == NCH - 1 else None)
            pB.written("M", mq)
            B_wt.read_by("M", mq)
            B_ub.read_by("M", mq)
            pB.wait_write("scalar")
            sop = em.op("scalar", lambda s, ic=ic, tt=tt, pA=pA: s.copy(
                qchunk(ic)[:, tt * 512:(tt + 1) * 512], pA), inc="S")
            pB.read_by("S", sop)
            if tt == 3:
                B_qc[ic].written("S", sop)
                B_q.written("S", sop)

        def emit_qnorm_piece(c):
            B_qc[c].wait_write("vector")
            mq = None
            for hb in range(4):
                sl = trig_slot[0]
                trig_slot[0] = (sl + 1) % NTRIG
                nb = nscr[sl]
                Bt = B_trig[sl]
                Bt.wait_reads("vector")
                Bt.wait_reads("scalar")
                vn = em.op("vector", lambda v, c=c, hb=hb, nb=nb: v.tensor_mul(
                    nb[:, 0:512], qchunk(c)[:, hb * 512:(hb + 1) * 512],
                    qchunk(c)[:, hb * 512:(hb + 1) * 512]), inc="V")
                Bt.written("V", vn)
                B_qc[c].read_by("V", vn)
                em.wait("tensor", "V", vn)
                if c == 0 and hb == 0:
                    pk(1, 1).wait_reads("tensor")
                for tq in range(4):
                    tch = hb * 4 + tq
                    mq = mm(psn_q[:, tch * 16 + 2 * c: tch * 16 + 2 * c + 2],
                            nb[:, tq * 128:(tq + 1) * 128], pairsel[:],
                            True, True, inc="M" if tq == 3 else None)
                Bt.read_by("M", mq)
            if c == NCH - 1:
                pk(1, 1).written("M", mq)

        def emit_proj_trig(i):
            h, th2 = steps[i]
            hc, hr = h // 2, (h % 2) * 64
            par = th2
            pjq = 3 if i % 2 == 0 else 2
            B_qc[hc].wait_write("tensor")
            B_qf2[par].wait_reads("tensor")
            for mc in range(2):
                for nn in range(2):
                    tt = th2 * 2 + nn
                    pjb = pk(pjq, nn)
                    pja = pb(pjq, nn)
                    pjb.wait_reads("tensor")
                    lhs = rfT[:, (h % 2) * 256 + mc * 128:
                              (h % 2) * 256 + (mc + 1) * 128]
                    rhs = A_Q[:, hc * T + tt * 512: hc * T + (tt + 1) * 512]
                    mq = mm(pja, lhs, rhs, True, True, inc="M")
                    pjb.written("M", mq)
                    B_q.read_by("M", mq)
                    cosd = qfu_ap(par, mc, nn * 512, 512)
                    sind = qfu_ap(par, 2 + mc, nn * 512, 512)
                    st = trig_pipeline(pja, pjb, 512, sind, cosd)
                    B_qf2[par].written("S", st)

        def emit_qkv(i):
            h, th2 = steps[i]
            hc, hr = h // 2, (h % 2) * 64
            par = th2
            B_kv.wait_write("tensor")
            B_qf2[par].wait_write("tensor")
            qb0, qb1 = pk(0, 0), pk(0, 1)
            qb0.wait_reads("tensor")
            qb1.wait_reads("tensor")
            mq = None
            for tti in range(2):
                qa = pb(0, tti)[0:65, :]
                for dc in range(4):
                    lhs = A_KV[:, h * 260 + dc * 65: h * 260 + (dc + 1) * 65]
                    rhs = qfu_ap(par, dc, tti * 512, 512)
                    mq = mm(qa, lhs, rhs, dc == 0, dc == 3,
                            inc="M" if (dc == 3 and tti == 1) else None)
            qb0.written("M", mq)
            qb1.written("M", mq)
            B_kv.read_by("M", mq)
            B_qf2[par].read_by("M", mq)
            qb0.wait_write("vector")
            qb0.wait_write("scalar")
            if th2 == 0:
                B_zst.wait_reads("scalar")
                B_q.wait_reads("vector")
                B_q.wait_reads("scalar")
            dst = A_Q[:, hc * T + th2 * 1024:
                      hc * T + (th2 + 1) * 1024][hr:hr + 64, :]
            if th2 == 0:
                szv = em.op("vector", lambda v, d=dst: v.tensor_copy(
                    d, PB[0][0:64, 0:1024]), inc="V")
                qb0.read_by("V", szv)
                qb1.read_by("V", szv)
            else:
                szv = em.op("scalar", lambda s, d=dst: s.copy(
                    d, PB[0][0:64, 0:1024]), inc="S")
                qb0.read_by("S", szv)
                qb1.read_by("S", szv)
            sz = em.op("scalar", lambda s, th2=th2: s.copy(
                A_ZST[64:65, th2 * 1024:(th2 + 1) * 1024],
                PB[0][64:65, 0:1024]), inc="S")
            qb0.read_by("S", sz)
            qb1.read_by("S", sz)
            if th2 == 1:
                B_zst.written("S", sz)
                B_q.written("S", szv)
                B_zfm.wait_reads("gpsimd")
                B_zst.wait_write("gpsimd")
                cz = em.op("gpsimd", lambda g, h=h: g.dma_start(
                    A_ZFM[h:h + 1, :], A_ZST[64:65, :]), inc="dG", incby=16)
                B_zst.read_by("dG", cz)
                if h == 15:
                    B_zfm.written("dG", cz)

        for ic in range(2):
            for tt in range(4):
                emit_qproj_tile(ic, tt)
            emit_qnorm_piece(ic)
        emit_proj_trig(0)
        for i in range(32):
            hc = i // 4
            if hc + 2 < NCH:
                emit_qproj_tile(hc + 2, i % 4)
                if i % 4 == 3:
                    emit_qnorm_piece(hc + 2)
            if i < 31:
                emit_proj_trig(i + 1)
            emit_qkv(i)
        pk(1, 1).wait_write("scalar")
        B_sqt.wait_reads("scalar")
        lastn = None
        for tch in range(16):
            lastn = em.op("scalar", lambda s, tch=tch: s.activation(
                sq_tm[:, tch * H:(tch + 1) * H],
                psn_q[:, tch * 16:tch * 16 + 16],
                AF.Exp, scale=f32(-0.5), bias=f32(-LNSM)), inc="S")
        pk(1, 1).read_by("S", lastn)
        B_sqt.written("S", lastn)

        if stop_after == "sweep1":
            return
        B_qfu.wait_reads("gpsimd")
        B_wl.wait_reads("gpsimd")
        src_wo = wo_d.rearrange("(c p) d -> p c d", p=128)
        dst_wo = A_WO.rearrange("p (c d) -> p c d", c=NCH)
        cwo = em.op("gpsimd", lambda g: g.dma_start(dst_wo, src_wo),
                    inc="dW", incby=16)
        B_wl.written("dW", cwo)
        B_wl.wait_write("tensor")
        B_wt.wait_reads("tensor")
        B_wt.wait_reads("scalar")
        wo_sq = [None]

        def emit_wo_piece(i):
            dc, icq = i // 2, i % 2
            pB = pk(1, icq)
            pA = pb(1, icq)
            pB.wait_reads("tensor")
            mq = None
            for ici in range(4):
                ic = icq * 4 + ici
                lhs = A_WO[:, ic * 1024 + dc * 128: ic * 1024 + (dc + 1) * 128]
                mq = mm(pA[:, ici * 128:(ici + 1) * 128], lhs, ident_bf[:],
                        True, True, inc="M" if ici == 3 else None)
            pB.written("M", mq)
            B_wl.read_by("M", mq)
            pB.wait_write("scalar")
            wo_sq[0] = em.op("scalar", lambda s, pA=pA, dc=dc, icq=icq: s.copy(
                A_WOT[:, dc * 1024 + icq * 512: dc * 1024 + (icq + 1) * 512],
                pA), inc="S")
            pB.read_by("S", wo_sq[0])

        B_zfm.wait_write("tensor")
        B_sqt.wait_write("vector")
        vw = None
        for tch in range(16):
            emit_wo_piece(tch)
            pB = pk(2, tch % 2)
            pA = pb(2, tch % 2)
            pB.wait_reads("tensor")
            mz = mm(pA[:, 0:16], A_ZFM[0:16, tch * 128:(tch + 1) * 128],
                    ident_bf[0:16, 0:16], True, True, inc="M")
            pB.written("M", mz)
            pB.wait_write("vector")
            B_wtm.wait_reads("vector")
            sqsl = sq_tm[:, tch * H:(tch + 1) * H]
            vh(lambda v, pA=pA, sq=sqsl: v.tensor_tensor(
                wsc_f[:, 0:16], sq, pA[:, 0:16], ALU.mult))
            vh(lambda v: v.tensor_scalar(
                wsc_f[:, 0:16], wsc_f[:, 0:16], 1e-6, None, ALU.max))
            vh(lambda v: v.reciprocal(wsc_f[:, 16:32], wsc_f[:, 0:16]))
            vw = vh(lambda v, sq=sqsl, tch=tch: v.tensor_tensor(
                w_tm[:, tch * H:(tch + 1) * H], sq, wsc_f[:, 16:32], ALU.mult))
            pB.read_by("V", vw)
        B_wt.written("S", wo_sq[0])
        B_wtm.written("V", vw)
        B_wtm.wait_write("tensor")
        vap = None
        for h in range(16):
            hc, hr = h // 2, (h % 2) * 64
            for tt in range(4):
                pB = pk(3, tt % 2)
                pA = pb(3, tt % 2)
                pB.wait_reads("tensor")
                mwb = None
                for tq in range(4):
                    tch = tt * 4 + tq
                    wcol = w_tm[:, tch * H + h: tch * H + h + 1]\
                        .to_broadcast([128, 64])
                    mwb = mm(pA[0:64, tq * 128:(tq + 1) * 128], wcol, ident_bf[:],
                             True, True, inc="M" if tq == 3 else None)
                pB.written("M", mwb)
                pB.wait_write("vector")
                ap_ = A_Q[:, hc * T + tt * 512:
                          hc * T + (tt + 1) * 512][hr:hr + 64, :]
                vap = em.op("vector", lambda v, a=ap_, pA=pA: v.tensor_tensor(
                    a, a, pA[0:64, :], ALU.mult), inc="V")
                pB.read_by("V", vap)
        B_q.written("V", vap)

        if stop_after == "sweep2":
            return
        B_wt.wait_write("tensor")
        B_q.wait_write("tensor")
        for jc in range(NCH):
            for tt in range(4):
                pB = pk(1, tt % 2)
                pA = pb(1, tt % 2)
                pB.wait_reads("tensor")
                mq = None
                for ic in range(NCH):
                    lhs = A_WOT[:, ic * 1024 + jc * 128: ic * 1024 + (jc + 1) * 128]
                    rhs = A_Q[:, ic * T + tt * 512: ic * T + (tt + 1) * 512]
                    mq = mm(pA, lhs, rhs, ic == 0, ic == NCH - 1,
                            inc="M" if ic == NCH - 1 else None)
                pB.written("M", mq)
                B_q.read_by("M", mq)
                B_wt.read_by("M", mq)
                pB.wait_write("vector")
                B_x[jc].wait_write("vector")
                x_ap = x_sb[:, jc * T + tt * 512: jc * T + (tt + 1) * 512]
                vx = em.op("vector", lambda v, x_ap=x_ap, pA=pA: v.tensor_add(
                    x_ap, x_ap, pA), inc="V")
                pB.read_by("V", vx)
                B_x[jc].written("V", vx)


    if mode == "ln1":
        emit_layernorm()
        emit_output_from_ub()
    elif mode == "ffn1":
        emit_layernorm()
        emit_ffn(0)
        barrier("out")
        for c in range(NCH):
            emit_out_chunk(c)
        finish_output()
    elif mode == "rt":
        emit_layernorm()
        emit_routing()
        barrier("out")
        for c in range(NCH):
            emit_out_chunk(c)
        finish_output()
    elif mode == "dbg_ineg":
        emit_layernorm()
        emit_routing()
        barrier("out")
        em.op("gpsimd", lambda g: g.dma_start(out_r[0][0:8, :], INEGB[0:8, :]),
              inc="dG", incby=16)
        em.op("gpsimd", lambda g: g.dma_start(out_r[1][0:8, :], LOGB[0:8, :]),
              inc="dG", incby=16)
        em.op("gpsimd", lambda g: g.dma_start(out_r[2][:, 0:128], lt_sb[:]),
              inc="dG", incby=16)
        em.op("gpsimd", lambda g: g.dma_start(out_r[3][:, 0:128], pe_bf[:]),
              inc="dG", incby=16)
        em.op("gpsimd", lambda g: g.dma_start(out_r[4][0:8, 0:1024], maskmat[:]),
              inc="dG", incby=16)
        em.wait("gpsimd", "dG", em.cnt["dG"])
    elif mode == "ffn_sh":
        emit_layernorm()
        emit_ffn(0, max_groups=1, max_th=2, skip_routing=True)
        barrier("out")
        for c in range(NCH):
            emit_out_chunk(c)
        finish_output()
    elif mode == "ffn_g1":
        emit_layernorm()
        emit_ffn(0, max_groups=1, max_th=2)
        barrier("out")
        for c in range(NCH):
            emit_out_chunk(c)
        finish_output()
    elif mode.startswith("attn_"):
        emit_layernorm()
        emit_ffn(0)
        emit_attention(stop_after=mode[5:])
        barrier("out")
        for c in range(NCH):
            emit_out_chunk(c)
        finish_output()
    elif mode == "attn":
        emit_layernorm()
        emit_ffn(0)
        emit_attention()
        barrier("out")
        for c in range(NCH):
            emit_out_chunk(c)
        finish_output()
    else:
        emit_layernorm()
        emit_ffn(0)
        emit_attention()
        emit_layernorm(home=LN_HOME_B)
        barrier("pre-ffn2")
        emit_ffn(1, emit_out_tail=True)
        finish_output()

    with nc.Block() as block:
        @block.tensor
        def _(t):
            for fn in em.ops["tensor"]:
                fn(t)

        @block.vector
        def _(v):
            for fn in em.ops["vector"]:
                fn(v)

        @block.scalar
        def _(s):
            for fn in em.ops["scalar"]:
                fn(s)

        @block.gpsimd
        def _(g):
            for fn in em.ops["gpsimd"]:
                fn(g)

        @block.sync
        def _(sy):
            for fn in em.ops["sync"]:
                fn(sy)

    ctx.close()
    return nc


_NC_CACHE = {}


def get_nc(mode="full"):
    if mode not in _NC_CACHE:
        _NC_CACHE[mode] = build_program(mode)
    return _NC_CACHE[mode]


def make_in_maps(L0_attn_x, L0_attn_wq, L0_attn_wk, L0_attn_wv, L0_attn_wo,
                 L0_attn_rf, L0_ffn_Ws1, L0_ffn_Ws2, L0_ffn_W1, L0_ffn_W2,
                 L0_ffn_Wr):
    cst = np.zeros((128, 1152), dtype=np.float32)
    cst[:, 0:128] = np.eye(128, dtype=np.float32)
    for e in range(8):
        cst[e, 128 + e * 128:128 + (e + 1) * 128] = -BIG
    shared = {
        "wq": np.ascontiguousarray(L0_attn_wq, dtype=np.float32),
        "wk": np.ascontiguousarray(L0_attn_wk, dtype=np.float32),
        "wv": np.ascontiguousarray(L0_attn_wv, dtype=np.float32),
        "wo": np.ascontiguousarray(L0_attn_wo, dtype=np.float32),
        "rf": np.ascontiguousarray(L0_attn_rf, dtype=np.float32),
        "ws1": np.ascontiguousarray(L0_ffn_Ws1, dtype=np.float32),
        "ws2": np.ascontiguousarray(L0_ffn_Ws2, dtype=np.float32),
        "w1": np.ascontiguousarray(L0_ffn_W1, dtype=np.float32),
        "w2": np.ascontiguousarray(L0_ffn_W2, dtype=np.float32),
        "wr": np.ascontiguousarray(L0_ffn_Wr, dtype=np.float32),
        "cst": cst,
    }
    in_maps = []
    for b in range(NCORES):
        m = dict(shared)
        m["x"] = np.ascontiguousarray(L0_attn_x[b].T, dtype=np.float32)
        in_maps.append(m)
    return in_maps


def kernel(**inputs):
    nc = get_nc("full")
    in_maps = make_in_maps(**inputs)
    res = run_bass_kernel_spmd(nc, in_maps, core_ids=list(range(NCORES)))
    out = np.stack([np.ascontiguousarray(res.results[b]["out"].T)
                    for b in range(NCORES)], axis=0)
    return out


# version 33
# speedup vs baseline: 1.1494x; 1.1494x over previous
import math
from contextlib import ExitStack

import numpy as np
import concourse.bass as bass
import concourse.mybir as mybir
from concourse.bass_utils import run_bass_kernel_spmd

F32 = mybir.dt.float32
BF16 = mybir.dt.bfloat16
AF = mybir.ActivationFunctionType
ALU = mybir.AluOpType
AX = mybir.AxisListType

D, T, NCH = 1024, 2048, 8
H, DH, MRF = 16, 64, 256
NE, FD = 8, 512
NCORES = 8

PI_F = float(np.float32(np.pi))
TWO_PI = float(np.float32(2 * np.pi))
INV2PI = float(np.float32(1.0 / (2 * np.pi)))
MAGIC = float(np.float32(12582912.0))
SCALE_2PI = float(np.float32(6.2831783))
BIG = 1.0e20
LNSM = float(np.float32(math.log(math.sqrt(MRF))))
EPS = 1.0e-5
f32 = np.float32

SEM_OWNER = {"M": "tensor", "V": "vector", "S": "scalar", "G": "gpsimd"}


class Emitter:
    ENGINES = ("tensor", "vector", "scalar", "gpsimd", "sync")

    def __init__(self, nc, ctx):
        self.nc, self.ctx = nc, ctx
        self.ops = {e: [] for e in self.ENGINES}
        self.sems, self.cnt, self.lastwait = {}, {}, {}

    def sem(self, name):
        self.sems[name] = self.ctx.enter_context(self.nc.semaphore(name))
        self.cnt[name] = 0

    def wait(self, eng, semname, val, force=False):
        if val is None or val <= 0 or (SEM_OWNER.get(semname) == eng and not force):
            return
        key = (eng, semname)
        if self.lastwait.get(key, -1) >= val:
            return
        self.lastwait[key] = val
        h = self.sems[semname]
        self.ops[eng].append(lambda en, h=h, v=val: en.wait_ge(h, v))

    def op(self, eng, fn, inc=None, incby=1):
        if inc is None:
            self.ops[eng].append(fn)
            return None
        h = self.sems[inc]
        self.cnt[inc] += incby
        v = self.cnt[inc]

        def run(en, fn=fn, h=h, incby=incby):
            fn(en).then_inc(h, incby)
        self.ops[eng].append(run)
        return v


class Buf:
    def __init__(self, em, name=""):
        self.em, self.name = em, name
        self.w = None
        self.r = []

    def wait_write(self, eng):
        if self.w:
            self.em.wait(eng, *self.w)

    def read_by(self, sem, cnt):
        if cnt is not None:
            self.r.append((sem, cnt))

    def wait_reads(self, eng):
        for (s, c) in self.r:
            self.em.wait(eng, s, c)
        if self.w:
            self.em.wait(eng, *self.w)

    def written(self, sem, cnt):
        if cnt is not None:
            self.w = (sem, cnt)
            self.r = []


def build_program(mode="full"):
    nc = bass.Bass("TRN2", target_bir_lowering=False, detect_race_conditions=False)

    x_in = nc.dram_tensor("x", [D, T], F32, kind="ExternalInput").ap()
    wq_d = nc.dram_tensor("wq", [D, D], F32, kind="ExternalInput").ap()
    wk_d = nc.dram_tensor("wk", [D, D], F32, kind="ExternalInput").ap()
    wv_d = nc.dram_tensor("wv", [D, D], F32, kind="ExternalInput").ap()
    wo_d = nc.dram_tensor("wo", [D, D], F32, kind="ExternalInput").ap()
    rf_d = nc.dram_tensor("rf", [MRF, DH], F32, kind="ExternalInput").ap()
    ws1_d = nc.dram_tensor("ws1", [D, FD], F32, kind="ExternalInput").ap()
    ws2_d = nc.dram_tensor("ws2", [FD, D], F32, kind="ExternalInput").ap()
    w1_d = nc.dram_tensor("w1", [NE, D, FD], F32, kind="ExternalInput").ap()
    w2_d = nc.dram_tensor("w2", [NE, FD, D], F32, kind="ExternalInput").ap()
    wr_d = nc.dram_tensor("wr", [D, NE], F32, kind="ExternalInput").ap()
    cst_d = nc.dram_tensor("cst", [128, 1152], F32, kind="ExternalInput").ap()
    out_d = nc.dram_tensor("out", [D, T], F32, kind="ExternalOutput").ap()

    ctx = ExitStack()
    SB = lambda name, shape, dt: ctx.enter_context(nc.sbuf_tensor(name, shape, dt))
    PSA = lambda name, shape: ctx.enter_context(nc.psum_tensor(name, shape, F32))

    em = Emitter(nc, ctx)
    for s in ("dW", "dX", "dG", "M", "V", "S", "G"):
        em.sem(s)

    x_sb = SB("x_sb", [128, NCH * T], F32)
    ident_bf = SB("ident_bf", [128, 128], BF16)
    rfT = SB("rfT", [128, 2 * MRF], BF16)
    rf_raw = SB("rf_raw", [128, 2 * DH], BF16)
    ones_row = SB("ones_row", [1, 512], BF16)
    ones_col = SB("ones_col", [128, 1], BF16)
    e0_col = SB("e0_col", [128, 2], BF16)
    e1_col = SB("e1_col", [128, 2], BF16)
    pairsel = SB("pairsel", [128, 2], BF16)
    maskmat = SB("maskmat", [8, 8 * 128], BF16)
    iota8 = SB("iota8", [128, 8], F32)
    wrb = SB("wrb", [128, NCH * 8], BF16)
    m_sl = SB("m_sl", [1, 512], F32)
    t_sl = SB("t_sl", [1, 512], F32)
    var_sl = SB("var_sl", [1, 512], F32)
    a_sl = SB("a_sl", [1, 512], F32)
    abf_sl = SB("abf_sl", [1, 512], BF16)
    bbf_sl = SB("bbf_sl", [1, 512], BF16)
    lt_sb = SB("lt_sb", [128, 128], F32)
    rs_sb = SB("rs_sb", [128, 128], F32)
    max1 = SB("max1", [128, 16], F32)
    idm = SB("idm", [128, 16], F32)
    max2 = SB("max2", [128, 16], F32)
    t1_sb = SB("t1_sb", [128, 16], F32)
    pe_tm = SB("pe_tm", [128, 128], F32)
    pe_bf = SB("pe_bf", [128, 128], BF16)
    sk_tm = SB("sk_tm", [128, 16 * H], F32)
    sq_tm = SB("sq_tm", [128, 16 * H], F32)
    skb_tm = SB("skb_tm", [128, 16 * H], BF16)
    w_tm = SB("w_tm", [128, 16 * H], BF16)
    wsc_f = SB("wsc_f", [128, 32], F32)
    NTRIG = 4
    nscr_t = SB("nscr", [128, 2048], BF16)
    dscr_t = SB("dscr", [128, 2048], BF16)
    nscr = [nscr_t[:, i * 512: min(i * 512 + 1024, 2048)] for i in range(NTRIG)]
    dscr = [dscr_t[:, i * 512: min(i * 512 + 1024, 2048)] for i in range(NTRIG)]
    iscr_s = SB("iscr", [128, 512], mybir.dt.int32)
    iscr = [iscr_s] * NTRIG
    vas_sb = [SB(f"vas{i}", [128, 416], BF16) for i in range(2)]

    ACOLS = 57472
    arena = SB("arena", [128, ACOLS], BF16)

    def ar(c0, cols):
        return arena[:, c0:c0 + cols]

    UB = ar(0, NCH * T)
    LN_HOME_A = ((ar(51200, 2048), ar(53248, 2048)),
                 [ar(55296, 512), ar(55808, 512)],
                 [ar(56320, 512), ar(56832, 512)])
    LN_HOME_B = ((ar(24576, 2048), ar(28736, 2048)),
                 [ar(26624, 512), ar(27136, 512)],
                 [ar(27648, 512), ar(28160, 512)])
    HS = [ar(16384 + i * 4096, 4096) for i in range(3)]
    W1S = [ar(28672 + i * 4096, 4096) for i in range(2)]
    W2S = [ar(36864 + i * 4096, 4096) for i in range(3)]
    RBUF = [ar(49152 + i * 1024, 1024) for i in range(2)]
    LOGB = ar(51200, T)
    INEGB = ar(53248, T)
    A_WL = ar(16384, 8192)
    A_WT = ar(24576, 8192)
    A_V = ar(32832, 16384)

    def A_K(c):
        if c < 4:
            return ar(16384 + c * T, T)
        return ar(49216 + (c - 4) * T, T)
    A_KV = ar(24576, H * 4 * 65)
    A_KF = [ar(28736, 2048), ar(30784, 2048)]
    A_NSQ = ar(28736, 512)
    A_VAS = vas_sb
    A_ZST = ar(28736, 2048)
    A_ZFM = ar(30784, 2048)
    A_Q = ar(32832, 16384)
    A_WL_Q = ar(16384, 8192)
    A_WT_Q = ar(49216, 8192)
    A_QFU = ar(16384, 8192)
    A_WO = ar(16384, 8192)
    A_WOT = ar(49216, 8192)

    PB = [PSA(f"pb{i}", [128, 1024]) for i in range(4)]
    pbuf = [Buf(em, f"pb{i // 2}.{i % 2}") for i in range(8)]

    def pb(i, half, cols=512):
        return PB[i][:, half * 512: half * 512 + cols]

    def pk(i, half):
        return pbuf[2 * i + half]

    class Two:
        def __init__(self, i):
            self.a, self.b = pbuf[2 * i], pbuf[2 * i + 1]

        def wait_write(self, eng):
            self.a.wait_write(eng); self.b.wait_write(eng)

        def read_by(self, sem, cnt):
            self.a.read_by(sem, cnt); self.b.read_by(sem, cnt)

        def wait_reads(self, eng):
            self.a.wait_reads(eng); self.b.wait_reads(eng)

        def written(self, sem, cnt):
            self.a.written(sem, cnt); self.b.written(sem, cnt)

    def pk2(i):
        return Two(i)

    def pb2(i):
        return PB[i][:, :]

    B_ub = Buf(em, "ub")
    B_x = [Buf(em, f"x{c}") for c in range(NCH)]
    B_sq = [Buf(em, "sq0"), Buf(em, "sq1")]
    B_ab = [Buf(em, "ab0"), Buf(em, "ab1")]
    B_lnsl = Buf(em, "lnsl")
    B_hs = [Buf(em, f"hs{i}") for i in range(3)]
    B_w1 = [Buf(em, "w1a"), Buf(em, "w1b")]
    B_w2 = [Buf(em, f"w2{i}") for i in range(3)]
    B_rb = [Buf(em, "rb0"), Buf(em, "rb1")]
    B_logb = Buf(em, "logb")
    B_inegb = Buf(em, "inegb")
    B_wrb = Buf(em, "wrb")
    B_rfT = Buf(em, "rfT")
    B_wl = Buf(em, "wl")
    B_wt = Buf(em, "wt")
    B_v = Buf(em, "v")
    B_k = Buf(em, "k")
    B_kv = Buf(em, "kv")
    B_kf = [Buf(em, "kf0"), Buf(em, "kf1")]
    B_nsq = Buf(em, "nsq")
    B_vas = [Buf(em, "vas0"), Buf(em, "vas1")]
    B_q = Buf(em, "q")
    B_qfu = Buf(em, "qfu")
    B_zst = None
    B_zfm = Buf(em, "zfm")
    B_sk = Buf(em, "sk")
    B_skb = Buf(em, "skb")
    B_sqt = Buf(em, "sqt")
    B_wtm = Buf(em, "wtm")
    B_trig = [Buf(em, f"trig{i}") for i in range(4)]
    trig_slot = [0]
    B_rt = Buf(em, "routing")
    B_zst = Buf(em, "zst")

    def barrier(tag=""):
        for eng in Emitter.ENGINES:
            for sname in ("dW", "dX", "dG", "M", "V", "S"):
                em.wait(eng, sname, em.cnt[sname])

    def vh(fn):
        c = em.op("vector", fn, inc="V")
        em.wait("vector", "V", c, force=True)
        return c

    def mm(out_ap, lhsT_ap, rhs_ap, start, stop, inc=None, tp=None):
        def fn(en, o=out_ap, l=lhsT_ap, r=rhs_ap, st=start, sp=stop, tp=tp):
            return en.matmul(o, l, r, start=st, stop=sp, tile_position=tp)
        return em.op("tensor", fn, inc=inc)

    def scopy(dst_ap, src_ap, inc="S", scale=None, func=AF.Copy, bias=None):
        kw = {}
        if scale is not None:
            kw["scale"] = f32(scale)
        if bias is not None:
            kw["bias"] = f32(bias)

        def fn(en, d=dst_ap, s=src_ap, func=func, kw=kw):
            return en.activation(d, s, func, **kw)
        return em.op("scalar", fn, inc=inc)

    em.op("gpsimd", lambda g: g.dma_start(ident_bf[:], cst_d[:, 0:128]), inc="dG", incby=16)
    em.op("gpsimd", lambda g: g.dma_start(maskmat[:], cst_d[0:8, 128:1152]), inc="dG", incby=16)
    rf_r = rf_d.rearrange("(c p) d -> c p d", p=128)
    em.op("gpsimd", lambda g: g.dma_start(rf_raw[:, 0:DH], rf_r[0]), inc="dG", incby=16)
    c_rf = em.op("gpsimd", lambda g: g.dma_start(rf_raw[:, DH:2 * DH], rf_r[1]),
                 inc="dG", incby=16)

    x_r = x_in.rearrange("(c p) t -> p c t", p=128)
    for hf in range(2):
        src = x_r[:, 4 * hf:4 * hf + 4, :]
        dst = x_sb[:, hf * 4 * T:(hf + 1) * 4 * T].rearrange("p (c t) -> p c t", c=4)
        cx = em.op("sync", lambda sy, d=dst, s=src: sy.dma_start(d, s),
                   inc="dX", incby=16)
        for c in range(4 * hf, 4 * hf + 4):
            B_x[c].written("dX", cx)

    for fn in (
        lambda v: v.memset(ones_row[:], 1.0),
        lambda v: v.memset(ones_col[:], 1.0),
        lambda v: v.memset(e0_col[:, 0:1], 1.0),
        lambda v: v.memset(e0_col[:, 1:2], 0.0),
        lambda v: v.memset(e1_col[:, 0:1], 0.0),
        lambda v: v.memset(e1_col[:, 1:2], 1.0),
        lambda v: v.memset(pairsel[:], 0.0),
        lambda v: v.memset(pairsel[0:64, 0:1], 1.0),
        lambda v: v.memset(pairsel[64:128, 1:2], 1.0),
    ):
        em.op("vector", fn)
    c_consts = None
    for e in range(8):
        c_consts = em.op("vector", lambda v, e=e: v.memset(iota8[:, e:e + 1], float(e)),
                         inc="V")

    vz = em.op("vector", lambda v: v.memset(rfT[:], 0.0), inc="V")
    em.wait("tensor", "dG", c_rf)
    pk2(0).wait_reads("tensor")
    c_t = None
    for mc in range(2):
        mm(PB[0][0:64, mc * 128:(mc + 1) * 128],
           rf_raw[:, mc * DH:(mc + 1) * DH], ident_bf[:], True, True)
        c_t = mm(PB[0][64:128, mc * 128:(mc + 1) * 128],
                 rf_raw[:, mc * DH:(mc + 1) * DH], ident_bf[:], True, True, inc="M")
    pk2(0).written("M", c_t)
    pk2(0).wait_write("scalar")
    em.wait("scalar", "V", vz)
    scopy(rfT[0:64, 0:256], PB[0][0:64, 0:256], scale=INV2PI)
    c_rfT = scopy(rfT[64:128, 256:512], PB[0][64:128, 0:256], scale=INV2PI)
    pk2(0).read_by("S", c_rfT)
    B_rfT.written("S", c_rfT)

    def emit_layernorm(home=None):
        sqs, abuf, bbuf = home if home else LN_HOME_A
        sq_scr = [sqs[0], sqs[1]]
        stA = [pb(0, 0), pb(0, 1), pb(1, 0), pb(1, 1)]
        stB = [pb(2, 0), pb(2, 1), pb(3, 0), pb(3, 1)]
        kA = [pk(0, 0), pk(0, 1), pk(1, 0), pk(1, 1)]
        kB = [pk(2, 0), pk(2, 1), pk(3, 0), pk(3, 1)]
        for b in kA + kB:
            b.wait_reads("tensor")
        for c in range(NCH):
            B_x[c].wait_write("vector")
            B_sq[c % 2].wait_reads("vector")
            if c < 2:
                B_ub.wait_reads("scalar")
                B_ub.wait_reads("vector")
                B_logb.wait_reads("vector")
                B_kv.wait_reads("vector")
            xb_ap = UB[:, c * T:(c + 1) * T]
            em.op("vector", lambda v, a=xb_ap, c=c: v.tensor_copy(
                a, x_sb[:, c * T:(c + 1) * T]))
            vs = em.op("vector", lambda v, a=xb_ap, c=c: v.tensor_mul(
                sq_scr[c % 2][:], a, a), inc="V")
            em.wait("tensor", "V", vs)
            last = None
            for tt in range(4):
                mm(stA[tt][0:1, :], ones_col[:, 0:1],
                   xb_ap[:, tt * 512:(tt + 1) * 512], c == 0, c == NCH - 1)
                last = mm(stB[tt][0:1, :], ones_col[:, 0:1],
                          sq_scr[c % 2][:, tt * 512:(tt + 1) * 512],
                          c == 0, c == NCH - 1,
                          inc="M" if tt == 3 else None)
            B_sq[c % 2].written("V", vs)
            B_sq[c % 2].read_by("M", last)
        for b in kA + kB:
            b.written("M", em.cnt["M"])
        uv = None
        for tt in range(4):
            kA[tt].wait_write("scalar")
            kB[tt].wait_write("vector")
            B_lnsl.wait_reads("scalar")
            em.op("scalar", lambda s, st=stA[tt]: s.activation(
                m_sl[:], st[0:1, :], AF.Identity, scale=f32(1.0 / D)))
            sm2 = em.op("scalar", lambda s: s.activation(
                t_sl[:], m_sl[:], AF.Square), inc="S")
            kA[tt].read_by("S", sm2)
            em.wait("vector", "S", sm2)
            B_lnsl.wait_reads("vector")
            vv = em.op("vector", lambda v, st=stB[tt]: v.scalar_tensor_tensor(
                var_sl[:], st[0:1, :], 1.0 / D, t_sl[:], ALU.mult,
                ALU.subtract), inc="V")
            kB[tt].read_by("V", vv)
            em.wait("scalar", "V", vv)
            em.op("scalar", lambda s: s.activation(
                t_sl[:], var_sl[:], AF.Ln, bias=f32(EPS)))
            sv = em.op("scalar", lambda s: s.activation(
                a_sl[:], t_sl[:], AF.Exp, scale=f32(-0.5)), inc="S")
            em.wait("vector", "S", sv)
            em.op("vector", lambda v: v.tensor_copy(abf_sl[:], a_sl[:]))
            vb = em.op("vector", lambda v: v.scalar_tensor_tensor(
                bbf_sl[:], m_sl[:], -1.0, a_sl[:], ALU.mult, ALU.mult),
                inc="V")
            B_lnsl.written("V", vb)
            B_lnsl.read_by("V", vb)
            em.wait("tensor", "V", vb)
            kA[tt].wait_reads("tensor")
            kB[tt].wait_reads("tensor")
            mm(stA[tt], ones_row[0:1, 0:128], abf_sl[0:1, :], True, True)
            mab = mm(stB[tt], ones_row[0:1, 0:128], bbf_sl[0:1, :], True, True,
                     inc="M")
            kA[tt].written("M", mab)
            kB[tt].written("M", mab)
            B_lnsl.read_by("M", mab)
            kA[tt].wait_write("scalar")
            kB[tt].wait_write("scalar")
            B_ab[tt % 2].wait_reads("scalar")
            em.op("scalar", lambda s, tt=tt: s.copy(abuf[tt % 2][:], stA[tt]))
            sab = em.op("scalar", lambda s, tt=tt: s.copy(bbuf[tt % 2][:], stB[tt]),
                        inc="S")
            kA[tt].read_by("S", sab)
            kB[tt].read_by("S", sab)
            B_ab[tt % 2].written("S", sab)
            em.wait("vector", "S", sab)
            for c in range(NCH):
                u_ap = UB[:, c * T + tt * 512: c * T + (tt + 1) * 512]
                em.op("vector", lambda v, u=u_ap, tt=tt: v.tensor_mul(
                    u, u, abuf[tt % 2][:]))
                uv = em.op("vector", lambda v, u=u_ap, tt=tt: v.tensor_add(
                    u, u, bbuf[tt % 2][:]), inc="V")
            B_ab[tt % 2].read_by("V", uv)
        B_ub.written("V", uv)

    def emit_routing():
        wr_r = wr_d.rearrange("(c p) e -> p c e", p=128)
        B_wrb.wait_reads("gpsimd")
        cw = em.op("gpsimd", lambda g: g.dma_start(
            wrb[:].rearrange("p (c e) -> p c e", c=NCH), wr_r), inc="dW", incby=16)
        B_wrb.written("dW", cw)
        B_ub.wait_write("tensor")
        B_wrb.wait_write("tensor")
        se = None
        for tt in range(4):
            lgb = pk(3, tt % 2)
            lga = pb(3, tt % 2)
            lgb.wait_reads("tensor")
            mq = None
            for c in range(NCH):
                mq = mm(lga[0:8, :], wrb[:, c * 8:(c + 1) * 8],
                        UB[:, c * T + tt * 512: c * T + (tt + 1) * 512],
                        c == 0, c == NCH - 1, inc="M" if c == NCH - 1 else None)
            lgb.written("M", mq)
            B_ub.read_by("M", mq)
            B_wrb.read_by("M", mq)
            lgb.wait_write("scalar")
            B_logb.wait_reads("scalar")
            se = em.op("scalar", lambda s, tt=tt, lga=lga: s.copy(
                LOGB[0:8, tt * 512:(tt + 1) * 512], lga[0:8, :]), inc="S")
            lgb.read_by("S", se)
        B_logb.written("S", se)
        B_logb.wait_write("tensor")
        pk(2, 0).wait_reads("tensor")
        mlt = None
        for tch in range(16):
            mlt = mm(pb(2, 0)[:, tch * 8:(tch + 1) * 8],
                     LOGB[0:8, tch * 128:(tch + 1) * 128], ident_bf[0:8, 0:8],
                     True, True, inc="M" if tch == 15 else None)
        pk(2, 0).written("M", mlt)
        B_logb.read_by("M", mlt)
        pk(2, 0).wait_write("vector")
        B_rt.wait_reads("vector")
        vl = em.op("vector", lambda v: v.tensor_copy(lt_sb[:], pb(2, 0)[:, 0:128]),
                   inc="V")
        pk(2, 0).read_by("V", vl)
        lt3 = lt_sb[:].rearrange("p (g e) -> p g e", e=8)
        rs3 = rs_sb[:].rearrange("p (g e) -> p g e", e=8)
        pe3 = pe_tm[:].rearrange("p (g e) -> p g e", e=8)
        m1b = max1[:].unsqueeze(2).to_broadcast([128, 16, 8])
        idb = idm[:].unsqueeze(2).to_broadcast([128, 16, 8])
        t1b = t1_sb[:].unsqueeze(2).to_broadcast([128, 16, 8])
        io3 = iota8[:].unsqueeze(1).to_broadcast([128, 16, 8])
        em.wait("vector", "V", c_consts)
        vh(lambda v: v.tensor_reduce(max1[:], lt3, AX.X, ALU.max))
        vh(lambda v: v.tensor_tensor(rs3, lt3, m1b, ALU.is_equal))
        vh(lambda v: v.scalar_tensor_tensor(
            rs3, rs3, -8.0, io3, ALU.mult, ALU.add))
        vh(lambda v: v.tensor_reduce(idm[:], rs3, AX.X, ALU.min))
        vh(lambda v: v.tensor_tensor(pe3, rs3, idb, ALU.is_equal))
        vh(lambda v: v.scalar_tensor_tensor(
            rs3, pe3, -1e30, lt3, ALU.mult, ALU.add))
        vh(lambda v: v.tensor_reduce(max2[:], rs3, AX.X, ALU.max))
        vg = vh(lambda v: v.tensor_sub(max2[:], max2[:], max1[:]))
        em.wait("scalar", "V", vg)
        sg = em.op("scalar", lambda s: s.activation(max2[:], max2[:], AF.Exp),
                   inc="S")
        em.wait("vector", "S", sg)
        vh(lambda v: v.tensor_scalar(
            max2[:], max2[:], 1.0, None, ALU.add))
        vh(lambda v: v.reciprocal(t1_sb[:], max2[:]))
        vh(lambda v: v.tensor_scalar(
            t1_sb[:], t1_sb[:], 1.0, None, ALU.is_ge))
        vh(lambda v: v.tensor_tensor(pe3, pe3, t1b, ALU.mult))
        vh(lambda v: v.tensor_scalar(
            pe_tm[:], pe_tm[:], -1.0, 1.0, ALU.mult, ALU.add))
        vp = vh(lambda v: v.tensor_copy(pe_bf[:], pe_tm[:]))
        B_rt.written("V", vp)
        em.wait("tensor", "V", vp)
        se2 = None
        for tch in range(16):
            pk(2, 1).wait_reads("tensor")
            mtb = mm(pb(2, 1)[0:8, 0:128], pe_bf[:, tch * 8:(tch + 1) * 8],
                     ident_bf[:, 0:128], True, True, inc="M")
            pk(2, 1).written("M", mtb)
            B_rt.read_by("M", mtb)
            pk(2, 1).wait_write("scalar")
            if tch == 0:
                B_inegb.wait_reads("scalar")
            se2 = em.op("scalar", lambda s, tch=tch: s.copy(
                INEGB[0:8, tch * 128:(tch + 1) * 128], pb(2, 1)[0:8, 0:128]),
                inc="S")
            pk(2, 1).read_by("S", se2)
        B_inegb.written("S", se2)


    def emit_ffn(ffn_idx, emit_out_tail=False, max_groups=3, max_th=2,
                 skip_routing=False):
        if not skip_routing:
            emit_routing()
        eseq = [8, 0, 1, 2, 3, 4, 5, 6, 7]
        w1_stream = [(th, e) for th in range(max_th)
                     for e in eseq[:3 * max_groups]]

        def load_w1(idx):
            th, e = w1_stream[idx]
            slot = idx % 2
            B_w1[slot].wait_reads("gpsimd")
            src = (ws1_d if e == 8 else w1_d[e]).rearrange("(c p) f -> p c f", p=128)
            dst = W1S[slot].rearrange("p (c f) -> p c f", c=NCH)
            cw = em.op("gpsimd", lambda g, s=src, d=dst: g.dma_start(d, s),
                       inc="dW", incby=16)
            B_w1[slot].written("dW", cw)

        def load_w2_group(th, gi):
            for k in range(3):
                e = eseq[3 * gi + k]
                B_w2[k].wait_reads("gpsimd")
                src = (ws2_d if e == 8 else w2_d[e]).rearrange(
                    "(c p) f -> p c f", p=128)
                dst = W2S[k].rearrange("p (c f) -> p c f", c=4)
                cw = em.op("gpsimd", lambda g, s=src, d=dst: g.dma_start(d, s),
                           inc="dW", incby=16)
                B_w2[k].written("dW", cw)

        load_w1(0)
        load_w1(1)
        load_w2_group(0, 0)
        for th in range(max_th):
            for gi in range(max_groups):
                grp = eseq[3 * gi:3 * gi + 3]
                for k, e in enumerate(grp):
                    idx = th * 3 * max_groups + 3 * gi + k
                    w1slot = idx % 2
                    hs_ap = HS[k]
                    B_w1[w1slot].wait_write("tensor")
                    B_ub.wait_write("tensor")
                    use_mask = (e != 8 and not skip_routing)
                    vh = None
                    for f in range(4):
                        pB = pk2(f % 2)
                        pA = pb2(f % 2)
                        pB.wait_reads("tensor")
                        for dc in range(NCH):
                            lhs = W1S[w1slot][:, dc * 512 + f * 128:
                                              dc * 512 + (f + 1) * 128]
                            for nn in range(2):
                                lastw1 = (dc == NCH - 1 and not use_mask)
                                cmz = mm(pA[:, nn * 512:(nn + 1) * 512], lhs, UB[:, dc * T + th * 1024 + nn * 512:
                                         dc * T + th * 1024 + (nn + 1) * 512],
                                   dc == 0, lastw1,
                                   inc="M" if (lastw1 and nn == 1) else None)
                        if use_mask:
                            B_inegb.wait_write("tensor")
                            for nn in range(2):
                                cmz = mm(pA[:, nn * 512:(nn + 1) * 512],
                                         maskmat[0:8, e * 128:(e + 1) * 128],
                                         INEGB[0:8, th * 1024 + nn * 512:
                                               th * 1024 + (nn + 1) * 512],
                                         False, True, inc="M" if nn == 1 else None)
                        pB.written("M", cmz)
                        if use_mask:
                            B_inegb.read_by("M", cmz)
                        B_w1[w1slot].read_by("M", cmz)
                        B_ub.read_by("M", cmz)
                        pB.wait_write("scalar")
                        B_rb[f % 2].wait_reads("scalar")
                        sr = em.op("scalar", lambda s, pA=pA, f=f: s.activation(
                            RBUF[f % 2][:], pA, AF.Relu), inc="S")
                        pB.read_by("S", sr)
                        B_rb[f % 2].written("S", sr)
                        em.wait("vector", "S", sr)
                        B_hs[k].wait_reads("vector")
                        vh = em.op("vector", lambda v, hs=hs_ap, f=f: v.tensor_mul(
                            hs[:, f * 1024:(f + 1) * 1024],
                            RBUF[f % 2][:], RBUF[f % 2][:]), inc="V")
                        B_rb[f % 2].read_by("V", vh)
                    B_hs[k].written("V", vh)
                    if idx + 2 < len(w1_stream):
                        load_w1(idx + 2)
                for k in range(3):
                    B_w2[k].wait_write("tensor")
                    B_hs[k].wait_write("tensor")
                for j in range(NCH):
                    for nn in range(2):
                        pB = pk(2, (2 * j + nn) % 2)
                        pA = pb(2, (2 * j + nn) % 2)
                        pB.wait_reads("tensor")
                        cmb = None
                        for k in range(3):
                            for f in range(4):
                                lhs = W2S[k][:, f * 1024 + j * 128:
                                             f * 1024 + (j + 1) * 128]
                                rhs = HS[k][:, f * 1024 + nn * 512:
                                            f * 1024 + (nn + 1) * 512]
                                cmb = mm(pA, lhs, rhs, (k == 0 and f == 0),
                                         (k == 2 and f == 3),
                                         inc="M" if (k == 2 and f == 3) else None)
                        pB.written("M", cmb)
                        for k in range(3):
                            B_w2[k].read_by("M", cmb)
                            B_hs[k].read_by("M", cmb)
                        pB.wait_write("vector")
                        B_x[j].wait_write("vector")
                        x_ap = x_sb[:, j * T + th * 1024 + nn * 512:
                                    j * T + th * 1024 + (nn + 1) * 512]
                        vx = em.op("vector", lambda v, pA=pA, x_ap=x_ap:
                                   v.scalar_tensor_tensor(
                                       x_ap, pA, 0.5, x_ap, ALU.mult, ALU.add),
                                   inc="V")
                        pB.read_by("V", vx)
                        B_x[j].written("V", vx)
                        if emit_out_tail and th == 1 and gi == 2:
                            if nn == 1:
                                emit_out_chunk(j)
                if not (th == 1 and gi == 2):
                    ngi, nth = (gi + 1, th) if gi < 2 else (0, th + 1)
                    load_w2_group(nth, ngi)

    out_r = out_d.rearrange("(c p) t -> c p t", p=128)
    odma = []

    def emit_out_chunk(c):
        B_x[c].wait_write("sync")
        co = em.op("sync", lambda sy, c=c: sy.dma_start(
            out_r[c], x_sb[:, c * T:(c + 1) * T]), inc="dX", incby=16)
        odma.append(co)

    def finish_output():
        em.wait("sync", "dX", em.cnt["dX"])

    def emit_output_from_ub():
        B_ub.wait_write("gpsimd")
        for c in range(NCH):
            co = em.op("gpsimd", lambda g, c=c: g.dma_start(
                out_r[c], UB[:, c * T:(c + 1) * T]), inc="dG", incby=16)
        em.wait("gpsimd", "dG", em.cnt["dG"])

    def load_and_transpose(w_dram, wl_ap, wt_ap):
        B_wl.wait_reads("gpsimd")
        src = w_dram.rearrange("(c p) d -> p c d", p=128)
        dst = wl_ap.rearrange("p (c d) -> p c d", c=NCH)
        cw = em.op("gpsimd", lambda g, s=src, d=dst: g.dma_start(d, s),
                   inc="dW", incby=16)
        B_wl.written("dW", cw)
        B_wl.wait_write("tensor")
        B_wt.wait_reads("tensor")
        B_wt.wait_reads("scalar")
        sq_ = None
        for dc in range(NCH):
            for icq in range(2):
                pB = pk(3, icq)
                pA = pb(3, icq)
                pB.wait_reads("tensor")
                mq = None
                for ici in range(4):
                    ic = icq * 4 + ici
                    lhs = wl_ap[:, ic * 1024 + dc * 128: ic * 1024 + (dc + 1) * 128]
                    mq = mm(pA[:, ici * 128:(ici + 1) * 128], lhs, ident_bf[:],
                            True, True, inc="M" if ici == 3 else None)
                pB.written("M", mq)
                B_wl.read_by("M", mq)
                pB.wait_write("vector")
                sq_ = em.op("vector", lambda v, pA=pA, dc=dc, icq=icq:
                            v.tensor_copy(
                    wt_ap[:, dc * 1024 + icq * 512: dc * 1024 + (icq + 1) * 512],
                    pA), inc="V")
                pB.read_by("V", sq_)
        B_wt.written("V", sq_)

    def proj_fm(wt_ap, dst_fn, bdst):
        B_wt.wait_write("tensor")
        B_ub.wait_write("tensor")
        bdst.wait_reads("tensor")
        bdst.wait_reads("scalar")
        last_s = None
        groups = [[(3, 0), (3, 1), (2, 0), (2, 1)], [(1, 0), (1, 1), (0, 0), (0, 1)]]
        for ic in range(NCH):
            tiles = groups[ic % 2]
            for (q, hq) in tiles:
                pk(q, hq).wait_reads("tensor")
            mq = None
            for dc in range(NCH):
                lhs = wt_ap[:, dc * 1024 + ic * 128: dc * 1024 + (ic + 1) * 128]
                for tt in range(4):
                    q, hq = tiles[tt]
                    rhs = UB[:, dc * T + tt * 512: dc * T + (tt + 1) * 512]
                    mq = mm(pb(q, hq), lhs, rhs, dc == 0, dc == NCH - 1,
                            inc="M" if (dc == NCH - 1 and tt == 3) else None)
            B_wt.read_by("M", mq)
            B_ub.read_by("M", mq)
            for tt in range(4):
                q, hq = tiles[tt]
                pk(q, hq).written("M", mq)
                pk(q, hq).wait_write("vector")
                last_s = em.op("vector", lambda v, q=q, hq=hq, ic=ic, tt=tt:
                               v.tensor_copy(
                    dst_fn(ic)[:, tt * 512:(tt + 1) * 512], pb(q, hq)), inc="V")
                pk(q, hq).read_by("V", last_s)
        bdst.written("V", last_s)

    def proj_tm(wt_ap, dst_ap, bdst):
        B_wt.wait_write("tensor")
        B_ub.wait_write("tensor")
        bdst.wait_reads("tensor")
        bdst.wait_reads("scalar")
        last_s = None
        for tch in range(16):
            q = 3 if tch % 2 == 0 else 2
            pB = pk2(q)
            pA = pb2(q)
            pB.wait_reads("tensor")
            mq = None
            for dc in range(NCH):
                lhs = UB[:, dc * T + tch * 128: dc * T + (tch + 1) * 128]
                for ih in range(2):
                    rhs = wt_ap[:, dc * 1024 + ih * 512: dc * 1024 + (ih + 1) * 512]
                    mq = mm(pA[:, ih * 512:(ih + 1) * 512], lhs, rhs,
                            dc == 0, dc == NCH - 1,
                            inc="M" if (dc == NCH - 1 and ih == 1) else None)
            pB.written("M", mq)
            B_wt.read_by("M", mq)
            B_ub.read_by("M", mq)
            pB.wait_write("vector")
            last_s = em.op("vector", lambda v, pA=pA, tch=tch: v.tensor_copy(
                dst_ap[:, tch * 1024:(tch + 1) * 1024], pA), inc="V")
            pB.read_by("V", last_s)
        bdst.written("V", last_s)

    def emit_norms(src_fn, bsrc, dst_sl, bdst):
        psn = pb(2, 0)[:, 0:256]
        pk(2, 0).wait_reads("tensor")
        bsrc.wait_write("vector")
        mq = None
        for c in range(NCH):
            for hb in range(4):
                B_nsq.wait_reads("vector")
                vn = em.op("vector", lambda v, c=c, hb=hb: v.tensor_mul(
                    A_NSQ[:], src_fn(c)[:, hb * 512:(hb + 1) * 512],
                    src_fn(c)[:, hb * 512:(hb + 1) * 512]), inc="V")
                B_nsq.written("V", vn)
                em.wait("tensor", "V", vn)
                for tq in range(4):
                    tch = hb * 4 + tq
                    mq = mm(psn[:, tch * 16 + 2 * c: tch * 16 + 2 * c + 2],
                            A_NSQ[:, tq * 128:(tq + 1) * 128], pairsel[:],
                            True, True,
                            inc="M" if tq == 3 else None)
                B_nsq.read_by("M", mq)
        pk(2, 0).written("M", mq)
        pk(2, 0).wait_write("scalar")
        bdst.wait_reads("scalar")
        last = None
        for tch in range(16):
            last = em.op("scalar", lambda s, tch=tch: s.activation(
                dst_sl[:, tch * H:(tch + 1) * H], psn[:, tch * 16:tch * 16 + 16],
                AF.Exp, scale=f32(-0.5), bias=f32(-LNSM)), inc="S")
        pk(2, 0).read_by("S", last)
        bdst.written("S", last)

    def trig_pipeline(pja, pjb, width, sin_dst, cos_dst):
        if width > 512:
            sl = 0 if trig_slot[0] in (0, 3) else 2
            trig_slot[0] = (sl + 2) % NTRIG
        else:
            sl = trig_slot[0]
            trig_slot[0] = (sl + 1) % NTRIG
        ns, ds, isc = nscr[sl], dscr[sl], iscr[sl]
        Bt = B_trig[sl]
        Bt.wait_reads("vector")
        pjb.wait_write("vector")
        for c0 in range(0, width, 512):
            em.op("vector", lambda v, pja=pja, c0=c0, isc=isc: v.tensor_scalar(
                isc[:, 0:512], pja[:, c0:c0 + 512], 8.5, None, ALU.add))
            em.op("vector", lambda v, pja=pja, c0=c0, ds=ds, isc=isc:
                  v.scalar_tensor_tensor(
                ds[:, c0:c0 + 512], pja[:, c0:c0 + 512], 8.5, isc[:, 0:512],
                ALU.add, ALU.subtract), inc="V")
        vd = em.op("vector", lambda v, w=width, ns=ns, ds=ds:
                   v.scalar_tensor_tensor(
            ns[:, 0:w], ds[:, 0:w], 0.5, ds[:, 0:w], ALU.is_ge,
            ALU.subtract), inc="V")
        pjb.read_by("V", vd)
        Bt.written("V", vd)
        em.wait("scalar", "V", vd)
        em.op("scalar", lambda s, sd=sin_dst, w=width, ns=ns: s.activation(
            sd, ns[:, 0:w], AF.Sin, scale=f32(-SCALE_2PI)))
        em.op("scalar", lambda s, w=width, ns=ns, ds=ds: s.activation(
            ds[:, 0:w], ns[:, 0:w], AF.Abs))
        st = em.op("scalar", lambda s, cd=cos_dst, w=width, ds=ds: s.activation(
            cd, ds[:, 0:w], AF.Sin, scale=f32(-SCALE_2PI), bias=f32(PI_F / 2)),
            inc="S")
        Bt.read_by("S", st)
        return st

    def emit_attention(stop_after=None):
        emit_layernorm()
        barrier("attn-start")
        load_and_transpose(wv_d, A_WL, A_WT)
        proj_tm(A_WT, A_V, B_v)
        if stop_after == "v":
            return
        barrier("pre-k")
        load_and_transpose(wk_d, A_WL, A_WT)
        proj_fm(A_WT, A_K, B_k)
        if stop_after == "k":
            return
        emit_norms(A_K, B_k, sk_tm, B_sk)
        if stop_after == "norms":
            return
        B_sk.wait_write("vector")
        B_skb.wait_reads("vector")
        vsk = em.op("vector", lambda v: v.tensor_copy(skb_tm[:], sk_tm[:]), inc="V")
        B_sk.read_by("V", vsk)
        B_skb.written("V", vsk)

        passes = [(0, 1), (2, 3), (4, 5), (6, 7)]
        pending_drain = [None, None]
        for pi, prs in enumerate(passes):
            npr = 2
            kvmap = [(0, 0), (0, 1), (1, 0), (1, 1)]
            kvb = [pk(*kvmap[k]) for k in range(4)]
            kva = [pb(*kvmap[k])[0:65, :] for k in range(4)]
            B_v.wait_write("tensor")
            B_skb.wait_write("vector")
            B_rfT.wait_write("tensor")
            h0 = 2 * prs[0]
            nh = 4
            mkv = [None] * 4

            def emit_vaug_proj_trig(tch):
                vsl = tch % 2
                B_vas[vsl].wait_reads("vector")
                B_v.wait_write("vector")
                vdst = A_VAS[vsl][:, 0:65 * nh]
                v3 = vdst.rearrange("p (h q) -> p h q", h=nh)[:, :, 0:64]
                s3 = skb_tm[:, tch * H + h0: tch * H + h0 + nh]\
                    .unsqueeze(2).to_broadcast([128, nh, 64])
                src3 = A_V[:, tch * 1024 + h0 * 64: tch * 1024 + (h0 + nh) * 64]\
                    .rearrange("p (h q) -> p h q", h=nh)
                vmu = em.op("vector", lambda v, a=v3, b=src3, c=s3:
                            v.tensor_tensor(a, b, c, ALU.mult), inc="V")
                vcol = vdst.rearrange("p (h q) -> p h q", h=nh)[:, :, 64:65]
                B_vas[vsl].wait_reads("scalar")
                vva = em.op("scalar", lambda s, a=vcol, tch=tch:
                            s.copy(a, skb_tm[:, tch * H + h0:
                                             tch * H + h0 + nh].unsqueeze(2)),
                            inc="S")
                B_vas[vsl].written("S", vva)
                em.wait("tensor", "S", vva)
                em.wait("tensor", "V", vmu)
                pjq = 3 if tch % 2 == 0 else 2
                pjb = pk2(pjq)
                pja = pb2(pjq)
                pjb.wait_reads("tensor")
                mpj = None
                for ip, hp in enumerate(prs):
                    lhsE = A_K(hp)[:, tch * 128:(tch + 1) * 128]
                    mpj = mm(pja[:, ip * 512:(ip + 1) * 512], lhsE, rfT[:, 0:512],
                             True, True, inc="M" if ip == 1 else None)
                B_k.read_by("M", mpj)
                pjb.written("M", mpj)
                kfsl = tch % 2
                kf = A_KF[kfsl]
                B_kf[kfsl].wait_reads("vector")
                B_kf[kfsl].wait_reads("scalar")
                st = None
                for ip in range(2):
                    kf2 = kf[:, ip * 1024:(ip + 1) * 1024]\
                        .rearrange("p (h q) -> p h q", h=2)
                    st = trig_pipeline(pb(pjq, ip), pk(pjq, ip), 512,
                                       kf2[:, :, 256:512], kf2[:, :, 0:256])
                B_kf[kfsl].written("S", st)

            def emit_kv(tch):
                vsl = tch % 2
                kfsl = tch % 2
                kf = A_KF[kfsl]
                B_kf[kfsl].wait_write("tensor")
                B_vas[vsl].wait_write("tensor")
                for k in range(4):
                    lhsV = A_VAS[vsl][:, k * 65: k * 65 + 65]
                    mkv[k] = mm(kva[k], lhsV, kf[:, k * 512:(k + 1) * 512],
                                tch == 0, tch == 15,
                                inc="M" if k == 3 else None)
                B_kf[kfsl].read_by("M", mkv[3])
                B_vas[vsl].read_by("M", mkv[3])

            drain_slots = []

            def drain_a(kvb=kvb, kva=kva, prs=prs, drain_slots=drain_slots):
                for ip2, hp2 in enumerate(prs):
                    for hh in range(2):
                        k = 2 * ip2 + hh
                        sl = trig_slot[0]
                        trig_slot[0] = (sl + 1) % NTRIG
                        nb = nscr[sl]
                        Bt = B_trig[sl]
                        kvb[k].wait_write("scalar")
                        Bt.wait_reads("scalar")
                        skv = em.op("scalar", lambda s, a=kva[k], nb=nb: s.copy(
                            nb[0:65, 0:512], a), inc="S")
                        kvb[k].read_by("S", skv)
                        Bt.written("S", skv)
                        drain_slots.append((sl, 2 * hp2 + hh))

            def drain_b(drain_slots=drain_slots):
                for ki, (sl, h) in enumerate(drain_slots):
                    nb = nscr[sl]
                    Bt = B_trig[sl]
                    psl = ki % 2
                    Bt.wait_write("tensor")
                    pk(3, psl).wait_reads("tensor")
                    mkt = None
                    for q in range(4):
                        mkt = mm(pb(3, psl)[:, q * 65:(q + 1) * 65],
                                 nb[0:65, q * 128:(q + 1) * 128],
                                 ident_bf[0:65, 0:65], True, True,
                                 inc="M" if q == 3 else None)
                    pk(3, psl).written("M", mkt)
                    Bt.read_by("M", mkt)
                    pk(3, psl).wait_write("scalar")
                    B_kv.wait_reads("scalar")
                    skv2 = em.op("scalar", lambda s, h=h, psl=psl: s.copy(
                        A_KV[:, h * 260:(h + 1) * 260], pb(3, psl)[:, 0:260]),
                        inc="S")
                    pk(3, psl).read_by("S", skv2)
                    B_kv.written("S", skv2)
                del drain_slots[:]

            if pending_drain[0] is not None:
                pending_drain[0]()
                pending_drain[1]()
            emit_vaug_proj_trig(0)
            emit_vaug_proj_trig(1)
            for b in kvb:
                b.wait_reads("tensor")
            for tch in range(16):
                if 1 <= tch < 15:
                    emit_vaug_proj_trig(tch + 1)
                emit_kv(tch)
            for k in range(4):
                kvb[k].written("M", em.cnt["M"])
            pending_drain = [drain_a, drain_b]
        pending_drain[0]()
        pending_drain[1]()

        if stop_after == "phaseA":
            return
        barrier("pre-q")
        load_and_transpose(wq_d, A_WL_Q, A_WT_Q)

        def qchunk(ic):
            return A_Q[:, ic * T:(ic + 1) * T]
        if stop_after == "q":
            proj_fm(A_WT_Q, qchunk, B_q)
            emit_norms(qchunk, B_q, sq_tm, B_sqt)
            return
        barrier("pre-sweep1")

        def qfu_ap(par, dc, lo, width):
            return A_QFU[:, par * 4096 + dc * 1024 + lo: par * 4096 + dc * 1024 + lo + width]
        B_qf2 = [Buf(em, "qf0"), Buf(em, "qf1")]
        steps = [(h, th2) for h in range(16) for th2 in range(2)]
        B_qc = [Buf(em, f"qc{ic}") for ic in range(NCH)]
        psn_q = pb(1, 1)[:, 0:256]

        def emit_qproj_tile(ic, tt):
            pB = pk(1, 0)
            pA = pb(1, 0)
            pB.wait_reads("tensor")
            B_wt.wait_write("tensor")
            B_ub.wait_write("tensor")
            mq = None
            for dc in range(NCH):
                lhs = A_WT_Q[:, dc * 1024 + ic * 128: dc * 1024 + (ic + 1) * 128]
                rhs = UB[:, dc * T + tt * 512: dc * T + (tt + 1) * 512]
                mq = mm(pA, lhs, rhs, dc == 0, dc == NCH - 1,
                        inc="M" if dc == NCH - 1 else None)
            pB.written("M", mq)
            B_wt.read_by("M", mq)
            B_ub.read_by("M", mq)
            pB.wait_write("scalar")
            sop = em.op("scalar", lambda s, ic=ic, tt=tt, pA=pA: s.copy(
                qchunk(ic)[:, tt * 512:(tt + 1) * 512], pA), inc="S")
            pB.read_by("S", sop)
            if tt == 3:
                B_qc[ic].written("S", sop)
                B_q.written("S", sop)

        def emit_qnorm_piece(c):
            B_qc[c].wait_write("vector")
            mq = None
            for hb in range(4):
                sl = trig_slot[0]
                trig_slot[0] = (sl + 1) % NTRIG
                nb = nscr[sl]
                Bt = B_trig[sl]
                Bt.wait_reads("vector")
                Bt.wait_reads("scalar")
                vn = em.op("vector", lambda v, c=c, hb=hb, nb=nb: v.tensor_mul(
                    nb[:, 0:512], qchunk(c)[:, hb * 512:(hb + 1) * 512],
                    qchunk(c)[:, hb * 512:(hb + 1) * 512]), inc="V")
                Bt.written("V", vn)
                B_qc[c].read_by("V", vn)
                em.wait("tensor", "V", vn)
                if c == 0 and hb == 0:
                    pk(1, 1).wait_reads("tensor")
                for tq in range(4):
                    tch = hb * 4 + tq
                    mq = mm(psn_q[:, tch * 16 + 2 * c: tch * 16 + 2 * c + 2],
                            nb[:, tq * 128:(tq + 1) * 128], pairsel[:],
                            True, True, inc="M" if tq == 3 else None)
                Bt.read_by("M", mq)
            if c == NCH - 1:
                pk(1, 1).written("M", mq)

        def emit_proj_trig(i):
            h, th2 = steps[i]
            hc, hr = h // 2, (h % 2) * 64
            par = th2
            pjq = 3 if i % 2 == 0 else 2
            B_qc[hc].wait_write("tensor")
            B_qf2[par].wait_reads("tensor")
            for mc in range(2):
                for nn in range(2):
                    tt = th2 * 2 + nn
                    pjb = pk(pjq, nn)
                    pja = pb(pjq, nn)
                    pjb.wait_reads("tensor")
                    lhs = rfT[:, (h % 2) * 256 + mc * 128:
                              (h % 2) * 256 + (mc + 1) * 128]
                    rhs = A_Q[:, hc * T + tt * 512: hc * T + (tt + 1) * 512]
                    mq = mm(pja, lhs, rhs, True, True, inc="M")
                    pjb.written("M", mq)
                    B_q.read_by("M", mq)
                    cosd = qfu_ap(par, mc, nn * 512, 512)
                    sind = qfu_ap(par, 2 + mc, nn * 512, 512)
                    st = trig_pipeline(pja, pjb, 512, sind, cosd)
                    B_qf2[par].written("S", st)

        def emit_qkv(i):
            h, th2 = steps[i]
            hc, hr = h // 2, (h % 2) * 64
            par = th2
            B_kv.wait_write("tensor")
            B_qf2[par].wait_write("tensor")
            qb0, qb1 = pk(0, 0), pk(0, 1)
            qb0.wait_reads("tensor")
            qb1.wait_reads("tensor")
            mq = None
            for tti in range(2):
                qa = pb(0, tti)[0:65, :]
                for dc in range(4):
                    lhs = A_KV[:, h * 260 + dc * 65: h * 260 + (dc + 1) * 65]
                    rhs = qfu_ap(par, dc, tti * 512, 512)
                    mq = mm(qa, lhs, rhs, dc == 0, dc == 3,
                            inc="M" if (dc == 3 and tti == 1) else None)
            qb0.written("M", mq)
            qb1.written("M", mq)
            B_kv.read_by("M", mq)
            B_qf2[par].read_by("M", mq)
            qb0.wait_write("vector")
            qb0.wait_write("scalar")
            if th2 == 0:
                B_zst.wait_reads("scalar")
                B_q.wait_reads("vector")
            dst = A_Q[:, hc * T + th2 * 1024:
                      hc * T + (th2 + 1) * 1024][hr:hr + 64, :]
            szv = em.op("vector", lambda v, d=dst: v.tensor_copy(
                d, PB[0][0:64, 0:1024]), inc="V")
            sz = em.op("scalar", lambda s, th2=th2: s.copy(
                A_ZST[64:65, th2 * 1024:(th2 + 1) * 1024],
                PB[0][64:65, 0:1024]), inc="S")
            qb0.read_by("V", szv)
            qb0.read_by("S", sz)
            qb1.read_by("V", szv)
            qb1.read_by("S", sz)
            if th2 == 1:
                B_zst.written("S", sz)
                B_q.written("V", szv)
                B_zfm.wait_reads("gpsimd")
                B_zst.wait_write("gpsimd")
                cz = em.op("gpsimd", lambda g, h=h: g.dma_start(
                    A_ZFM[h:h + 1, :], A_ZST[64:65, :]), inc="dG", incby=16)
                B_zst.read_by("dG", cz)
                if h == 15:
                    B_zfm.written("dG", cz)

        for ic in range(2):
            for tt in range(4):
                emit_qproj_tile(ic, tt)
            emit_qnorm_piece(ic)
        emit_proj_trig(0)
        for i in range(32):
            hc = i // 4
            if hc + 2 < NCH:
                emit_qproj_tile(hc + 2, i % 4)
                if i % 4 == 3:
                    emit_qnorm_piece(hc + 2)
            if i < 31:
                emit_proj_trig(i + 1)
            emit_qkv(i)
        pk(1, 1).wait_write("scalar")
        B_sqt.wait_reads("scalar")
        lastn = None
        for tch in range(16):
            lastn = em.op("scalar", lambda s, tch=tch: s.activation(
                sq_tm[:, tch * H:(tch + 1) * H],
                psn_q[:, tch * 16:tch * 16 + 16],
                AF.Exp, scale=f32(-0.5), bias=f32(-LNSM)), inc="S")
        pk(1, 1).read_by("S", lastn)
        B_sqt.written("S", lastn)

        if stop_after == "sweep1":
            return
        B_qfu.wait_reads("gpsimd")
        B_wl.wait_reads("gpsimd")
        src_wo = wo_d.rearrange("(c p) d -> p c d", p=128)
        dst_wo = A_WO.rearrange("p (c d) -> p c d", c=NCH)
        cwo = em.op("gpsimd", lambda g: g.dma_start(dst_wo, src_wo),
                    inc="dW", incby=16)
        B_wl.written("dW", cwo)
        B_wl.wait_write("tensor")
        B_wt.wait_reads("tensor")
        B_wt.wait_reads("scalar")
        wo_sq = [None]

        def emit_wo_piece(i):
            dc, icq = i // 2, i % 2
            pB = pk(1, icq)
            pA = pb(1, icq)
            pB.wait_reads("tensor")
            mq = None
            for ici in range(4):
                ic = icq * 4 + ici
                lhs = A_WO[:, ic * 1024 + dc * 128: ic * 1024 + (dc + 1) * 128]
                mq = mm(pA[:, ici * 128:(ici + 1) * 128], lhs, ident_bf[:],
                        True, True, inc="M" if ici == 3 else None)
            pB.written("M", mq)
            B_wl.read_by("M", mq)
            pB.wait_write("scalar")
            wo_sq[0] = em.op("scalar", lambda s, pA=pA, dc=dc, icq=icq: s.copy(
                A_WOT[:, dc * 1024 + icq * 512: dc * 1024 + (icq + 1) * 512],
                pA), inc="S")
            pB.read_by("S", wo_sq[0])

        B_zfm.wait_write("tensor")
        B_sqt.wait_write("vector")
        vw = None
        for tch in range(16):
            emit_wo_piece(tch)
            pB = pk(2, tch % 2)
            pA = pb(2, tch % 2)
            pB.wait_reads("tensor")
            mz = mm(pA[:, 0:16], A_ZFM[0:16, tch * 128:(tch + 1) * 128],
                    ident_bf[0:16, 0:16], True, True, inc="M")
            pB.written("M", mz)
            pB.wait_write("vector")
            B_wtm.wait_reads("vector")
            sqsl = sq_tm[:, tch * H:(tch + 1) * H]
            vh(lambda v, pA=pA, sq=sqsl: v.tensor_tensor(
                wsc_f[:, 0:16], sq, pA[:, 0:16], ALU.mult))
            vh(lambda v: v.tensor_scalar(
                wsc_f[:, 0:16], wsc_f[:, 0:16], 1e-6, None, ALU.max))
            vh(lambda v: v.reciprocal(wsc_f[:, 16:32], wsc_f[:, 0:16]))
            vw = vh(lambda v, sq=sqsl, tch=tch: v.tensor_tensor(
                w_tm[:, tch * H:(tch + 1) * H], sq, wsc_f[:, 16:32], ALU.mult))
            pB.read_by("V", vw)
        B_wt.written("S", wo_sq[0])
        B_wtm.written("V", vw)
        B_wtm.wait_write("tensor")
        vap = None
        for h in range(16):
            hc, hr = h // 2, (h % 2) * 64
            for tt in range(4):
                pB = pk(3, tt % 2)
                pA = pb(3, tt % 2)
                pB.wait_reads("tensor")
                mwb = None
                for tq in range(4):
                    tch = tt * 4 + tq
                    wcol = w_tm[:, tch * H + h: tch * H + h + 1]\
                        .to_broadcast([128, 64])
                    mwb = mm(pA[0:64, tq * 128:(tq + 1) * 128], wcol, ident_bf[:],
                             True, True, inc="M" if tq == 3 else None)
                pB.written("M", mwb)
                pB.wait_write("vector")
                ap_ = A_Q[:, hc * T + tt * 512:
                          hc * T + (tt + 1) * 512][hr:hr + 64, :]
                vap = em.op("vector", lambda v, a=ap_, pA=pA: v.tensor_tensor(
                    a, a, pA[0:64, :], ALU.mult), inc="V")
                pB.read_by("V", vap)
        B_q.written("V", vap)

        if stop_after == "sweep2":
            return
        B_wt.wait_write("tensor")
        B_q.wait_write("tensor")
        for jc in range(NCH):
            for tt in range(4):
                pB = pk(1, tt % 2)
                pA = pb(1, tt % 2)
                pB.wait_reads("tensor")
                mq = None
                for ic in range(NCH):
                    lhs = A_WOT[:, ic * 1024 + jc * 128: ic * 1024 + (jc + 1) * 128]
                    rhs = A_Q[:, ic * T + tt * 512: ic * T + (tt + 1) * 512]
                    mq = mm(pA, lhs, rhs, ic == 0, ic == NCH - 1,
                            inc="M" if ic == NCH - 1 else None)
                pB.written("M", mq)
                B_q.read_by("M", mq)
                B_wt.read_by("M", mq)
                pB.wait_write("vector")
                B_x[jc].wait_write("vector")
                x_ap = x_sb[:, jc * T + tt * 512: jc * T + (tt + 1) * 512]
                vx = em.op("vector", lambda v, x_ap=x_ap, pA=pA: v.tensor_add(
                    x_ap, x_ap, pA), inc="V")
                pB.read_by("V", vx)
                B_x[jc].written("V", vx)


    if mode == "ln1":
        emit_layernorm()
        emit_output_from_ub()
    elif mode == "ffn1":
        emit_layernorm()
        emit_ffn(0)
        barrier("out")
        for c in range(NCH):
            emit_out_chunk(c)
        finish_output()
    elif mode == "rt":
        emit_layernorm()
        emit_routing()
        barrier("out")
        for c in range(NCH):
            emit_out_chunk(c)
        finish_output()
    elif mode == "dbg_ineg":
        emit_layernorm()
        emit_routing()
        barrier("out")
        em.op("gpsimd", lambda g: g.dma_start(out_r[0][0:8, :], INEGB[0:8, :]),
              inc="dG", incby=16)
        em.op("gpsimd", lambda g: g.dma_start(out_r[1][0:8, :], LOGB[0:8, :]),
              inc="dG", incby=16)
        em.op("gpsimd", lambda g: g.dma_start(out_r[2][:, 0:128], lt_sb[:]),
              inc="dG", incby=16)
        em.op("gpsimd", lambda g: g.dma_start(out_r[3][:, 0:128], pe_bf[:]),
              inc="dG", incby=16)
        em.op("gpsimd", lambda g: g.dma_start(out_r[4][0:8, 0:1024], maskmat[:]),
              inc="dG", incby=16)
        em.wait("gpsimd", "dG", em.cnt["dG"])
    elif mode == "ffn_sh":
        emit_layernorm()
        emit_ffn(0, max_groups=1, max_th=2, skip_routing=True)
        barrier("out")
        for c in range(NCH):
            emit_out_chunk(c)
        finish_output()
    elif mode == "ffn_g1":
        emit_layernorm()
        emit_ffn(0, max_groups=1, max_th=2)
        barrier("out")
        for c in range(NCH):
            emit_out_chunk(c)
        finish_output()
    elif mode.startswith("attn_"):
        emit_layernorm()
        emit_ffn(0)
        emit_attention(stop_after=mode[5:])
        barrier("out")
        for c in range(NCH):
            emit_out_chunk(c)
        finish_output()
    elif mode == "attn":
        emit_layernorm()
        emit_ffn(0)
        emit_attention()
        barrier("out")
        for c in range(NCH):
            emit_out_chunk(c)
        finish_output()
    else:
        emit_layernorm()
        emit_ffn(0)
        emit_attention()
        emit_layernorm(home=LN_HOME_B)
        barrier("pre-ffn2")
        emit_ffn(1, emit_out_tail=True)
        finish_output()

    with nc.Block() as block:
        @block.tensor
        def _(t):
            for fn in em.ops["tensor"]:
                fn(t)

        @block.vector
        def _(v):
            for fn in em.ops["vector"]:
                fn(v)

        @block.scalar
        def _(s):
            for fn in em.ops["scalar"]:
                fn(s)

        @block.gpsimd
        def _(g):
            for fn in em.ops["gpsimd"]:
                fn(g)

        @block.sync
        def _(sy):
            for fn in em.ops["sync"]:
                fn(sy)

    ctx.close()
    return nc


_NC_CACHE = {}


def get_nc(mode="full"):
    if mode not in _NC_CACHE:
        _NC_CACHE[mode] = build_program(mode)
    return _NC_CACHE[mode]


def make_in_maps(L0_attn_x, L0_attn_wq, L0_attn_wk, L0_attn_wv, L0_attn_wo,
                 L0_attn_rf, L0_ffn_Ws1, L0_ffn_Ws2, L0_ffn_W1, L0_ffn_W2,
                 L0_ffn_Wr):
    cst = np.zeros((128, 1152), dtype=np.float32)
    cst[:, 0:128] = np.eye(128, dtype=np.float32)
    for e in range(8):
        cst[e, 128 + e * 128:128 + (e + 1) * 128] = -BIG
    shared = {
        "wq": np.ascontiguousarray(L0_attn_wq, dtype=np.float32),
        "wk": np.ascontiguousarray(L0_attn_wk, dtype=np.float32),
        "wv": np.ascontiguousarray(L0_attn_wv, dtype=np.float32),
        "wo": np.ascontiguousarray(L0_attn_wo, dtype=np.float32),
        "rf": np.ascontiguousarray(L0_attn_rf, dtype=np.float32),
        "ws1": np.ascontiguousarray(L0_ffn_Ws1, dtype=np.float32),
        "ws2": np.ascontiguousarray(L0_ffn_Ws2, dtype=np.float32),
        "w1": np.ascontiguousarray(L0_ffn_W1, dtype=np.float32),
        "w2": np.ascontiguousarray(L0_ffn_W2, dtype=np.float32),
        "wr": np.ascontiguousarray(L0_ffn_Wr, dtype=np.float32),
        "cst": cst,
    }
    in_maps = []
    for b in range(NCORES):
        m = dict(shared)
        m["x"] = np.ascontiguousarray(L0_attn_x[b].T, dtype=np.float32)
        in_maps.append(m)
    return in_maps


def kernel(**inputs):
    nc = get_nc("full")
    in_maps = make_in_maps(**inputs)
    res = run_bass_kernel_spmd(nc, in_maps, core_ids=list(range(NCORES)))
    out = np.stack([np.ascontiguousarray(res.results[b]["out"].T)
                    for b in range(NCORES)], axis=0)
    return out


# version 34
# speedup vs baseline: 1.2170x; 1.0588x over previous
import math
from contextlib import ExitStack

import numpy as np
import concourse.bass as bass
import concourse.mybir as mybir
from concourse.bass_utils import run_bass_kernel_spmd

F32 = mybir.dt.float32
BF16 = mybir.dt.bfloat16
AF = mybir.ActivationFunctionType
ALU = mybir.AluOpType
AX = mybir.AxisListType

D, T, NCH = 1024, 2048, 8
H, DH, MRF = 16, 64, 256
NE, FD = 8, 512
NCORES = 8

PI_F = float(np.float32(np.pi))
TWO_PI = float(np.float32(2 * np.pi))
INV2PI = float(np.float32(1.0 / (2 * np.pi)))
MAGIC = float(np.float32(12582912.0))
SCALE_2PI = float(np.float32(6.2831783))
BIG = 1.0e20
LNSM = float(np.float32(math.log(math.sqrt(MRF))))
EPS = 1.0e-5
f32 = np.float32

SEM_OWNER = {"M": "tensor", "V": "vector", "S": "scalar", "G": "gpsimd"}


class Emitter:
    ENGINES = ("tensor", "vector", "scalar", "gpsimd", "sync")

    def __init__(self, nc, ctx):
        self.nc, self.ctx = nc, ctx
        self.ops = {e: [] for e in self.ENGINES}
        self.sems, self.cnt, self.lastwait = {}, {}, {}

    def sem(self, name):
        self.sems[name] = self.ctx.enter_context(self.nc.semaphore(name))
        self.cnt[name] = 0

    def wait(self, eng, semname, val, force=False):
        if val is None or val <= 0 or (SEM_OWNER.get(semname) == eng and not force):
            return
        key = (eng, semname)
        if self.lastwait.get(key, -1) >= val:
            return
        self.lastwait[key] = val
        h = self.sems[semname]
        self.ops[eng].append(lambda en, h=h, v=val: en.wait_ge(h, v))

    def op(self, eng, fn, inc=None, incby=1):
        if inc is None:
            self.ops[eng].append(fn)
            return None
        h = self.sems[inc]
        self.cnt[inc] += incby
        v = self.cnt[inc]

        def run(en, fn=fn, h=h, incby=incby):
            fn(en).then_inc(h, incby)
        self.ops[eng].append(run)
        return v


class Buf:
    def __init__(self, em, name=""):
        self.em, self.name = em, name
        self.w = None
        self.r = []

    def wait_write(self, eng):
        if self.w:
            self.em.wait(eng, *self.w)

    def read_by(self, sem, cnt):
        if cnt is not None:
            self.r.append((sem, cnt))

    def wait_reads(self, eng):
        for (s, c) in self.r:
            self.em.wait(eng, s, c)
        if self.w:
            self.em.wait(eng, *self.w)

    def written(self, sem, cnt):
        if cnt is not None:
            self.w = (sem, cnt)
            self.r = []


def build_program(mode="full"):
    nc = bass.Bass("TRN2", target_bir_lowering=False, detect_race_conditions=False)

    x_in = nc.dram_tensor("x", [D, T], F32, kind="ExternalInput").ap()
    wq_d = nc.dram_tensor("wq", [D, D], F32, kind="ExternalInput").ap()
    wk_d = nc.dram_tensor("wk", [D, D], F32, kind="ExternalInput").ap()
    wv_d = nc.dram_tensor("wv", [D, D], F32, kind="ExternalInput").ap()
    wo_d = nc.dram_tensor("wo", [D, D], F32, kind="ExternalInput").ap()
    rf_d = nc.dram_tensor("rf", [MRF, DH], F32, kind="ExternalInput").ap()
    ws1_d = nc.dram_tensor("ws1", [D, FD], F32, kind="ExternalInput").ap()
    ws2_d = nc.dram_tensor("ws2", [FD, D], F32, kind="ExternalInput").ap()
    w1_d = nc.dram_tensor("w1", [NE, D, FD], F32, kind="ExternalInput").ap()
    w2_d = nc.dram_tensor("w2", [NE, FD, D], F32, kind="ExternalInput").ap()
    wr_d = nc.dram_tensor("wr", [D, NE], F32, kind="ExternalInput").ap()
    cst_d = nc.dram_tensor("cst", [128, 1152], F32, kind="ExternalInput").ap()
    out_d = nc.dram_tensor("out", [D, T], F32, kind="ExternalOutput").ap()

    ctx = ExitStack()
    SB = lambda name, shape, dt: ctx.enter_context(nc.sbuf_tensor(name, shape, dt))
    PSA = lambda name, shape: ctx.enter_context(nc.psum_tensor(name, shape, F32))

    em = Emitter(nc, ctx)
    for s in ("dW", "dX", "dG", "M", "V", "S", "G"):
        em.sem(s)

    x_sb = SB("x_sb", [128, NCH * T], F32)
    ident_bf = SB("ident_bf", [128, 128], BF16)
    rfT = SB("rfT", [128, 2 * MRF], BF16)
    rf_raw = SB("rf_raw", [128, 2 * DH], BF16)
    ones_row = SB("ones_row", [1, 512], BF16)
    ones_col = SB("ones_col", [128, 1], BF16)
    e0_col = SB("e0_col", [128, 2], BF16)
    e1_col = SB("e1_col", [128, 2], BF16)
    pairsel = SB("pairsel", [128, 2], BF16)
    maskmat = SB("maskmat", [8, 8 * 128], BF16)
    iota8 = SB("iota8", [128, 8], F32)
    wrb = SB("wrb", [128, NCH * 8], BF16)
    m_sl = SB("m_sl", [1, 512], F32)
    t_sl = SB("t_sl", [1, 512], F32)
    var_sl = SB("var_sl", [1, 512], F32)
    a_sl = SB("a_sl", [1, 512], F32)
    abf_sl = SB("abf_sl", [1, 512], BF16)
    bbf_sl = SB("bbf_sl", [1, 512], BF16)
    lt_sb = SB("lt_sb", [128, 128], F32)
    rs_sb = SB("rs_sb", [128, 128], F32)
    max1 = SB("max1", [128, 16], F32)
    idm = SB("idm", [128, 16], F32)
    max2 = SB("max2", [128, 16], F32)
    t1_sb = SB("t1_sb", [128, 16], F32)
    pe_tm = SB("pe_tm", [128, 128], F32)
    pe_bf = SB("pe_bf", [128, 128], BF16)
    sk_tm = SB("sk_tm", [128, 16 * H], F32)
    sq_tm = SB("sq_tm", [128, 16 * H], F32)
    skb_tm = SB("skb_tm", [128, 16 * H], BF16)
    w_tm = SB("w_tm", [128, 16 * H], BF16)
    wsc_f = SB("wsc_f", [128, 32], F32)
    NTRIG = 4
    nscr_t = SB("nscr", [128, 2048], BF16)
    dscr_t = SB("dscr", [128, 2048], BF16)
    nscr = [nscr_t[:, i * 512: min(i * 512 + 1024, 2048)] for i in range(NTRIG)]
    dscr = [dscr_t[:, i * 512: min(i * 512 + 1024, 2048)] for i in range(NTRIG)]
    iscr_s = SB("iscr", [128, 512], mybir.dt.int32)
    iscr = [iscr_s] * NTRIG
    vas_sb = [SB(f"vas{i}", [128, 416], BF16) for i in range(2)]

    ACOLS = 57472
    arena = SB("arena", [128, ACOLS], BF16)

    def ar(c0, cols):
        return arena[:, c0:c0 + cols]

    UB = ar(0, NCH * T)
    LN_HOME_A = ((ar(51200, 2048), ar(53248, 2048)),
                 [ar(55296, 512), ar(55808, 512)],
                 [ar(56320, 512), ar(56832, 512)])
    LN_HOME_B = ((ar(24576, 2048), ar(28736, 2048)),
                 [ar(26624, 512), ar(27136, 512)],
                 [ar(27648, 512), ar(28160, 512)])
    HS = [ar(16384 + i * 4096, 4096) for i in range(3)]
    W1S = [ar(28672 + i * 4096, 4096) for i in range(2)]
    W2S = [ar(36864 + i * 4096, 4096) for i in range(3)]
    RBUF = [ar(49152 + i * 1024, 1024) for i in range(2)]
    LOGB = ar(51200, T)
    INEGB = ar(53248, T)
    A_WL = ar(16384, 8192)
    A_WT = ar(24576, 8192)
    A_V = ar(32832, 16384)

    def A_K(c):
        if c < 4:
            return ar(16384 + c * T, T)
        return ar(49216 + (c - 4) * T, T)
    A_KV = ar(24576, H * 4 * 65)
    A_KF = [ar(28736, 2048), ar(30784, 2048)]
    A_NSQ = ar(28736, 512)
    A_VAS = vas_sb
    A_ZST = ar(28736, 2048)
    A_ZFM = ar(30784, 2048)
    A_Q = ar(32832, 16384)
    A_WL_Q = ar(16384, 8192)
    A_WT_Q = ar(49216, 8192)
    A_QFU = ar(16384, 8192)
    A_WO = ar(16384, 8192)
    A_WOT = ar(49216, 8192)

    PB = [PSA(f"pb{i}", [128, 1024]) for i in range(4)]
    pbuf = [Buf(em, f"pb{i // 2}.{i % 2}") for i in range(8)]

    def pb(i, half, cols=512):
        return PB[i][:, half * 512: half * 512 + cols]

    def pk(i, half):
        return pbuf[2 * i + half]

    class Two:
        def __init__(self, i):
            self.a, self.b = pbuf[2 * i], pbuf[2 * i + 1]

        def wait_write(self, eng):
            self.a.wait_write(eng); self.b.wait_write(eng)

        def read_by(self, sem, cnt):
            self.a.read_by(sem, cnt); self.b.read_by(sem, cnt)

        def wait_reads(self, eng):
            self.a.wait_reads(eng); self.b.wait_reads(eng)

        def written(self, sem, cnt):
            self.a.written(sem, cnt); self.b.written(sem, cnt)

    def pk2(i):
        return Two(i)

    def pb2(i):
        return PB[i][:, :]

    B_ub = Buf(em, "ub")
    B_x = [Buf(em, f"x{c}") for c in range(NCH)]
    B_sq = [Buf(em, "sq0"), Buf(em, "sq1")]
    B_ab = [Buf(em, "ab0"), Buf(em, "ab1")]
    B_lnsl = Buf(em, "lnsl")
    B_hs = [Buf(em, f"hs{i}") for i in range(3)]
    B_w1 = [Buf(em, "w1a"), Buf(em, "w1b")]
    B_w2 = [Buf(em, f"w2{i}") for i in range(3)]
    B_rb = [Buf(em, "rb0"), Buf(em, "rb1")]
    B_logb = Buf(em, "logb")
    B_inegb = Buf(em, "inegb")
    B_wrb = Buf(em, "wrb")
    B_rfT = Buf(em, "rfT")
    B_wl = Buf(em, "wl")
    B_wt = Buf(em, "wt")
    B_v = Buf(em, "v")
    B_k = Buf(em, "k")
    B_kv = Buf(em, "kv")
    B_kf = [Buf(em, "kf0"), Buf(em, "kf1")]
    B_nsq = Buf(em, "nsq")
    B_vas = [Buf(em, "vas0"), Buf(em, "vas1")]
    B_q = Buf(em, "q")
    B_qfu = Buf(em, "qfu")
    B_zst = None
    B_zfm = Buf(em, "zfm")
    B_sk = Buf(em, "sk")
    B_skb = Buf(em, "skb")
    B_sqt = Buf(em, "sqt")
    B_wtm = Buf(em, "wtm")
    B_trig = [Buf(em, f"trig{i}") for i in range(4)]
    trig_slot = [0]
    B_rt = Buf(em, "routing")
    B_zst = Buf(em, "zst")

    def barrier(tag=""):
        for eng in Emitter.ENGINES:
            for sname in ("dW", "dX", "dG", "M", "V", "S"):
                em.wait(eng, sname, em.cnt[sname])

    def vh(fn):
        c = em.op("vector", fn, inc="V")
        em.wait("vector", "V", c, force=True)
        return c

    def mm(out_ap, lhsT_ap, rhs_ap, start, stop, inc=None, tp=None):
        def fn(en, o=out_ap, l=lhsT_ap, r=rhs_ap, st=start, sp=stop, tp=tp):
            return en.matmul(o, l, r, start=st, stop=sp, tile_position=tp)
        return em.op("tensor", fn, inc=inc)

    def scopy(dst_ap, src_ap, inc="S", scale=None, func=AF.Copy, bias=None):
        kw = {}
        if scale is not None:
            kw["scale"] = f32(scale)
        if bias is not None:
            kw["bias"] = f32(bias)

        def fn(en, d=dst_ap, s=src_ap, func=func, kw=kw):
            return en.activation(d, s, func, **kw)
        return em.op("scalar", fn, inc=inc)

    em.op("gpsimd", lambda g: g.dma_start(ident_bf[:], cst_d[:, 0:128]), inc="dG", incby=16)
    em.op("gpsimd", lambda g: g.dma_start(maskmat[:], cst_d[0:8, 128:1152]), inc="dG", incby=16)
    rf_r = rf_d.rearrange("(c p) d -> c p d", p=128)
    em.op("gpsimd", lambda g: g.dma_start(rf_raw[:, 0:DH], rf_r[0]), inc="dG", incby=16)
    c_rf = em.op("gpsimd", lambda g: g.dma_start(rf_raw[:, DH:2 * DH], rf_r[1]),
                 inc="dG", incby=16)

    x_r = x_in.rearrange("(c p) t -> p c t", p=128)
    for hf in range(2):
        src = x_r[:, 4 * hf:4 * hf + 4, :]
        dst = x_sb[:, hf * 4 * T:(hf + 1) * 4 * T].rearrange("p (c t) -> p c t", c=4)
        cx = em.op("sync", lambda sy, d=dst, s=src: sy.dma_start(d, s),
                   inc="dX", incby=16)
        for c in range(4 * hf, 4 * hf + 4):
            B_x[c].written("dX", cx)

    for fn in (
        lambda v: v.memset(ones_row[:], 1.0),
        lambda v: v.memset(ones_col[:], 1.0),
        lambda v: v.memset(e0_col[:, 0:1], 1.0),
        lambda v: v.memset(e0_col[:, 1:2], 0.0),
        lambda v: v.memset(e1_col[:, 0:1], 0.0),
        lambda v: v.memset(e1_col[:, 1:2], 1.0),
        lambda v: v.memset(pairsel[:], 0.0),
        lambda v: v.memset(pairsel[0:64, 0:1], 1.0),
        lambda v: v.memset(pairsel[64:128, 1:2], 1.0),
    ):
        em.op("vector", fn)
    c_consts = None
    for e in range(8):
        c_consts = em.op("vector", lambda v, e=e: v.memset(iota8[:, e:e + 1], float(e)),
                         inc="V")

    vz = em.op("vector", lambda v: v.memset(rfT[:], 0.0), inc="V")
    em.wait("tensor", "dG", c_rf)
    pk2(0).wait_reads("tensor")
    c_t = None
    for mc in range(2):
        mm(PB[0][0:64, mc * 128:(mc + 1) * 128],
           rf_raw[:, mc * DH:(mc + 1) * DH], ident_bf[:], True, True)
        c_t = mm(PB[0][64:128, mc * 128:(mc + 1) * 128],
                 rf_raw[:, mc * DH:(mc + 1) * DH], ident_bf[:], True, True, inc="M")
    pk2(0).written("M", c_t)
    pk2(0).wait_write("scalar")
    em.wait("scalar", "V", vz)
    scopy(rfT[0:64, 0:256], PB[0][0:64, 0:256], scale=INV2PI)
    c_rfT = scopy(rfT[64:128, 256:512], PB[0][64:128, 0:256], scale=INV2PI)
    pk2(0).read_by("S", c_rfT)
    B_rfT.written("S", c_rfT)

    def emit_layernorm(home=None):
        sqs, abuf, bbuf = home if home else LN_HOME_A
        sq_scr = [sqs[0], sqs[1]]
        stA = [pb(0, 0), pb(0, 1), pb(1, 0), pb(1, 1)]
        stB = [pb(2, 0), pb(2, 1), pb(3, 0), pb(3, 1)]
        kA = [pk(0, 0), pk(0, 1), pk(1, 0), pk(1, 1)]
        kB = [pk(2, 0), pk(2, 1), pk(3, 0), pk(3, 1)]
        for b in kA + kB:
            b.wait_reads("tensor")
        for c in range(NCH):
            B_x[c].wait_write("vector")
            B_sq[c % 2].wait_reads("vector")
            if c < 2:
                B_ub.wait_reads("scalar")
                B_ub.wait_reads("vector")
                B_logb.wait_reads("vector")
                B_kv.wait_reads("vector")
            xb_ap = UB[:, c * T:(c + 1) * T]
            em.op("vector", lambda v, a=xb_ap, c=c: v.tensor_copy(
                a, x_sb[:, c * T:(c + 1) * T]))
            vs = em.op("vector", lambda v, a=xb_ap, c=c: v.tensor_mul(
                sq_scr[c % 2][:], a, a), inc="V")
            em.wait("tensor", "V", vs)
            last = None
            for tt in range(4):
                mm(stA[tt][0:1, :], ones_col[:, 0:1],
                   xb_ap[:, tt * 512:(tt + 1) * 512], c == 0, c == NCH - 1)
                last = mm(stB[tt][0:1, :], ones_col[:, 0:1],
                          sq_scr[c % 2][:, tt * 512:(tt + 1) * 512],
                          c == 0, c == NCH - 1,
                          inc="M" if tt == 3 else None)
            B_sq[c % 2].written("V", vs)
            B_sq[c % 2].read_by("M", last)
        for b in kA + kB:
            b.written("M", em.cnt["M"])
        uv = None
        for tt in range(4):
            kA[tt].wait_write("scalar")
            kB[tt].wait_write("vector")
            B_lnsl.wait_reads("scalar")
            em.op("scalar", lambda s, st=stA[tt]: s.activation(
                m_sl[:], st[0:1, :], AF.Identity, scale=f32(1.0 / D)))
            sm2 = em.op("scalar", lambda s: s.activation(
                t_sl[:], m_sl[:], AF.Square), inc="S")
            kA[tt].read_by("S", sm2)
            em.wait("vector", "S", sm2)
            B_lnsl.wait_reads("vector")
            vv = em.op("vector", lambda v, st=stB[tt]: v.scalar_tensor_tensor(
                var_sl[:], st[0:1, :], 1.0 / D, t_sl[:], ALU.mult,
                ALU.subtract), inc="V")
            kB[tt].read_by("V", vv)
            em.wait("scalar", "V", vv)
            em.op("scalar", lambda s: s.activation(
                t_sl[:], var_sl[:], AF.Ln, bias=f32(EPS)))
            sv = em.op("scalar", lambda s: s.activation(
                a_sl[:], t_sl[:], AF.Exp, scale=f32(-0.5)), inc="S")
            em.wait("vector", "S", sv)
            em.op("vector", lambda v: v.tensor_copy(abf_sl[:], a_sl[:]))
            vb = em.op("vector", lambda v: v.scalar_tensor_tensor(
                bbf_sl[:], m_sl[:], -1.0, a_sl[:], ALU.mult, ALU.mult),
                inc="V")
            B_lnsl.written("V", vb)
            B_lnsl.read_by("V", vb)
            em.wait("tensor", "V", vb)
            kA[tt].wait_reads("tensor")
            kB[tt].wait_reads("tensor")
            mm(stA[tt], ones_row[0:1, 0:128], abf_sl[0:1, :], True, True)
            mab = mm(stB[tt], ones_row[0:1, 0:128], bbf_sl[0:1, :], True, True,
                     inc="M")
            kA[tt].written("M", mab)
            kB[tt].written("M", mab)
            B_lnsl.read_by("M", mab)
            kA[tt].wait_write("scalar")
            kB[tt].wait_write("scalar")
            B_ab[tt % 2].wait_reads("scalar")
            em.op("scalar", lambda s, tt=tt: s.copy(abuf[tt % 2][:], stA[tt]))
            sab = em.op("scalar", lambda s, tt=tt: s.copy(bbuf[tt % 2][:], stB[tt]),
                        inc="S")
            kA[tt].read_by("S", sab)
            kB[tt].read_by("S", sab)
            B_ab[tt % 2].written("S", sab)
            em.wait("vector", "S", sab)
            for c in range(NCH):
                u_ap = UB[:, c * T + tt * 512: c * T + (tt + 1) * 512]
                em.op("vector", lambda v, u=u_ap, tt=tt: v.tensor_mul(
                    u, u, abuf[tt % 2][:]))
                uv = em.op("vector", lambda v, u=u_ap, tt=tt: v.tensor_add(
                    u, u, bbuf[tt % 2][:]), inc="V")
            B_ab[tt % 2].read_by("V", uv)
        B_ub.written("V", uv)

    def emit_routing():
        wr_r = wr_d.rearrange("(c p) e -> p c e", p=128)
        B_wrb.wait_reads("gpsimd")
        cw = em.op("gpsimd", lambda g: g.dma_start(
            wrb[:].rearrange("p (c e) -> p c e", c=NCH), wr_r), inc="dW", incby=16)
        B_wrb.written("dW", cw)
        B_ub.wait_write("tensor")
        B_wrb.wait_write("tensor")
        se = None
        for tt in range(4):
            lgb = pk(3, tt % 2)
            lga = pb(3, tt % 2)
            lgb.wait_reads("tensor")
            mq = None
            for c in range(NCH):
                mq = mm(lga[0:8, :], wrb[:, c * 8:(c + 1) * 8],
                        UB[:, c * T + tt * 512: c * T + (tt + 1) * 512],
                        c == 0, c == NCH - 1, inc="M" if c == NCH - 1 else None)
            lgb.written("M", mq)
            B_ub.read_by("M", mq)
            B_wrb.read_by("M", mq)
            lgb.wait_write("scalar")
            B_logb.wait_reads("scalar")
            se = em.op("scalar", lambda s, tt=tt, lga=lga: s.copy(
                LOGB[0:8, tt * 512:(tt + 1) * 512], lga[0:8, :]), inc="S")
            lgb.read_by("S", se)
        B_logb.written("S", se)
        B_logb.wait_write("tensor")
        pk(2, 0).wait_reads("tensor")
        mlt = None
        for tch in range(16):
            mlt = mm(pb(2, 0)[:, tch * 8:(tch + 1) * 8],
                     LOGB[0:8, tch * 128:(tch + 1) * 128], ident_bf[0:8, 0:8],
                     True, True, inc="M" if tch == 15 else None)
        pk(2, 0).written("M", mlt)
        B_logb.read_by("M", mlt)
        pk(2, 0).wait_write("vector")
        B_rt.wait_reads("vector")
        vl = em.op("vector", lambda v: v.tensor_copy(lt_sb[:], pb(2, 0)[:, 0:128]),
                   inc="V")
        pk(2, 0).read_by("V", vl)
        lt3 = lt_sb[:].rearrange("p (g e) -> p g e", e=8)
        rs3 = rs_sb[:].rearrange("p (g e) -> p g e", e=8)
        pe3 = pe_tm[:].rearrange("p (g e) -> p g e", e=8)
        m1b = max1[:].unsqueeze(2).to_broadcast([128, 16, 8])
        idb = idm[:].unsqueeze(2).to_broadcast([128, 16, 8])
        t1b = t1_sb[:].unsqueeze(2).to_broadcast([128, 16, 8])
        io3 = iota8[:].unsqueeze(1).to_broadcast([128, 16, 8])
        em.wait("vector", "V", c_consts)
        vh(lambda v: v.tensor_reduce(max1[:], lt3, AX.X, ALU.max))
        vh(lambda v: v.tensor_tensor(rs3, lt3, m1b, ALU.is_equal))
        vh(lambda v: v.scalar_tensor_tensor(
            rs3, rs3, -8.0, io3, ALU.mult, ALU.add))
        vh(lambda v: v.tensor_reduce(idm[:], rs3, AX.X, ALU.min))
        vh(lambda v: v.tensor_tensor(pe3, rs3, idb, ALU.is_equal))
        vh(lambda v: v.scalar_tensor_tensor(
            rs3, pe3, -1e30, lt3, ALU.mult, ALU.add))
        vh(lambda v: v.tensor_reduce(max2[:], rs3, AX.X, ALU.max))
        vg = vh(lambda v: v.tensor_sub(max2[:], max2[:], max1[:]))
        em.wait("scalar", "V", vg)
        sg = em.op("scalar", lambda s: s.activation(max2[:], max2[:], AF.Exp),
                   inc="S")
        em.wait("vector", "S", sg)
        vh(lambda v: v.tensor_scalar(
            max2[:], max2[:], 1.0, None, ALU.add))
        vh(lambda v: v.reciprocal(t1_sb[:], max2[:]))
        vh(lambda v: v.tensor_scalar(
            t1_sb[:], t1_sb[:], 1.0, None, ALU.is_ge))
        vh(lambda v: v.tensor_tensor(pe3, pe3, t1b, ALU.mult))
        vh(lambda v: v.tensor_scalar(
            pe_tm[:], pe_tm[:], -1.0, 1.0, ALU.mult, ALU.add))
        vp = vh(lambda v: v.tensor_copy(pe_bf[:], pe_tm[:]))
        B_rt.written("V", vp)
        em.wait("tensor", "V", vp)
        se2 = None
        for tch in range(16):
            pk(2, 1).wait_reads("tensor")
            mtb = mm(pb(2, 1)[0:8, 0:128], pe_bf[:, tch * 8:(tch + 1) * 8],
                     ident_bf[:, 0:128], True, True, inc="M")
            pk(2, 1).written("M", mtb)
            B_rt.read_by("M", mtb)
            pk(2, 1).wait_write("scalar")
            if tch == 0:
                B_inegb.wait_reads("scalar")
            se2 = em.op("scalar", lambda s, tch=tch: s.copy(
                INEGB[0:8, tch * 128:(tch + 1) * 128], pb(2, 1)[0:8, 0:128]),
                inc="S")
            pk(2, 1).read_by("S", se2)
        B_inegb.written("S", se2)


    def emit_ffn(ffn_idx, emit_out_tail=False, max_groups=3, max_th=2,
                 skip_routing=False):
        if not skip_routing:
            emit_routing()
        eseq = [8, 0, 1, 2, 3, 4, 5, 6, 7]
        w1_stream = [(th, e) for th in range(max_th)
                     for e in eseq[:3 * max_groups]]

        def load_w1(idx):
            th, e = w1_stream[idx]
            slot = idx % 2
            B_w1[slot].wait_reads("gpsimd")
            src = (ws1_d if e == 8 else w1_d[e]).rearrange("(c p) f -> p c f", p=128)
            dst = W1S[slot].rearrange("p (c f) -> p c f", c=NCH)
            cw = em.op("gpsimd", lambda g, s=src, d=dst: g.dma_start(d, s),
                       inc="dW", incby=16)
            B_w1[slot].written("dW", cw)

        def load_w2_group(th, gi):
            for k in range(3):
                e = eseq[3 * gi + k]
                B_w2[k].wait_reads("gpsimd")
                src = (ws2_d if e == 8 else w2_d[e]).rearrange(
                    "(c p) f -> p c f", p=128)
                dst = W2S[k].rearrange("p (c f) -> p c f", c=4)
                cw = em.op("gpsimd", lambda g, s=src, d=dst: g.dma_start(d, s),
                           inc="dW", incby=16)
                B_w2[k].written("dW", cw)

        load_w1(0)
        load_w1(1)
        load_w2_group(0, 0)
        for th in range(max_th):
            for gi in range(max_groups):
                grp = eseq[3 * gi:3 * gi + 3]
                for k, e in enumerate(grp):
                    idx = th * 3 * max_groups + 3 * gi + k
                    w1slot = idx % 2
                    hs_ap = HS[k]
                    B_w1[w1slot].wait_write("tensor")
                    B_ub.wait_write("tensor")
                    use_mask = (e != 8 and not skip_routing)
                    vh = None
                    for f in range(4):
                        pB = pk2(f % 2)
                        pA = pb2(f % 2)
                        pB.wait_reads("tensor")
                        for dc in range(NCH):
                            lhs = W1S[w1slot][:, dc * 512 + f * 128:
                                              dc * 512 + (f + 1) * 128]
                            for nn in range(2):
                                lastw1 = (dc == NCH - 1 and not use_mask)
                                cmz = mm(pA[:, nn * 512:(nn + 1) * 512], lhs, UB[:, dc * T + th * 1024 + nn * 512:
                                         dc * T + th * 1024 + (nn + 1) * 512],
                                   dc == 0, lastw1,
                                   inc="M" if (lastw1 and nn == 1) else None)
                        if use_mask:
                            B_inegb.wait_write("tensor")
                            for nn in range(2):
                                cmz = mm(pA[:, nn * 512:(nn + 1) * 512],
                                         maskmat[0:8, e * 128:(e + 1) * 128],
                                         INEGB[0:8, th * 1024 + nn * 512:
                                               th * 1024 + (nn + 1) * 512],
                                         False, True, inc="M" if nn == 1 else None)
                        pB.written("M", cmz)
                        if use_mask:
                            B_inegb.read_by("M", cmz)
                        B_w1[w1slot].read_by("M", cmz)
                        B_ub.read_by("M", cmz)
                        pB.wait_write("scalar")
                        B_rb[f % 2].wait_reads("scalar")
                        sr = em.op("scalar", lambda s, pA=pA, f=f: s.activation(
                            RBUF[f % 2][:], pA, AF.Relu), inc="S")
                        pB.read_by("S", sr)
                        B_rb[f % 2].written("S", sr)
                        em.wait("vector", "S", sr)
                        B_hs[k].wait_reads("vector")
                        vh = em.op("vector", lambda v, hs=hs_ap, f=f: v.tensor_mul(
                            hs[:, f * 1024:(f + 1) * 1024],
                            RBUF[f % 2][:], RBUF[f % 2][:]), inc="V")
                        B_rb[f % 2].read_by("V", vh)
                    B_hs[k].written("V", vh)
                    if idx + 2 < len(w1_stream):
                        load_w1(idx + 2)
                for k in range(3):
                    B_w2[k].wait_write("tensor")
                    B_hs[k].wait_write("tensor")
                for j in range(NCH):
                    for nn in range(2):
                        pB = pk(2, (2 * j + nn) % 2)
                        pA = pb(2, (2 * j + nn) % 2)
                        pB.wait_reads("tensor")
                        cmb = None
                        for k in range(3):
                            for f in range(4):
                                lhs = W2S[k][:, f * 1024 + j * 128:
                                             f * 1024 + (j + 1) * 128]
                                rhs = HS[k][:, f * 1024 + nn * 512:
                                            f * 1024 + (nn + 1) * 512]
                                cmb = mm(pA, lhs, rhs, (k == 0 and f == 0),
                                         (k == 2 and f == 3),
                                         inc="M" if (k == 2 and f == 3) else None)
                        pB.written("M", cmb)
                        for k in range(3):
                            B_w2[k].read_by("M", cmb)
                            B_hs[k].read_by("M", cmb)
                        pB.wait_write("vector")
                        B_x[j].wait_write("vector")
                        x_ap = x_sb[:, j * T + th * 1024 + nn * 512:
                                    j * T + th * 1024 + (nn + 1) * 512]
                        vx = em.op("vector", lambda v, pA=pA, x_ap=x_ap:
                                   v.scalar_tensor_tensor(
                                       x_ap, pA, 0.5, x_ap, ALU.mult, ALU.add),
                                   inc="V")
                        pB.read_by("V", vx)
                        B_x[j].written("V", vx)
                        if emit_out_tail and th == 1 and gi == 2:
                            if nn == 1:
                                emit_out_chunk(j)
                if not (th == 1 and gi == 2):
                    ngi, nth = (gi + 1, th) if gi < 2 else (0, th + 1)
                    load_w2_group(nth, ngi)

    out_r = out_d.rearrange("(c p) t -> c p t", p=128)
    odma = []

    def emit_out_chunk(c):
        B_x[c].wait_write("sync")
        co = em.op("sync", lambda sy, c=c: sy.dma_start(
            out_r[c], x_sb[:, c * T:(c + 1) * T]), inc="dX", incby=16)
        odma.append(co)

    def finish_output():
        em.wait("sync", "dX", em.cnt["dX"])

    def emit_output_from_ub():
        B_ub.wait_write("gpsimd")
        for c in range(NCH):
            co = em.op("gpsimd", lambda g, c=c: g.dma_start(
                out_r[c], UB[:, c * T:(c + 1) * T]), inc="dG", incby=16)
        em.wait("gpsimd", "dG", em.cnt["dG"])

    def load_w(w_dram, wl_ap):
        B_wl.wait_reads("gpsimd")
        src = w_dram.rearrange("(c p) d -> p c d", p=128)
        dst = wl_ap.rearrange("p (c d) -> p c d", c=NCH)
        cw = em.op("gpsimd", lambda g, s=src, d=dst: g.dma_start(d, s),
                   inc="dW", incby=16)
        B_wl.written("dW", cw)

    def load_and_transpose(w_dram, wl_ap, wt_ap, skip_load=False):
        if not skip_load:
            load_w(w_dram, wl_ap)
        B_wl.wait_write("tensor")
        B_wt.wait_reads("tensor")
        B_wt.wait_reads("scalar")
        sq_ = None
        for dc in range(NCH):
            for icq in range(2):
                pB = pk(3, icq)
                pA = pb(3, icq)
                pB.wait_reads("tensor")
                mq = None
                for ici in range(4):
                    ic = icq * 4 + ici
                    lhs = wl_ap[:, ic * 1024 + dc * 128: ic * 1024 + (dc + 1) * 128]
                    mq = mm(pA[:, ici * 128:(ici + 1) * 128], lhs, ident_bf[:],
                            True, True, inc="M" if ici == 3 else None)
                pB.written("M", mq)
                B_wl.read_by("M", mq)
                pB.wait_write("vector")
                sq_ = em.op("vector", lambda v, pA=pA, dc=dc, icq=icq:
                            v.tensor_copy(
                    wt_ap[:, dc * 1024 + icq * 512: dc * 1024 + (icq + 1) * 512],
                    pA), inc="V")
                pB.read_by("V", sq_)
        B_wt.written("V", sq_)

    def proj_fm(wt_ap, dst_fn, bdst):
        B_wt.wait_write("tensor")
        B_ub.wait_write("tensor")
        bdst.wait_reads("tensor")
        bdst.wait_reads("scalar")
        last_s = None
        groups = [[(3, 0), (3, 1), (2, 0), (2, 1)], [(1, 0), (1, 1), (0, 0), (0, 1)]]
        for ic in range(NCH):
            tiles = groups[ic % 2]
            for (q, hq) in tiles:
                pk(q, hq).wait_reads("tensor")
            mq = None
            for dc in range(NCH):
                lhs = wt_ap[:, dc * 1024 + ic * 128: dc * 1024 + (ic + 1) * 128]
                for tt in range(4):
                    q, hq = tiles[tt]
                    rhs = UB[:, dc * T + tt * 512: dc * T + (tt + 1) * 512]
                    mq = mm(pb(q, hq), lhs, rhs, dc == 0, dc == NCH - 1,
                            inc="M" if (dc == NCH - 1 and tt == 3) else None)
            B_wt.read_by("M", mq)
            B_ub.read_by("M", mq)
            for tt in range(4):
                q, hq = tiles[tt]
                pk(q, hq).written("M", mq)
                pk(q, hq).wait_write("vector")
                last_s = em.op("vector", lambda v, q=q, hq=hq, ic=ic, tt=tt:
                               v.tensor_copy(
                    dst_fn(ic)[:, tt * 512:(tt + 1) * 512], pb(q, hq)), inc="V")
                pk(q, hq).read_by("V", last_s)
        bdst.written("V", last_s)

    def proj_tm(wt_ap, dst_ap, bdst):
        B_wt.wait_write("tensor")
        B_ub.wait_write("tensor")
        bdst.wait_reads("tensor")
        bdst.wait_reads("scalar")
        last_s = None
        for tch in range(16):
            q = 3 if tch % 2 == 0 else 2
            pB = pk2(q)
            pA = pb2(q)
            pB.wait_reads("tensor")
            mq = None
            for dc in range(NCH):
                lhs = UB[:, dc * T + tch * 128: dc * T + (tch + 1) * 128]
                for ih in range(2):
                    rhs = wt_ap[:, dc * 1024 + ih * 512: dc * 1024 + (ih + 1) * 512]
                    mq = mm(pA[:, ih * 512:(ih + 1) * 512], lhs, rhs,
                            dc == 0, dc == NCH - 1,
                            inc="M" if (dc == NCH - 1 and ih == 1) else None)
            pB.written("M", mq)
            B_wt.read_by("M", mq)
            B_ub.read_by("M", mq)
            pB.wait_write("vector")
            last_s = em.op("vector", lambda v, pA=pA, tch=tch: v.tensor_copy(
                dst_ap[:, tch * 1024:(tch + 1) * 1024], pA), inc="V")
            pB.read_by("V", last_s)
        bdst.written("V", last_s)

    def emit_norms(src_fn, bsrc, dst_sl, bdst):
        psn = pb(2, 0)[:, 0:256]
        pk(2, 0).wait_reads("tensor")
        bsrc.wait_write("vector")
        mq = None
        for c in range(NCH):
            for hb in range(4):
                B_nsq.wait_reads("vector")
                vn = em.op("vector", lambda v, c=c, hb=hb: v.tensor_mul(
                    A_NSQ[:], src_fn(c)[:, hb * 512:(hb + 1) * 512],
                    src_fn(c)[:, hb * 512:(hb + 1) * 512]), inc="V")
                B_nsq.written("V", vn)
                em.wait("tensor", "V", vn)
                for tq in range(4):
                    tch = hb * 4 + tq
                    mq = mm(psn[:, tch * 16 + 2 * c: tch * 16 + 2 * c + 2],
                            A_NSQ[:, tq * 128:(tq + 1) * 128], pairsel[:],
                            True, True,
                            inc="M" if tq == 3 else None)
                B_nsq.read_by("M", mq)
        pk(2, 0).written("M", mq)
        pk(2, 0).wait_write("scalar")
        bdst.wait_reads("scalar")
        last = None
        for tch in range(16):
            last = em.op("scalar", lambda s, tch=tch: s.activation(
                dst_sl[:, tch * H:(tch + 1) * H], psn[:, tch * 16:tch * 16 + 16],
                AF.Exp, scale=f32(-0.5), bias=f32(-LNSM)), inc="S")
        pk(2, 0).read_by("S", last)
        bdst.written("S", last)

    def trig_pipeline(pja, pjb, width, sin_dst, cos_dst):
        if width > 512:
            sl = 0 if trig_slot[0] in (0, 3) else 2
            trig_slot[0] = (sl + 2) % NTRIG
        else:
            sl = trig_slot[0]
            trig_slot[0] = (sl + 1) % NTRIG
        ns, ds, isc = nscr[sl], dscr[sl], iscr[sl]
        Bt = B_trig[sl]
        Bt.wait_reads("vector")
        pjb.wait_write("vector")
        for c0 in range(0, width, 512):
            em.op("vector", lambda v, pja=pja, c0=c0, isc=isc: v.tensor_scalar(
                isc[:, 0:512], pja[:, c0:c0 + 512], 8.5, None, ALU.add))
            em.op("vector", lambda v, pja=pja, c0=c0, ds=ds, isc=isc:
                  v.scalar_tensor_tensor(
                ds[:, c0:c0 + 512], pja[:, c0:c0 + 512], 8.5, isc[:, 0:512],
                ALU.add, ALU.subtract), inc="V")
        vd = em.op("vector", lambda v, w=width, ns=ns, ds=ds:
                   v.scalar_tensor_tensor(
            ns[:, 0:w], ds[:, 0:w], 0.5, ds[:, 0:w], ALU.is_ge,
            ALU.subtract), inc="V")
        pjb.read_by("V", vd)
        Bt.written("V", vd)
        em.wait("scalar", "V", vd)
        em.op("scalar", lambda s, sd=sin_dst, w=width, ns=ns: s.activation(
            sd, ns[:, 0:w], AF.Sin, scale=f32(-SCALE_2PI)))
        em.op("scalar", lambda s, w=width, ns=ns, ds=ds: s.activation(
            ds[:, 0:w], ns[:, 0:w], AF.Abs))
        st = em.op("scalar", lambda s, cd=cos_dst, w=width, ds=ds: s.activation(
            cd, ds[:, 0:w], AF.Sin, scale=f32(-SCALE_2PI), bias=f32(PI_F / 2)),
            inc="S")
        Bt.read_by("S", st)
        return st

    def emit_attention(stop_after=None):
        B_hs[0].wait_reads("gpsimd")
        B_hs[1].wait_reads("gpsimd")
        load_w(wv_d, A_WL)
        emit_layernorm()
        barrier("attn-start")
        load_and_transpose(wv_d, A_WL, A_WT, skip_load=True)
        load_w(wk_d, A_WL)
        proj_tm(A_WT, A_V, B_v)
        if stop_after == "v":
            return
        barrier("pre-k")
        load_and_transpose(wk_d, A_WL, A_WT, skip_load=True)
        proj_fm(A_WT, A_K, B_k)
        if stop_after == "k":
            return
        emit_norms(A_K, B_k, sk_tm, B_sk)
        if stop_after == "norms":
            return
        B_sk.wait_write("vector")
        B_skb.wait_reads("vector")
        vsk = em.op("vector", lambda v: v.tensor_copy(skb_tm[:], sk_tm[:]), inc="V")
        B_sk.read_by("V", vsk)
        B_skb.written("V", vsk)

        passes = [(0, 1), (2, 3), (4, 5), (6, 7)]
        pending_drain = [None, None]
        for pi, prs in enumerate(passes):
            npr = 2
            kvmap = [(0, 0), (0, 1), (1, 0), (1, 1)]
            kvb = [pk(*kvmap[k]) for k in range(4)]
            kva = [pb(*kvmap[k])[0:65, :] for k in range(4)]
            B_v.wait_write("tensor")
            B_skb.wait_write("vector")
            B_rfT.wait_write("tensor")
            h0 = 2 * prs[0]
            nh = 4
            mkv = [None] * 4

            def emit_vaug_proj_trig(tch):
                vsl = tch % 2
                B_vas[vsl].wait_reads("vector")
                B_v.wait_write("vector")
                vdst = A_VAS[vsl][:, 0:65 * nh]
                v3 = vdst.rearrange("p (h q) -> p h q", h=nh)[:, :, 0:64]
                s3 = skb_tm[:, tch * H + h0: tch * H + h0 + nh]\
                    .unsqueeze(2).to_broadcast([128, nh, 64])
                src3 = A_V[:, tch * 1024 + h0 * 64: tch * 1024 + (h0 + nh) * 64]\
                    .rearrange("p (h q) -> p h q", h=nh)
                em.op("vector", lambda v, a=v3, b=src3, c=s3:
                      v.tensor_tensor(a, b, c, ALU.mult))
                vcol = vdst.rearrange("p (h q) -> p h q", h=nh)[:, :, 64:65]
                vva = em.op("vector", lambda v, a=vcol, tch=tch:
                            v.tensor_copy(a, skb_tm[:, tch * H + h0:
                                                    tch * H + h0 + nh].unsqueeze(2)),
                            inc="V")
                B_vas[vsl].written("V", vva)
                em.wait("tensor", "V", vva)
                pjq = 3 if tch % 2 == 0 else 2
                pjb = pk2(pjq)
                pja = pb2(pjq)
                pjb.wait_reads("tensor")
                mpj = None
                for ip, hp in enumerate(prs):
                    lhsE = A_K(hp)[:, tch * 128:(tch + 1) * 128]
                    mpj = mm(pja[:, ip * 512:(ip + 1) * 512], lhsE, rfT[:, 0:512],
                             True, True, inc="M" if ip == 1 else None)
                B_k.read_by("M", mpj)
                pjb.written("M", mpj)
                kfsl = tch % 2
                kf = A_KF[kfsl]
                B_kf[kfsl].wait_reads("vector")
                B_kf[kfsl].wait_reads("scalar")
                st = None
                for ip in range(2):
                    kf2 = kf[:, ip * 1024:(ip + 1) * 1024]\
                        .rearrange("p (h q) -> p h q", h=2)
                    st = trig_pipeline(pb(pjq, ip), pk(pjq, ip), 512,
                                       kf2[:, :, 256:512], kf2[:, :, 0:256])
                B_kf[kfsl].written("S", st)

            def emit_kv(tch):
                vsl = tch % 2
                kfsl = tch % 2
                kf = A_KF[kfsl]
                B_kf[kfsl].wait_write("tensor")
                B_vas[vsl].wait_write("tensor")
                for k in range(4):
                    lhsV = A_VAS[vsl][:, k * 65: k * 65 + 65]
                    mkv[k] = mm(kva[k], lhsV, kf[:, k * 512:(k + 1) * 512],
                                tch == 0, tch == 15,
                                inc="M" if k == 3 else None)
                B_kf[kfsl].read_by("M", mkv[3])
                B_vas[vsl].read_by("M", mkv[3])

            drain_slots = []

            def drain_a(kvb=kvb, kva=kva, prs=prs, drain_slots=drain_slots):
                for ip2, hp2 in enumerate(prs):
                    for hh in range(2):
                        k = 2 * ip2 + hh
                        sl = trig_slot[0]
                        trig_slot[0] = (sl + 1) % NTRIG
                        nb = nscr[sl]
                        Bt = B_trig[sl]
                        kvb[k].wait_write("scalar")
                        Bt.wait_reads("scalar")
                        skv = em.op("scalar", lambda s, a=kva[k], nb=nb: s.copy(
                            nb[0:65, 0:512], a), inc="S")
                        kvb[k].read_by("S", skv)
                        Bt.written("S", skv)
                        drain_slots.append((sl, 2 * hp2 + hh))

            def drain_b(drain_slots=drain_slots):
                for ki, (sl, h) in enumerate(drain_slots):
                    nb = nscr[sl]
                    Bt = B_trig[sl]
                    psl = ki % 2
                    Bt.wait_write("tensor")
                    pk(3, psl).wait_reads("tensor")
                    mkt = None
                    for q in range(4):
                        mkt = mm(pb(3, psl)[:, q * 65:(q + 1) * 65],
                                 nb[0:65, q * 128:(q + 1) * 128],
                                 ident_bf[0:65, 0:65], True, True,
                                 inc="M" if q == 3 else None)
                    pk(3, psl).written("M", mkt)
                    Bt.read_by("M", mkt)
                    pk(3, psl).wait_write("scalar")
                    B_kv.wait_reads("scalar")
                    skv2 = em.op("scalar", lambda s, h=h, psl=psl: s.copy(
                        A_KV[:, h * 260:(h + 1) * 260], pb(3, psl)[:, 0:260]),
                        inc="S")
                    pk(3, psl).read_by("S", skv2)
                    B_kv.written("S", skv2)
                del drain_slots[:]

            if pending_drain[0] is not None:
                pending_drain[0]()
                pending_drain[1]()
            emit_vaug_proj_trig(0)
            emit_vaug_proj_trig(1)
            for b in kvb:
                b.wait_reads("tensor")
            for tch in range(16):
                if 1 <= tch < 15:
                    emit_vaug_proj_trig(tch + 1)
                emit_kv(tch)
            for k in range(4):
                kvb[k].written("M", em.cnt["M"])
            pending_drain = [drain_a, drain_b]
        pending_drain[0]()
        pending_drain[1]()

        if stop_after == "phaseA":
            return
        barrier("pre-q")
        load_and_transpose(wq_d, A_WL_Q, A_WT_Q)

        def qchunk(ic):
            return A_Q[:, ic * T:(ic + 1) * T]
        if stop_after == "q":
            proj_fm(A_WT_Q, qchunk, B_q)
            emit_norms(qchunk, B_q, sq_tm, B_sqt)
            return
        barrier("pre-sweep1")

        def qfu_ap(par, dc, lo, width):
            return A_QFU[:, par * 4096 + dc * 1024 + lo: par * 4096 + dc * 1024 + lo + width]
        B_qf2 = [Buf(em, "qf0"), Buf(em, "qf1")]
        steps = [(h, th2) for h in range(16) for th2 in range(2)]
        B_qc = [Buf(em, f"qc{ic}") for ic in range(NCH)]
        psn_q = pb(1, 1)[:, 0:256]

        def emit_qproj_tile(ic, tt):
            pB = pk(1, 0)
            pA = pb(1, 0)
            pB.wait_reads("tensor")
            B_wt.wait_write("tensor")
            B_ub.wait_write("tensor")
            mq = None
            for dc in range(NCH):
                lhs = A_WT_Q[:, dc * 1024 + ic * 128: dc * 1024 + (ic + 1) * 128]
                rhs = UB[:, dc * T + tt * 512: dc * T + (tt + 1) * 512]
                mq = mm(pA, lhs, rhs, dc == 0, dc == NCH - 1,
                        inc="M" if dc == NCH - 1 else None)
            pB.written("M", mq)
            B_wt.read_by("M", mq)
            B_ub.read_by("M", mq)
            pB.wait_write("scalar")
            sop = em.op("scalar", lambda s, ic=ic, tt=tt, pA=pA: s.copy(
                qchunk(ic)[:, tt * 512:(tt + 1) * 512], pA), inc="S")
            pB.read_by("S", sop)
            if tt == 3:
                B_qc[ic].written("S", sop)
                B_q.written("S", sop)

        def emit_qnorm_piece(c):
            B_qc[c].wait_write("vector")
            mq = None
            for hb in range(4):
                sl = trig_slot[0]
                trig_slot[0] = (sl + 1) % NTRIG
                nb = nscr[sl]
                Bt = B_trig[sl]
                Bt.wait_reads("vector")
                Bt.wait_reads("scalar")
                vn = em.op("vector", lambda v, c=c, hb=hb, nb=nb: v.tensor_mul(
                    nb[:, 0:512], qchunk(c)[:, hb * 512:(hb + 1) * 512],
                    qchunk(c)[:, hb * 512:(hb + 1) * 512]), inc="V")
                Bt.written("V", vn)
                B_qc[c].read_by("V", vn)
                em.wait("tensor", "V", vn)
                if c == 0 and hb == 0:
                    pk(1, 1).wait_reads("tensor")
                for tq in range(4):
                    tch = hb * 4 + tq
                    mq = mm(psn_q[:, tch * 16 + 2 * c: tch * 16 + 2 * c + 2],
                            nb[:, tq * 128:(tq + 1) * 128], pairsel[:],
                            True, True, inc="M" if tq == 3 else None)
                Bt.read_by("M", mq)
            if c == NCH - 1:
                pk(1, 1).written("M", mq)

        def emit_proj_trig(i):
            h, th2 = steps[i]
            hc, hr = h // 2, (h % 2) * 64
            par = th2
            pjq = 3 if i % 2 == 0 else 2
            B_qc[hc].wait_write("tensor")
            B_qf2[par].wait_reads("tensor")
            for mc in range(2):
                for nn in range(2):
                    tt = th2 * 2 + nn
                    pjb = pk(pjq, nn)
                    pja = pb(pjq, nn)
                    pjb.wait_reads("tensor")
                    lhs = rfT[:, (h % 2) * 256 + mc * 128:
                              (h % 2) * 256 + (mc + 1) * 128]
                    rhs = A_Q[:, hc * T + tt * 512: hc * T + (tt + 1) * 512]
                    mq = mm(pja, lhs, rhs, True, True, inc="M")
                    pjb.written("M", mq)
                    B_q.read_by("M", mq)
                    cosd = qfu_ap(par, mc, nn * 512, 512)
                    sind = qfu_ap(par, 2 + mc, nn * 512, 512)
                    st = trig_pipeline(pja, pjb, 512, sind, cosd)
                    B_qf2[par].written("S", st)

        def emit_qkv(i):
            h, th2 = steps[i]
            hc, hr = h // 2, (h % 2) * 64
            par = th2
            B_kv.wait_write("tensor")
            B_qf2[par].wait_write("tensor")
            qb0, qb1 = pk(0, 0), pk(0, 1)
            qb0.wait_reads("tensor")
            qb1.wait_reads("tensor")
            mq = None
            for tti in range(2):
                qa = pb(0, tti)[0:65, :]
                for dc in range(4):
                    lhs = A_KV[:, h * 260 + dc * 65: h * 260 + (dc + 1) * 65]
                    rhs = qfu_ap(par, dc, tti * 512, 512)
                    mq = mm(qa, lhs, rhs, dc == 0, dc == 3,
                            inc="M" if (dc == 3 and tti == 1) else None)
            qb0.written("M", mq)
            qb1.written("M", mq)
            B_kv.read_by("M", mq)
            B_qf2[par].read_by("M", mq)
            qb0.wait_write("vector")
            qb0.wait_write("scalar")
            if th2 == 0:
                B_zst.wait_reads("scalar")
                B_q.wait_reads("vector")
            dst = A_Q[:, hc * T + th2 * 1024:
                      hc * T + (th2 + 1) * 1024][hr:hr + 64, :]
            szv = em.op("vector", lambda v, d=dst: v.tensor_copy(
                d, PB[0][0:64, 0:1024]), inc="V")
            sz = em.op("scalar", lambda s, th2=th2: s.copy(
                A_ZST[64:65, th2 * 1024:(th2 + 1) * 1024],
                PB[0][64:65, 0:1024]), inc="S")
            qb0.read_by("V", szv)
            qb0.read_by("S", sz)
            qb1.read_by("V", szv)
            qb1.read_by("S", sz)
            if th2 == 1:
                B_zst.written("S", sz)
                B_q.written("V", szv)
                B_zfm.wait_reads("gpsimd")
                B_zst.wait_write("gpsimd")
                cz = em.op("gpsimd", lambda g, h=h: g.dma_start(
                    A_ZFM[h:h + 1, :], A_ZST[64:65, :]), inc="dG", incby=16)
                B_zst.read_by("dG", cz)
                if h == 15:
                    B_zfm.written("dG", cz)

        for ic in range(2):
            for tt in range(4):
                emit_qproj_tile(ic, tt)
            emit_qnorm_piece(ic)
        emit_proj_trig(0)
        for i in range(32):
            hc = i // 4
            if hc + 2 < NCH:
                emit_qproj_tile(hc + 2, i % 4)
                if i % 4 == 3:
                    emit_qnorm_piece(hc + 2)
            if i < 31:
                emit_proj_trig(i + 1)
            emit_qkv(i)
        pk(1, 1).wait_write("scalar")
        B_sqt.wait_reads("scalar")
        lastn = None
        for tch in range(16):
            lastn = em.op("scalar", lambda s, tch=tch: s.activation(
                sq_tm[:, tch * H:(tch + 1) * H],
                psn_q[:, tch * 16:tch * 16 + 16],
                AF.Exp, scale=f32(-0.5), bias=f32(-LNSM)), inc="S")
        pk(1, 1).read_by("S", lastn)
        B_sqt.written("S", lastn)

        if stop_after == "sweep1":
            return
        B_qfu.wait_reads("gpsimd")
        B_wl.wait_reads("gpsimd")
        src_wo = wo_d.rearrange("(c p) d -> p c d", p=128)
        dst_wo = A_WO.rearrange("p (c d) -> p c d", c=NCH)
        cwo = em.op("gpsimd", lambda g: g.dma_start(dst_wo, src_wo),
                    inc="dW", incby=16)
        B_wl.written("dW", cwo)
        B_wl.wait_write("tensor")
        B_wt.wait_reads("tensor")
        B_wt.wait_reads("scalar")
        wo_sq = [None]

        def emit_wo_piece(i):
            dc, icq = i // 2, i % 2
            pB = pk(1, icq)
            pA = pb(1, icq)
            pB.wait_reads("tensor")
            mq = None
            for ici in range(4):
                ic = icq * 4 + ici
                lhs = A_WO[:, ic * 1024 + dc * 128: ic * 1024 + (dc + 1) * 128]
                mq = mm(pA[:, ici * 128:(ici + 1) * 128], lhs, ident_bf[:],
                        True, True, inc="M" if ici == 3 else None)
            pB.written("M", mq)
            B_wl.read_by("M", mq)
            pB.wait_write("scalar")
            wo_sq[0] = em.op("scalar", lambda s, pA=pA, dc=dc, icq=icq: s.copy(
                A_WOT[:, dc * 1024 + icq * 512: dc * 1024 + (icq + 1) * 512],
                pA), inc="S")
            pB.read_by("S", wo_sq[0])

        B_zfm.wait_write("tensor")
        B_sqt.wait_write("vector")
        vw = None
        for tch in range(16):
            emit_wo_piece(tch)
            pB = pk(2, tch % 2)
            pA = pb(2, tch % 2)
            pB.wait_reads("tensor")
            mz = mm(pA[:, 0:16], A_ZFM[0:16, tch * 128:(tch + 1) * 128],
                    ident_bf[0:16, 0:16], True, True, inc="M")
            pB.written("M", mz)
            pB.wait_write("vector")
            B_wtm.wait_reads("vector")
            sqsl = sq_tm[:, tch * H:(tch + 1) * H]
            vh(lambda v, pA=pA, sq=sqsl: v.tensor_tensor(
                wsc_f[:, 0:16], sq, pA[:, 0:16], ALU.mult))
            vh(lambda v: v.tensor_scalar(
                wsc_f[:, 0:16], wsc_f[:, 0:16], 1e-6, None, ALU.max))
            vh(lambda v: v.reciprocal(wsc_f[:, 16:32], wsc_f[:, 0:16]))
            vw = vh(lambda v, sq=sqsl, tch=tch: v.tensor_tensor(
                w_tm[:, tch * H:(tch + 1) * H], sq, wsc_f[:, 16:32], ALU.mult))
            pB.read_by("V", vw)
        B_wt.written("S", wo_sq[0])
        B_wtm.written("V", vw)
        B_wtm.wait_write("tensor")
        vap = None
        for h in range(16):
            hc, hr = h // 2, (h % 2) * 64
            for tt in range(4):
                pB = pk(3, tt % 2)
                pA = pb(3, tt % 2)
                pB.wait_reads("tensor")
                mwb = None
                for tq in range(4):
                    tch = tt * 4 + tq
                    wcol = w_tm[:, tch * H + h: tch * H + h + 1]\
                        .to_broadcast([128, 64])
                    mwb = mm(pA[0:64, tq * 128:(tq + 1) * 128], wcol, ident_bf[:],
                             True, True, inc="M" if tq == 3 else None)
                pB.written("M", mwb)
                pB.wait_write("vector")
                ap_ = A_Q[:, hc * T + tt * 512:
                          hc * T + (tt + 1) * 512][hr:hr + 64, :]
                vap = em.op("vector", lambda v, a=ap_, pA=pA: v.tensor_tensor(
                    a, a, pA[0:64, :], ALU.mult), inc="V")
                pB.read_by("V", vap)
        B_q.written("V", vap)

        if stop_after == "sweep2":
            return
        B_wt.wait_write("tensor")
        B_q.wait_write("tensor")
        for jc in range(NCH):
            for tt in range(4):
                pB = pk(1, tt % 2)
                pA = pb(1, tt % 2)
                pB.wait_reads("tensor")
                mq = None
                for ic in range(NCH):
                    lhs = A_WOT[:, ic * 1024 + jc * 128: ic * 1024 + (jc + 1) * 128]
                    rhs = A_Q[:, ic * T + tt * 512: ic * T + (tt + 1) * 512]
                    mq = mm(pA, lhs, rhs, ic == 0, ic == NCH - 1,
                            inc="M" if ic == NCH - 1 else None)
                pB.written("M", mq)
                B_q.read_by("M", mq)
                B_wt.read_by("M", mq)
                pB.wait_write("vector")
                B_x[jc].wait_write("vector")
                x_ap = x_sb[:, jc * T + tt * 512: jc * T + (tt + 1) * 512]
                vx = em.op("vector", lambda v, x_ap=x_ap, pA=pA: v.tensor_add(
                    x_ap, x_ap, pA), inc="V")
                pB.read_by("V", vx)
                B_x[jc].written("V", vx)


    if mode == "ln1":
        emit_layernorm()
        emit_output_from_ub()
    elif mode == "ffn1":
        emit_layernorm()
        emit_ffn(0)
        barrier("out")
        for c in range(NCH):
            emit_out_chunk(c)
        finish_output()
    elif mode == "rt":
        emit_layernorm()
        emit_routing()
        barrier("out")
        for c in range(NCH):
            emit_out_chunk(c)
        finish_output()
    elif mode == "dbg_ineg":
        emit_layernorm()
        emit_routing()
        barrier("out")
        em.op("gpsimd", lambda g: g.dma_start(out_r[0][0:8, :], INEGB[0:8, :]),
              inc="dG", incby=16)
        em.op("gpsimd", lambda g: g.dma_start(out_r[1][0:8, :], LOGB[0:8, :]),
              inc="dG", incby=16)
        em.op("gpsimd", lambda g: g.dma_start(out_r[2][:, 0:128], lt_sb[:]),
              inc="dG", incby=16)
        em.op("gpsimd", lambda g: g.dma_start(out_r[3][:, 0:128], pe_bf[:]),
              inc="dG", incby=16)
        em.op("gpsimd", lambda g: g.dma_start(out_r[4][0:8, 0:1024], maskmat[:]),
              inc="dG", incby=16)
        em.wait("gpsimd", "dG", em.cnt["dG"])
    elif mode == "ffn_sh":
        emit_layernorm()
        emit_ffn(0, max_groups=1, max_th=2, skip_routing=True)
        barrier("out")
        for c in range(NCH):
            emit_out_chunk(c)
        finish_output()
    elif mode == "ffn_g1":
        emit_layernorm()
        emit_ffn(0, max_groups=1, max_th=2)
        barrier("out")
        for c in range(NCH):
            emit_out_chunk(c)
        finish_output()
    elif mode.startswith("attn_"):
        emit_layernorm()
        emit_ffn(0)
        emit_attention(stop_after=mode[5:])
        barrier("out")
        for c in range(NCH):
            emit_out_chunk(c)
        finish_output()
    elif mode == "attn":
        emit_layernorm()
        emit_ffn(0)
        emit_attention()
        barrier("out")
        for c in range(NCH):
            emit_out_chunk(c)
        finish_output()
    else:
        emit_layernorm()
        emit_ffn(0)
        emit_attention()
        emit_layernorm(home=LN_HOME_B)
        barrier("pre-ffn2")
        emit_ffn(1, emit_out_tail=True)
        finish_output()

    with nc.Block() as block:
        @block.tensor
        def _(t):
            for fn in em.ops["tensor"]:
                fn(t)

        @block.vector
        def _(v):
            for fn in em.ops["vector"]:
                fn(v)

        @block.scalar
        def _(s):
            for fn in em.ops["scalar"]:
                fn(s)

        @block.gpsimd
        def _(g):
            for fn in em.ops["gpsimd"]:
                fn(g)

        @block.sync
        def _(sy):
            for fn in em.ops["sync"]:
                fn(sy)

    ctx.close()
    return nc


_NC_CACHE = {}


def get_nc(mode="full"):
    if mode not in _NC_CACHE:
        _NC_CACHE[mode] = build_program(mode)
    return _NC_CACHE[mode]


def make_in_maps(L0_attn_x, L0_attn_wq, L0_attn_wk, L0_attn_wv, L0_attn_wo,
                 L0_attn_rf, L0_ffn_Ws1, L0_ffn_Ws2, L0_ffn_W1, L0_ffn_W2,
                 L0_ffn_Wr):
    cst = np.zeros((128, 1152), dtype=np.float32)
    cst[:, 0:128] = np.eye(128, dtype=np.float32)
    for e in range(8):
        cst[e, 128 + e * 128:128 + (e + 1) * 128] = -BIG
    shared = {
        "wq": np.ascontiguousarray(L0_attn_wq, dtype=np.float32),
        "wk": np.ascontiguousarray(L0_attn_wk, dtype=np.float32),
        "wv": np.ascontiguousarray(L0_attn_wv, dtype=np.float32),
        "wo": np.ascontiguousarray(L0_attn_wo, dtype=np.float32),
        "rf": np.ascontiguousarray(L0_attn_rf, dtype=np.float32),
        "ws1": np.ascontiguousarray(L0_ffn_Ws1, dtype=np.float32),
        "ws2": np.ascontiguousarray(L0_ffn_Ws2, dtype=np.float32),
        "w1": np.ascontiguousarray(L0_ffn_W1, dtype=np.float32),
        "w2": np.ascontiguousarray(L0_ffn_W2, dtype=np.float32),
        "wr": np.ascontiguousarray(L0_ffn_Wr, dtype=np.float32),
        "cst": cst,
    }
    in_maps = []
    for b in range(NCORES):
        m = dict(shared)
        m["x"] = np.ascontiguousarray(L0_attn_x[b].T, dtype=np.float32)
        in_maps.append(m)
    return in_maps


def kernel(**inputs):
    nc = get_nc("full")
    in_maps = make_in_maps(**inputs)
    res = run_bass_kernel_spmd(nc, in_maps, core_ids=list(range(NCORES)))
    out = np.stack([np.ascontiguousarray(res.results[b]["out"].T)
                    for b in range(NCORES)], axis=0)
    return out


# version 35 (final)
# speedup vs baseline: 1.2248x; 1.0064x over previous
import math
from contextlib import ExitStack

import numpy as np
import concourse.bass as bass
import concourse.mybir as mybir
from concourse.bass_utils import run_bass_kernel_spmd

F32 = mybir.dt.float32
BF16 = mybir.dt.bfloat16
AF = mybir.ActivationFunctionType
ALU = mybir.AluOpType
AX = mybir.AxisListType

D, T, NCH = 1024, 2048, 8
H, DH, MRF = 16, 64, 256
NE, FD = 8, 512
NCORES = 8

PI_F = float(np.float32(np.pi))
TWO_PI = float(np.float32(2 * np.pi))
INV2PI = float(np.float32(1.0 / (2 * np.pi)))
MAGIC = float(np.float32(12582912.0))
SCALE_2PI = float(np.float32(6.2831783))
BIG = 1.0e20
LNSM = float(np.float32(math.log(math.sqrt(MRF))))
EPS = 1.0e-5
f32 = np.float32

SEM_OWNER = {"M": "tensor", "V": "vector", "S": "scalar", "G": "gpsimd"}


class Emitter:
    ENGINES = ("tensor", "vector", "scalar", "gpsimd", "sync")

    def __init__(self, nc, ctx):
        self.nc, self.ctx = nc, ctx
        self.ops = {e: [] for e in self.ENGINES}
        self.sems, self.cnt, self.lastwait = {}, {}, {}

    def sem(self, name):
        self.sems[name] = self.ctx.enter_context(self.nc.semaphore(name))
        self.cnt[name] = 0

    def wait(self, eng, semname, val, force=False):
        if val is None or val <= 0 or (SEM_OWNER.get(semname) == eng and not force):
            return
        key = (eng, semname)
        if self.lastwait.get(key, -1) >= val:
            return
        self.lastwait[key] = val
        h = self.sems[semname]
        self.ops[eng].append(lambda en, h=h, v=val: en.wait_ge(h, v))

    def op(self, eng, fn, inc=None, incby=1):
        if inc is None:
            self.ops[eng].append(fn)
            return None
        h = self.sems[inc]
        self.cnt[inc] += incby
        v = self.cnt[inc]

        def run(en, fn=fn, h=h, incby=incby):
            fn(en).then_inc(h, incby)
        self.ops[eng].append(run)
        return v


class Buf:
    def __init__(self, em, name=""):
        self.em, self.name = em, name
        self.w = None
        self.r = []

    def wait_write(self, eng):
        if self.w:
            self.em.wait(eng, *self.w)

    def read_by(self, sem, cnt):
        if cnt is not None:
            self.r.append((sem, cnt))

    def wait_reads(self, eng):
        for (s, c) in self.r:
            self.em.wait(eng, s, c)
        if self.w:
            self.em.wait(eng, *self.w)

    def written(self, sem, cnt):
        if cnt is not None:
            self.w = (sem, cnt)
            self.r = []


def build_program(mode="full"):
    nc = bass.Bass("TRN2", target_bir_lowering=False, detect_race_conditions=False)

    x_in = nc.dram_tensor("x", [D, T], F32, kind="ExternalInput").ap()
    wq_d = nc.dram_tensor("wq", [D, D], F32, kind="ExternalInput").ap()
    wk_d = nc.dram_tensor("wk", [D, D], F32, kind="ExternalInput").ap()
    wv_d = nc.dram_tensor("wv", [D, D], F32, kind="ExternalInput").ap()
    wo_d = nc.dram_tensor("wo", [D, D], F32, kind="ExternalInput").ap()
    rf_d = nc.dram_tensor("rf", [MRF, DH], F32, kind="ExternalInput").ap()
    ws1_d = nc.dram_tensor("ws1", [D, FD], F32, kind="ExternalInput").ap()
    ws2_d = nc.dram_tensor("ws2", [FD, D], F32, kind="ExternalInput").ap()
    w1_d = nc.dram_tensor("w1", [NE, D, FD], F32, kind="ExternalInput").ap()
    w2_d = nc.dram_tensor("w2", [NE, FD, D], F32, kind="ExternalInput").ap()
    wr_d = nc.dram_tensor("wr", [D, NE], F32, kind="ExternalInput").ap()
    cst_d = nc.dram_tensor("cst", [128, 1152], F32, kind="ExternalInput").ap()
    out_d = nc.dram_tensor("out", [D, T], F32, kind="ExternalOutput").ap()

    ctx = ExitStack()
    SB = lambda name, shape, dt: ctx.enter_context(nc.sbuf_tensor(name, shape, dt))
    PSA = lambda name, shape: ctx.enter_context(nc.psum_tensor(name, shape, F32))

    em = Emitter(nc, ctx)
    for s in ("dW", "dX", "dG", "M", "V", "S", "G"):
        em.sem(s)

    x_sb = SB("x_sb", [128, NCH * T], F32)
    ident_bf = SB("ident_bf", [128, 128], BF16)
    rfT = SB("rfT", [128, 2 * MRF], BF16)
    rf_raw = SB("rf_raw", [128, 2 * DH], BF16)
    ones_row = SB("ones_row", [1, 512], BF16)
    ones_col = SB("ones_col", [128, 1], BF16)
    e0_col = SB("e0_col", [128, 2], BF16)
    e1_col = SB("e1_col", [128, 2], BF16)
    pairsel = SB("pairsel", [128, 2], BF16)
    maskmat = SB("maskmat", [8, 8 * 128], BF16)
    iota8 = SB("iota8", [128, 8], F32)
    wrb = SB("wrb", [128, NCH * 8], BF16)
    m_sl = SB("m_sl", [1, 512], F32)
    t_sl = SB("t_sl", [1, 512], F32)
    var_sl = SB("var_sl", [1, 512], F32)
    a_sl = SB("a_sl", [1, 512], F32)
    abf_sl = SB("abf_sl", [1, 512], BF16)
    bbf_sl = SB("bbf_sl", [1, 512], BF16)
    lt_sb = SB("lt_sb", [128, 128], F32)
    rs_sb = SB("rs_sb", [128, 128], F32)
    max1 = SB("max1", [128, 16], F32)
    idm = SB("idm", [128, 16], F32)
    max2 = SB("max2", [128, 16], F32)
    t1_sb = SB("t1_sb", [128, 16], F32)
    pe_tm = SB("pe_tm", [128, 128], F32)
    pe_bf = SB("pe_bf", [128, 128], BF16)
    sk_tm = SB("sk_tm", [128, 16 * H], F32)
    sq_tm = SB("sq_tm", [128, 16 * H], F32)
    skb_tm = SB("skb_tm", [128, 16 * H], BF16)
    w_tm = SB("w_tm", [128, 16 * H], BF16)
    wsc_f = SB("wsc_f", [128, 32], F32)
    NTRIG = 4
    nscr_t = SB("nscr", [128, 2048], BF16)
    dscr_t = SB("dscr", [128, 2048], BF16)
    nscr = [nscr_t[:, i * 512: min(i * 512 + 1024, 2048)] for i in range(NTRIG)]
    dscr = [dscr_t[:, i * 512: min(i * 512 + 1024, 2048)] for i in range(NTRIG)]
    iscr_s = SB("iscr", [128, 512], mybir.dt.int32)
    iscr = [iscr_s] * NTRIG
    vas_sb = [SB(f"vas{i}", [128, 416], BF16) for i in range(2)]

    ACOLS = 57472
    arena = SB("arena", [128, ACOLS], BF16)

    def ar(c0, cols):
        return arena[:, c0:c0 + cols]

    UB = ar(0, NCH * T)
    LN_HOME_A = ((ar(51200, 2048), ar(53248, 2048)),
                 [ar(55296, 512), ar(55808, 512)],
                 [ar(56320, 512), ar(56832, 512)])
    LN_HOME_B = ((ar(24576, 2048), ar(28736, 2048)),
                 [ar(26624, 512), ar(27136, 512)],
                 [ar(27648, 512), ar(28160, 512)])
    HS = [ar(16384 + i * 4096, 4096) for i in range(3)]
    W1S = [ar(28672 + i * 4096, 4096) for i in range(2)]
    W2S = [ar(36864 + i * 4096, 4096) for i in range(3)]
    RBUF = [ar(49152 + i * 1024, 1024) for i in range(2)]
    LOGB = ar(51200, T)
    INEGB = ar(53248, T)
    A_WL = ar(16384, 8192)
    A_WT = ar(24576, 8192)
    A_V = ar(32832, 16384)

    def A_K(c):
        if c < 4:
            return ar(16384 + c * T, T)
        return ar(49216 + (c - 4) * T, T)
    A_KV = ar(24576, H * 4 * 65)
    A_KF = [ar(28736, 2048), ar(30784, 2048)]
    A_NSQ = ar(28736, 512)
    A_VAS = vas_sb
    A_ZST = ar(28736, 2048)
    A_ZFM = ar(30784, 2048)
    A_Q = ar(32832, 16384)
    A_WL_Q = ar(16384, 8192)
    A_WT_Q = ar(49216, 8192)
    A_QFU = ar(16384, 8192)
    A_WO = ar(16384, 8192)
    A_WOT = ar(49216, 8192)

    PB = [PSA(f"pb{i}", [128, 1024]) for i in range(4)]
    pbuf = [Buf(em, f"pb{i // 2}.{i % 2}") for i in range(8)]

    def pb(i, half, cols=512):
        return PB[i][:, half * 512: half * 512 + cols]

    def pk(i, half):
        return pbuf[2 * i + half]

    class Two:
        def __init__(self, i):
            self.a, self.b = pbuf[2 * i], pbuf[2 * i + 1]

        def wait_write(self, eng):
            self.a.wait_write(eng); self.b.wait_write(eng)

        def read_by(self, sem, cnt):
            self.a.read_by(sem, cnt); self.b.read_by(sem, cnt)

        def wait_reads(self, eng):
            self.a.wait_reads(eng); self.b.wait_reads(eng)

        def written(self, sem, cnt):
            self.a.written(sem, cnt); self.b.written(sem, cnt)

    def pk2(i):
        return Two(i)

    def pb2(i):
        return PB[i][:, :]

    B_ub = Buf(em, "ub")
    B_x = [Buf(em, f"x{c}") for c in range(NCH)]
    B_sq = [Buf(em, "sq0"), Buf(em, "sq1")]
    B_ab = [Buf(em, "ab0"), Buf(em, "ab1")]
    B_lnsl = Buf(em, "lnsl")
    B_hs = [Buf(em, f"hs{i}") for i in range(3)]
    B_w1 = [Buf(em, "w1a"), Buf(em, "w1b")]
    B_w2 = [Buf(em, f"w2{i}") for i in range(3)]
    B_rb = [Buf(em, "rb0"), Buf(em, "rb1")]
    B_logb = Buf(em, "logb")
    B_inegb = Buf(em, "inegb")
    B_wrb = Buf(em, "wrb")
    B_rfT = Buf(em, "rfT")
    B_wl = Buf(em, "wl")
    B_wt = Buf(em, "wt")
    B_v = Buf(em, "v")
    B_k = Buf(em, "k")
    B_kv = Buf(em, "kv")
    B_kf = [Buf(em, "kf0"), Buf(em, "kf1")]
    B_nsq = Buf(em, "nsq")
    B_vas = [Buf(em, "vas0"), Buf(em, "vas1")]
    B_q = Buf(em, "q")
    B_qfu = Buf(em, "qfu")
    B_zst = None
    B_zfm = Buf(em, "zfm")
    B_sk = Buf(em, "sk")
    B_skb = Buf(em, "skb")
    B_sqt = Buf(em, "sqt")
    B_wtm = Buf(em, "wtm")
    B_trig = [Buf(em, f"trig{i}") for i in range(4)]
    trig_slot = [0]
    B_rt = Buf(em, "routing")
    B_zst = Buf(em, "zst")

    def barrier(tag=""):
        for eng in Emitter.ENGINES:
            for sname in ("dW", "dX", "dG", "M", "V", "S"):
                em.wait(eng, sname, em.cnt[sname])

    def vh(fn):
        c = em.op("vector", fn, inc="V")
        em.wait("vector", "V", c, force=True)
        return c

    def mm(out_ap, lhsT_ap, rhs_ap, start, stop, inc=None, tp=None):
        def fn(en, o=out_ap, l=lhsT_ap, r=rhs_ap, st=start, sp=stop, tp=tp):
            return en.matmul(o, l, r, start=st, stop=sp, tile_position=tp)
        return em.op("tensor", fn, inc=inc)

    def scopy(dst_ap, src_ap, inc="S", scale=None, func=AF.Copy, bias=None):
        kw = {}
        if scale is not None:
            kw["scale"] = f32(scale)
        if bias is not None:
            kw["bias"] = f32(bias)

        def fn(en, d=dst_ap, s=src_ap, func=func, kw=kw):
            return en.activation(d, s, func, **kw)
        return em.op("scalar", fn, inc=inc)

    em.op("gpsimd", lambda g: g.dma_start(ident_bf[:], cst_d[:, 0:128]), inc="dG", incby=16)
    em.op("gpsimd", lambda g: g.dma_start(maskmat[:], cst_d[0:8, 128:1152]), inc="dG", incby=16)
    rf_r = rf_d.rearrange("(c p) d -> c p d", p=128)
    em.op("gpsimd", lambda g: g.dma_start(rf_raw[:, 0:DH], rf_r[0]), inc="dG", incby=16)
    c_rf = em.op("gpsimd", lambda g: g.dma_start(rf_raw[:, DH:2 * DH], rf_r[1]),
                 inc="dG", incby=16)

    x_r = x_in.rearrange("(c p) t -> p c t", p=128)
    for hf in range(2):
        src = x_r[:, 4 * hf:4 * hf + 4, :]
        dst = x_sb[:, hf * 4 * T:(hf + 1) * 4 * T].rearrange("p (c t) -> p c t", c=4)
        cx = em.op("sync", lambda sy, d=dst, s=src: sy.dma_start(d, s),
                   inc="dX", incby=16)
        for c in range(4 * hf, 4 * hf + 4):
            B_x[c].written("dX", cx)

    for fn in (
        lambda v: v.memset(ones_row[:], 1.0),
        lambda v: v.memset(ones_col[:], 1.0),
        lambda v: v.memset(e0_col[:, 0:1], 1.0),
        lambda v: v.memset(e0_col[:, 1:2], 0.0),
        lambda v: v.memset(e1_col[:, 0:1], 0.0),
        lambda v: v.memset(e1_col[:, 1:2], 1.0),
        lambda v: v.memset(pairsel[:], 0.0),
        lambda v: v.memset(pairsel[0:64, 0:1], 1.0),
        lambda v: v.memset(pairsel[64:128, 1:2], 1.0),
    ):
        em.op("vector", fn)
    c_consts = None
    for e in range(8):
        c_consts = em.op("vector", lambda v, e=e: v.memset(iota8[:, e:e + 1], float(e)),
                         inc="V")

    vz = em.op("vector", lambda v: v.memset(rfT[:], 0.0), inc="V")
    em.wait("tensor", "dG", c_rf)
    pk2(0).wait_reads("tensor")
    c_t = None
    for mc in range(2):
        mm(PB[0][0:64, mc * 128:(mc + 1) * 128],
           rf_raw[:, mc * DH:(mc + 1) * DH], ident_bf[:], True, True)
        c_t = mm(PB[0][64:128, mc * 128:(mc + 1) * 128],
                 rf_raw[:, mc * DH:(mc + 1) * DH], ident_bf[:], True, True, inc="M")
    pk2(0).written("M", c_t)
    pk2(0).wait_write("scalar")
    em.wait("scalar", "V", vz)
    scopy(rfT[0:64, 0:256], PB[0][0:64, 0:256], scale=INV2PI)
    c_rfT = scopy(rfT[64:128, 256:512], PB[0][64:128, 0:256], scale=INV2PI)
    pk2(0).read_by("S", c_rfT)
    B_rfT.written("S", c_rfT)

    def emit_layernorm(home=None):
        sqs, abuf, bbuf = home if home else LN_HOME_A
        sq_scr = [sqs[0], sqs[1]]
        stA = [pb(0, 0), pb(0, 1), pb(1, 0), pb(1, 1)]
        stB = [pb(2, 0), pb(2, 1), pb(3, 0), pb(3, 1)]
        kA = [pk(0, 0), pk(0, 1), pk(1, 0), pk(1, 1)]
        kB = [pk(2, 0), pk(2, 1), pk(3, 0), pk(3, 1)]
        for b in kA + kB:
            b.wait_reads("tensor")
        for c in range(NCH):
            B_x[c].wait_write("vector")
            B_sq[c % 2].wait_reads("vector")
            if c < 2:
                B_ub.wait_reads("scalar")
                B_ub.wait_reads("vector")
                B_logb.wait_reads("vector")
                B_kv.wait_reads("vector")
            xb_ap = UB[:, c * T:(c + 1) * T]
            em.op("vector", lambda v, a=xb_ap, c=c: v.tensor_copy(
                a, x_sb[:, c * T:(c + 1) * T]))
            vs = em.op("vector", lambda v, a=xb_ap, c=c: v.tensor_mul(
                sq_scr[c % 2][:], a, a), inc="V")
            em.wait("tensor", "V", vs)
            last = None
            for tt in range(4):
                mm(stA[tt][0:1, :], ones_col[:, 0:1],
                   xb_ap[:, tt * 512:(tt + 1) * 512], c == 0, c == NCH - 1)
                last = mm(stB[tt][0:1, :], ones_col[:, 0:1],
                          sq_scr[c % 2][:, tt * 512:(tt + 1) * 512],
                          c == 0, c == NCH - 1,
                          inc="M" if tt == 3 else None)
            B_sq[c % 2].written("V", vs)
            B_sq[c % 2].read_by("M", last)
        for b in kA + kB:
            b.written("M", em.cnt["M"])
        uv = None
        for tt in range(4):
            kA[tt].wait_write("scalar")
            kB[tt].wait_write("vector")
            B_lnsl.wait_reads("scalar")
            em.op("scalar", lambda s, st=stA[tt]: s.activation(
                m_sl[:], st[0:1, :], AF.Identity, scale=f32(1.0 / D)))
            sm2 = em.op("scalar", lambda s: s.activation(
                t_sl[:], m_sl[:], AF.Square), inc="S")
            kA[tt].read_by("S", sm2)
            em.wait("vector", "S", sm2)
            B_lnsl.wait_reads("vector")
            vv = em.op("vector", lambda v, st=stB[tt]: v.scalar_tensor_tensor(
                var_sl[:], st[0:1, :], 1.0 / D, t_sl[:], ALU.mult,
                ALU.subtract), inc="V")
            kB[tt].read_by("V", vv)
            em.wait("scalar", "V", vv)
            em.op("scalar", lambda s: s.activation(
                t_sl[:], var_sl[:], AF.Ln, bias=f32(EPS)))
            sv = em.op("scalar", lambda s: s.activation(
                a_sl[:], t_sl[:], AF.Exp, scale=f32(-0.5)), inc="S")
            em.wait("vector", "S", sv)
            em.op("vector", lambda v: v.tensor_copy(abf_sl[:], a_sl[:]))
            vb = em.op("vector", lambda v: v.scalar_tensor_tensor(
                bbf_sl[:], m_sl[:], -1.0, a_sl[:], ALU.mult, ALU.mult),
                inc="V")
            B_lnsl.written("V", vb)
            B_lnsl.read_by("V", vb)
            em.wait("tensor", "V", vb)
            kA[tt].wait_reads("tensor")
            kB[tt].wait_reads("tensor")
            mm(stA[tt], ones_row[0:1, 0:128], abf_sl[0:1, :], True, True)
            mab = mm(stB[tt], ones_row[0:1, 0:128], bbf_sl[0:1, :], True, True,
                     inc="M")
            kA[tt].written("M", mab)
            kB[tt].written("M", mab)
            B_lnsl.read_by("M", mab)
            kA[tt].wait_write("scalar")
            kB[tt].wait_write("scalar")
            B_ab[tt % 2].wait_reads("scalar")
            em.op("scalar", lambda s, tt=tt: s.copy(abuf[tt % 2][:], stA[tt]))
            sab = em.op("scalar", lambda s, tt=tt: s.copy(bbuf[tt % 2][:], stB[tt]),
                        inc="S")
            kA[tt].read_by("S", sab)
            kB[tt].read_by("S", sab)
            B_ab[tt % 2].written("S", sab)
            em.wait("vector", "S", sab)
            for c in range(NCH):
                u_ap = UB[:, c * T + tt * 512: c * T + (tt + 1) * 512]
                em.op("vector", lambda v, u=u_ap, tt=tt: v.tensor_mul(
                    u, u, abuf[tt % 2][:]))
                uv = em.op("vector", lambda v, u=u_ap, tt=tt: v.tensor_add(
                    u, u, bbuf[tt % 2][:]), inc="V")
            B_ab[tt % 2].read_by("V", uv)
        B_ub.written("V", uv)

    def emit_routing():
        wr_r = wr_d.rearrange("(c p) e -> p c e", p=128)
        B_wrb.wait_reads("gpsimd")
        cw = em.op("gpsimd", lambda g: g.dma_start(
            wrb[:].rearrange("p (c e) -> p c e", c=NCH), wr_r), inc="dW", incby=16)
        B_wrb.written("dW", cw)
        B_ub.wait_write("tensor")
        B_wrb.wait_write("tensor")
        se = None
        for tt in range(4):
            lgb = pk(3, tt % 2)
            lga = pb(3, tt % 2)
            lgb.wait_reads("tensor")
            mq = None
            for c in range(NCH):
                mq = mm(lga[0:8, :], wrb[:, c * 8:(c + 1) * 8],
                        UB[:, c * T + tt * 512: c * T + (tt + 1) * 512],
                        c == 0, c == NCH - 1, inc="M" if c == NCH - 1 else None)
            lgb.written("M", mq)
            B_ub.read_by("M", mq)
            B_wrb.read_by("M", mq)
            lgb.wait_write("scalar")
            B_logb.wait_reads("scalar")
            se = em.op("scalar", lambda s, tt=tt, lga=lga: s.copy(
                LOGB[0:8, tt * 512:(tt + 1) * 512], lga[0:8, :]), inc="S")
            lgb.read_by("S", se)
        B_logb.written("S", se)
        B_logb.wait_write("tensor")
        pk(2, 0).wait_reads("tensor")
        mlt = None
        for tch in range(16):
            mlt = mm(pb(2, 0)[:, tch * 8:(tch + 1) * 8],
                     LOGB[0:8, tch * 128:(tch + 1) * 128], ident_bf[0:8, 0:8],
                     True, True, inc="M" if tch == 15 else None)
        pk(2, 0).written("M", mlt)
        B_logb.read_by("M", mlt)
        pk(2, 0).wait_write("vector")
        B_rt.wait_reads("vector")
        vl = em.op("vector", lambda v: v.tensor_copy(lt_sb[:], pb(2, 0)[:, 0:128]),
                   inc="V")
        pk(2, 0).read_by("V", vl)
        lt3 = lt_sb[:].rearrange("p (g e) -> p g e", e=8)
        rs3 = rs_sb[:].rearrange("p (g e) -> p g e", e=8)
        pe3 = pe_tm[:].rearrange("p (g e) -> p g e", e=8)
        m1b = max1[:].unsqueeze(2).to_broadcast([128, 16, 8])
        idb = idm[:].unsqueeze(2).to_broadcast([128, 16, 8])
        t1b = t1_sb[:].unsqueeze(2).to_broadcast([128, 16, 8])
        io3 = iota8[:].unsqueeze(1).to_broadcast([128, 16, 8])
        em.wait("vector", "V", c_consts)
        vh(lambda v: v.tensor_reduce(max1[:], lt3, AX.X, ALU.max))
        vh(lambda v: v.tensor_tensor(rs3, lt3, m1b, ALU.is_equal))
        vh(lambda v: v.scalar_tensor_tensor(
            rs3, rs3, -8.0, io3, ALU.mult, ALU.add))
        vh(lambda v: v.tensor_reduce(idm[:], rs3, AX.X, ALU.min))
        vh(lambda v: v.tensor_tensor(pe3, rs3, idb, ALU.is_equal))
        vh(lambda v: v.scalar_tensor_tensor(
            rs3, pe3, -1e30, lt3, ALU.mult, ALU.add))
        vh(lambda v: v.tensor_reduce(max2[:], rs3, AX.X, ALU.max))
        vg = vh(lambda v: v.tensor_sub(max2[:], max2[:], max1[:]))
        em.wait("scalar", "V", vg)
        sg = em.op("scalar", lambda s: s.activation(max2[:], max2[:], AF.Exp),
                   inc="S")
        em.wait("vector", "S", sg)
        vh(lambda v: v.tensor_scalar(
            max2[:], max2[:], 1.0, None, ALU.add))
        vh(lambda v: v.reciprocal(t1_sb[:], max2[:]))
        vh(lambda v: v.tensor_scalar(
            t1_sb[:], t1_sb[:], 1.0, None, ALU.is_ge))
        vh(lambda v: v.tensor_tensor(pe3, pe3, t1b, ALU.mult))
        vh(lambda v: v.tensor_scalar(
            pe_tm[:], pe_tm[:], -1.0, 1.0, ALU.mult, ALU.add))
        vp = vh(lambda v: v.tensor_copy(pe_bf[:], pe_tm[:]))
        B_rt.written("V", vp)
        em.wait("tensor", "V", vp)
        se2 = None
        for tch in range(16):
            pk(2, 1).wait_reads("tensor")
            mtb = mm(pb(2, 1)[0:8, 0:128], pe_bf[:, tch * 8:(tch + 1) * 8],
                     ident_bf[:, 0:128], True, True, inc="M")
            pk(2, 1).written("M", mtb)
            B_rt.read_by("M", mtb)
            pk(2, 1).wait_write("scalar")
            if tch == 0:
                B_inegb.wait_reads("scalar")
            se2 = em.op("scalar", lambda s, tch=tch: s.copy(
                INEGB[0:8, tch * 128:(tch + 1) * 128], pb(2, 1)[0:8, 0:128]),
                inc="S")
            pk(2, 1).read_by("S", se2)
        B_inegb.written("S", se2)


    def emit_ffn(ffn_idx, emit_out_tail=False, max_groups=3, max_th=2,
                 skip_routing=False):
        if not skip_routing:
            emit_routing()
        eseq = [8, 0, 1, 2, 3, 4, 5, 6, 7]
        w1_stream = [(th, e) for th in range(max_th)
                     for e in eseq[:3 * max_groups]]

        def load_w1(idx):
            th, e = w1_stream[idx]
            slot = idx % 2
            B_w1[slot].wait_reads("gpsimd")
            src = (ws1_d if e == 8 else w1_d[e]).rearrange("(c p) f -> p c f", p=128)
            dst = W1S[slot].rearrange("p (c f) -> p c f", c=NCH)
            cw = em.op("gpsimd", lambda g, s=src, d=dst: g.dma_start(d, s),
                       inc="dW", incby=16)
            B_w1[slot].written("dW", cw)

        def load_w2_group(th, gi):
            for k in range(3):
                e = eseq[3 * gi + k]
                B_w2[k].wait_reads("gpsimd")
                src = (ws2_d if e == 8 else w2_d[e]).rearrange(
                    "(c p) f -> p c f", p=128)
                dst = W2S[k].rearrange("p (c f) -> p c f", c=4)
                cw = em.op("gpsimd", lambda g, s=src, d=dst: g.dma_start(d, s),
                           inc="dW", incby=16)
                B_w2[k].written("dW", cw)

        load_w1(0)
        load_w1(1)
        load_w2_group(0, 0)
        for th in range(max_th):
            for gi in range(max_groups):
                grp = eseq[3 * gi:3 * gi + 3]
                for k, e in enumerate(grp):
                    idx = th * 3 * max_groups + 3 * gi + k
                    w1slot = idx % 2
                    hs_ap = HS[k]
                    B_w1[w1slot].wait_write("tensor")
                    B_ub.wait_write("tensor")
                    use_mask = (e != 8 and not skip_routing)
                    vh = None
                    for f in range(4):
                        pB = pk2(f % 2)
                        pA = pb2(f % 2)
                        pB.wait_reads("tensor")
                        for dc in range(NCH):
                            lhs = W1S[w1slot][:, dc * 512 + f * 128:
                                              dc * 512 + (f + 1) * 128]
                            for nn in range(2):
                                lastw1 = (dc == NCH - 1 and not use_mask)
                                cmz = mm(pA[:, nn * 512:(nn + 1) * 512], lhs, UB[:, dc * T + th * 1024 + nn * 512:
                                         dc * T + th * 1024 + (nn + 1) * 512],
                                   dc == 0, lastw1,
                                   inc="M" if (lastw1 and nn == 1) else None)
                        if use_mask:
                            B_inegb.wait_write("tensor")
                            for nn in range(2):
                                cmz = mm(pA[:, nn * 512:(nn + 1) * 512],
                                         maskmat[0:8, e * 128:(e + 1) * 128],
                                         INEGB[0:8, th * 1024 + nn * 512:
                                               th * 1024 + (nn + 1) * 512],
                                         False, True, inc="M" if nn == 1 else None)
                        pB.written("M", cmz)
                        if use_mask:
                            B_inegb.read_by("M", cmz)
                        B_w1[w1slot].read_by("M", cmz)
                        B_ub.read_by("M", cmz)
                        pB.wait_write("scalar")
                        B_rb[f % 2].wait_reads("scalar")
                        sr = em.op("scalar", lambda s, pA=pA, f=f: s.activation(
                            RBUF[f % 2][:], pA, AF.Relu), inc="S")
                        pB.read_by("S", sr)
                        B_rb[f % 2].written("S", sr)
                        em.wait("vector", "S", sr)
                        B_hs[k].wait_reads("vector")
                        vh = em.op("vector", lambda v, hs=hs_ap, f=f: v.tensor_mul(
                            hs[:, f * 1024:(f + 1) * 1024],
                            RBUF[f % 2][:], RBUF[f % 2][:]), inc="V")
                        B_rb[f % 2].read_by("V", vh)
                    B_hs[k].written("V", vh)
                    if idx + 2 < len(w1_stream):
                        load_w1(idx + 2)
                for k in range(3):
                    B_w2[k].wait_write("tensor")
                    B_hs[k].wait_write("tensor")
                for j in range(NCH):
                    for nn in range(2):
                        pB = pk(2, (2 * j + nn) % 2)
                        pA = pb(2, (2 * j + nn) % 2)
                        pB.wait_reads("tensor")
                        cmb = None
                        for k in range(3):
                            for f in range(4):
                                lhs = W2S[k][:, f * 1024 + j * 128:
                                             f * 1024 + (j + 1) * 128]
                                rhs = HS[k][:, f * 1024 + nn * 512:
                                            f * 1024 + (nn + 1) * 512]
                                cmb = mm(pA, lhs, rhs, (k == 0 and f == 0),
                                         (k == 2 and f == 3),
                                         inc="M" if (k == 2 and f == 3) else None)
                        pB.written("M", cmb)
                        for k in range(3):
                            B_w2[k].read_by("M", cmb)
                            B_hs[k].read_by("M", cmb)
                        pB.wait_write("vector")
                        B_x[j].wait_write("vector")
                        x_ap = x_sb[:, j * T + th * 1024 + nn * 512:
                                    j * T + th * 1024 + (nn + 1) * 512]
                        vx = em.op("vector", lambda v, pA=pA, x_ap=x_ap:
                                   v.scalar_tensor_tensor(
                                       x_ap, pA, 0.5, x_ap, ALU.mult, ALU.add),
                                   inc="V")
                        pB.read_by("V", vx)
                        B_x[j].written("V", vx)
                        if emit_out_tail and th == 1 and gi == 2:
                            if nn == 1:
                                emit_out_chunk(j)
                if not (th == 1 and gi == 2):
                    ngi, nth = (gi + 1, th) if gi < 2 else (0, th + 1)
                    load_w2_group(nth, ngi)

    out_r = out_d.rearrange("(c p) t -> c p t", p=128)
    odma = []

    def emit_out_chunk(c):
        B_x[c].wait_write("sync")
        co = em.op("sync", lambda sy, c=c: sy.dma_start(
            out_r[c], x_sb[:, c * T:(c + 1) * T]), inc="dX", incby=16)
        odma.append(co)

    def finish_output():
        em.wait("sync", "dX", em.cnt["dX"])

    def emit_output_from_ub():
        B_ub.wait_write("gpsimd")
        for c in range(NCH):
            co = em.op("gpsimd", lambda g, c=c: g.dma_start(
                out_r[c], UB[:, c * T:(c + 1) * T]), inc="dG", incby=16)
        em.wait("gpsimd", "dG", em.cnt["dG"])

    def load_w(w_dram, wl_ap):
        B_wl.wait_reads("gpsimd")
        src = w_dram.rearrange("(c p) d -> p c d", p=128)
        dst = wl_ap.rearrange("p (c d) -> p c d", c=NCH)
        cw = em.op("gpsimd", lambda g, s=src, d=dst: g.dma_start(d, s),
                   inc="dW", incby=16)
        B_wl.written("dW", cw)

    def load_and_transpose(w_dram, wl_ap, wt_ap, skip_load=False):
        if not skip_load:
            load_w(w_dram, wl_ap)
        B_wl.wait_write("tensor")
        B_wt.wait_reads("tensor")
        B_wt.wait_reads("scalar")
        sq_ = None
        for dc in range(NCH):
            for icq in range(2):
                pB = pk(3, icq)
                pA = pb(3, icq)
                pB.wait_reads("tensor")
                mq = None
                for ici in range(4):
                    ic = icq * 4 + ici
                    lhs = wl_ap[:, ic * 1024 + dc * 128: ic * 1024 + (dc + 1) * 128]
                    mq = mm(pA[:, ici * 128:(ici + 1) * 128], lhs, ident_bf[:],
                            True, True, inc="M" if ici == 3 else None)
                pB.written("M", mq)
                B_wl.read_by("M", mq)
                pB.wait_write("vector")
                sq_ = em.op("vector", lambda v, pA=pA, dc=dc, icq=icq:
                            v.tensor_copy(
                    wt_ap[:, dc * 1024 + icq * 512: dc * 1024 + (icq + 1) * 512],
                    pA), inc="V")
                pB.read_by("V", sq_)
        B_wt.written("V", sq_)

    def proj_fm(wt_ap, dst_fn, bdst):
        B_wt.wait_write("tensor")
        B_ub.wait_write("tensor")
        bdst.wait_reads("tensor")
        bdst.wait_reads("scalar")
        last_s = None
        groups = [[(3, 0), (3, 1), (2, 0), (2, 1)], [(1, 0), (1, 1), (0, 0), (0, 1)]]
        for ic in range(NCH):
            tiles = groups[ic % 2]
            for (q, hq) in tiles:
                pk(q, hq).wait_reads("tensor")
            mq = None
            for dc in range(NCH):
                lhs = wt_ap[:, dc * 1024 + ic * 128: dc * 1024 + (ic + 1) * 128]
                for tt in range(4):
                    q, hq = tiles[tt]
                    rhs = UB[:, dc * T + tt * 512: dc * T + (tt + 1) * 512]
                    mq = mm(pb(q, hq), lhs, rhs, dc == 0, dc == NCH - 1,
                            inc="M" if (dc == NCH - 1 and tt == 3) else None)
            B_wt.read_by("M", mq)
            B_ub.read_by("M", mq)
            for tt in range(4):
                q, hq = tiles[tt]
                pk(q, hq).written("M", mq)
                pk(q, hq).wait_write("vector")
                last_s = em.op("vector", lambda v, q=q, hq=hq, ic=ic, tt=tt:
                               v.tensor_copy(
                    dst_fn(ic)[:, tt * 512:(tt + 1) * 512], pb(q, hq)), inc="V")
                pk(q, hq).read_by("V", last_s)
        bdst.written("V", last_s)

    def proj_tm(wt_ap, dst_ap, bdst):
        B_wt.wait_write("tensor")
        B_ub.wait_write("tensor")
        bdst.wait_reads("tensor")
        bdst.wait_reads("scalar")
        last_s = None
        for tch in range(16):
            q = 3 if tch % 2 == 0 else 2
            pB = pk2(q)
            pA = pb2(q)
            pB.wait_reads("tensor")
            mq = None
            for dc in range(NCH):
                lhs = UB[:, dc * T + tch * 128: dc * T + (tch + 1) * 128]
                for ih in range(2):
                    rhs = wt_ap[:, dc * 1024 + ih * 512: dc * 1024 + (ih + 1) * 512]
                    mq = mm(pA[:, ih * 512:(ih + 1) * 512], lhs, rhs,
                            dc == 0, dc == NCH - 1,
                            inc="M" if (dc == NCH - 1 and ih == 1) else None)
            pB.written("M", mq)
            B_wt.read_by("M", mq)
            B_ub.read_by("M", mq)
            pB.wait_write("vector")
            last_s = em.op("vector", lambda v, pA=pA, tch=tch: v.tensor_copy(
                dst_ap[:, tch * 1024:(tch + 1) * 1024], pA), inc="V")
            pB.read_by("V", last_s)
        bdst.written("V", last_s)

    def emit_norms(src_fn, bsrc, dst_sl, bdst):
        psn = pb(2, 0)[:, 0:256]
        pk(2, 0).wait_reads("tensor")
        bsrc.wait_write("vector")
        mq = None
        for c in range(NCH):
            for hb in range(4):
                B_nsq.wait_reads("vector")
                vn = em.op("vector", lambda v, c=c, hb=hb: v.tensor_mul(
                    A_NSQ[:], src_fn(c)[:, hb * 512:(hb + 1) * 512],
                    src_fn(c)[:, hb * 512:(hb + 1) * 512]), inc="V")
                B_nsq.written("V", vn)
                em.wait("tensor", "V", vn)
                for tq in range(4):
                    tch = hb * 4 + tq
                    mq = mm(psn[:, tch * 16 + 2 * c: tch * 16 + 2 * c + 2],
                            A_NSQ[:, tq * 128:(tq + 1) * 128], pairsel[:],
                            True, True,
                            inc="M" if tq == 3 else None)
                B_nsq.read_by("M", mq)
        pk(2, 0).written("M", mq)
        pk(2, 0).wait_write("scalar")
        bdst.wait_reads("scalar")
        last = None
        for tch in range(16):
            last = em.op("scalar", lambda s, tch=tch: s.activation(
                dst_sl[:, tch * H:(tch + 1) * H], psn[:, tch * 16:tch * 16 + 16],
                AF.Exp, scale=f32(-0.5), bias=f32(-LNSM)), inc="S")
        pk(2, 0).read_by("S", last)
        bdst.written("S", last)

    def trig_pipeline(pja, pjb, width, sin_dst, cos_dst):
        if width > 512:
            sl = 0 if trig_slot[0] in (0, 3) else 2
            trig_slot[0] = (sl + 2) % NTRIG
        else:
            sl = trig_slot[0]
            trig_slot[0] = (sl + 1) % NTRIG
        ns, ds, isc = nscr[sl], dscr[sl], iscr[sl]
        Bt = B_trig[sl]
        Bt.wait_reads("vector")
        pjb.wait_write("vector")
        for c0 in range(0, width, 512):
            em.op("vector", lambda v, pja=pja, c0=c0, isc=isc: v.tensor_scalar(
                isc[:, 0:512], pja[:, c0:c0 + 512], 8.5, None, ALU.add))
            em.op("vector", lambda v, pja=pja, c0=c0, ds=ds, isc=isc:
                  v.scalar_tensor_tensor(
                ds[:, c0:c0 + 512], pja[:, c0:c0 + 512], 8.5, isc[:, 0:512],
                ALU.add, ALU.subtract), inc="V")
        vd = em.op("vector", lambda v, w=width, ns=ns, ds=ds:
                   v.scalar_tensor_tensor(
            ns[:, 0:w], ds[:, 0:w], 0.5, ds[:, 0:w], ALU.is_ge,
            ALU.subtract), inc="V")
        pjb.read_by("V", vd)
        Bt.written("V", vd)
        em.wait("scalar", "V", vd)
        em.op("scalar", lambda s, sd=sin_dst, w=width, ns=ns: s.activation(
            sd, ns[:, 0:w], AF.Sin, scale=f32(-SCALE_2PI)))
        em.op("scalar", lambda s, w=width, ns=ns, ds=ds: s.activation(
            ds[:, 0:w], ns[:, 0:w], AF.Abs))
        st = em.op("scalar", lambda s, cd=cos_dst, w=width, ds=ds: s.activation(
            cd, ds[:, 0:w], AF.Sin, scale=f32(-SCALE_2PI), bias=f32(PI_F / 2)),
            inc="S")
        Bt.read_by("S", st)
        return st

    def emit_attention(stop_after=None):
        B_hs[0].wait_reads("gpsimd")
        B_hs[1].wait_reads("gpsimd")
        load_w(wv_d, A_WL)
        emit_layernorm()
        barrier("attn-start")
        load_and_transpose(wv_d, A_WL, A_WT, skip_load=True)
        load_w(wk_d, A_WL)
        proj_tm(A_WT, A_V, B_v)
        if stop_after == "v":
            return
        barrier("pre-k")
        load_and_transpose(wk_d, A_WL, A_WT, skip_load=True)
        proj_fm(A_WT, A_K, B_k)
        if stop_after == "k":
            return
        emit_norms(A_K, B_k, sk_tm, B_sk)
        if stop_after == "norms":
            return
        B_sk.wait_write("vector")
        B_skb.wait_reads("vector")
        vsk = em.op("vector", lambda v: v.tensor_copy(skb_tm[:], sk_tm[:]), inc="V")
        B_sk.read_by("V", vsk)
        B_skb.written("V", vsk)

        passes = [(0, 1), (2, 3), (4, 5), (6, 7)]
        pending_drain = [None, None]
        for pi, prs in enumerate(passes):
            npr = 2
            kvmap = [(0, 0), (0, 1), (1, 0), (1, 1)]
            kvb = [pk(*kvmap[k]) for k in range(4)]
            kva = [pb(*kvmap[k])[0:65, :] for k in range(4)]
            B_v.wait_write("tensor")
            B_skb.wait_write("vector")
            B_rfT.wait_write("tensor")
            h0 = 2 * prs[0]
            nh = 4
            mkv = [None] * 4

            def emit_vaug_proj_trig(tch):
                vsl = tch % 2
                B_vas[vsl].wait_reads("vector")
                B_v.wait_write("vector")
                vdst = A_VAS[vsl][:, 0:65 * nh]
                v3 = vdst.rearrange("p (h q) -> p h q", h=nh)[:, :, 0:64]
                s3 = skb_tm[:, tch * H + h0: tch * H + h0 + nh]\
                    .unsqueeze(2).to_broadcast([128, nh, 64])
                src3 = A_V[:, tch * 1024 + h0 * 64: tch * 1024 + (h0 + nh) * 64]\
                    .rearrange("p (h q) -> p h q", h=nh)
                em.op("vector", lambda v, a=v3, b=src3, c=s3:
                      v.tensor_tensor(a, b, c, ALU.mult))
                vcol = vdst.rearrange("p (h q) -> p h q", h=nh)[:, :, 64:65]
                vva = em.op("vector", lambda v, a=vcol, tch=tch:
                            v.tensor_copy(a, skb_tm[:, tch * H + h0:
                                                    tch * H + h0 + nh].unsqueeze(2)),
                            inc="V")
                B_vas[vsl].written("V", vva)
                em.wait("tensor", "V", vva)
                pjq = 3 if tch % 2 == 0 else 2
                pjb = pk2(pjq)
                pja = pb2(pjq)
                pjb.wait_reads("tensor")
                mpj = None
                for ip, hp in enumerate(prs):
                    lhsE = A_K(hp)[:, tch * 128:(tch + 1) * 128]
                    mpj = mm(pja[:, ip * 512:(ip + 1) * 512], lhsE, rfT[:, 0:512],
                             True, True, inc="M" if ip == 1 else None)
                B_k.read_by("M", mpj)
                pjb.written("M", mpj)
                kfsl = tch % 2
                kf = A_KF[kfsl]
                B_kf[kfsl].wait_reads("vector")
                B_kf[kfsl].wait_reads("scalar")
                st = None
                for ip in range(2):
                    kf2 = kf[:, ip * 1024:(ip + 1) * 1024]\
                        .rearrange("p (h q) -> p h q", h=2)
                    st = trig_pipeline(pb(pjq, ip), pk(pjq, ip), 512,
                                       kf2[:, :, 256:512], kf2[:, :, 0:256])
                B_kf[kfsl].written("S", st)

            def emit_kv(tch):
                vsl = tch % 2
                kfsl = tch % 2
                kf = A_KF[kfsl]
                B_kf[kfsl].wait_write("tensor")
                B_vas[vsl].wait_write("tensor")
                for k in range(4):
                    lhsV = A_VAS[vsl][:, k * 65: k * 65 + 65]
                    mkv[k] = mm(kva[k], lhsV, kf[:, k * 512:(k + 1) * 512],
                                tch == 0, tch == 15,
                                inc="M" if k == 3 else None)
                B_kf[kfsl].read_by("M", mkv[3])
                B_vas[vsl].read_by("M", mkv[3])

            drain_slots = []

            def drain_a(kvb=kvb, kva=kva, prs=prs, drain_slots=drain_slots):
                for ip2, hp2 in enumerate(prs):
                    for hh in range(2):
                        k = 2 * ip2 + hh
                        sl = trig_slot[0]
                        trig_slot[0] = (sl + 1) % NTRIG
                        nb = nscr[sl]
                        Bt = B_trig[sl]
                        kvb[k].wait_write("scalar")
                        Bt.wait_reads("scalar")
                        skv = em.op("scalar", lambda s, a=kva[k], nb=nb: s.copy(
                            nb[0:65, 0:512], a), inc="S")
                        kvb[k].read_by("S", skv)
                        Bt.written("S", skv)
                        drain_slots.append((sl, 2 * hp2 + hh))

            def drain_b(drain_slots=drain_slots):
                for ki, (sl, h) in enumerate(drain_slots):
                    nb = nscr[sl]
                    Bt = B_trig[sl]
                    psl = ki % 2
                    Bt.wait_write("tensor")
                    pk(3, psl).wait_reads("tensor")
                    mkt = None
                    for q in range(4):
                        mkt = mm(pb(3, psl)[:, q * 65:(q + 1) * 65],
                                 nb[0:65, q * 128:(q + 1) * 128],
                                 ident_bf[0:65, 0:65], True, True,
                                 inc="M" if q == 3 else None)
                    pk(3, psl).written("M", mkt)
                    Bt.read_by("M", mkt)
                    pk(3, psl).wait_write("scalar")
                    B_kv.wait_reads("scalar")
                    skv2 = em.op("scalar", lambda s, h=h, psl=psl: s.copy(
                        A_KV[:, h * 260:(h + 1) * 260], pb(3, psl)[:, 0:260]),
                        inc="S")
                    pk(3, psl).read_by("S", skv2)
                    B_kv.written("S", skv2)
                del drain_slots[:]

            if pending_drain[0] is not None:
                pending_drain[0]()
                pending_drain[1]()
            emit_vaug_proj_trig(0)
            emit_vaug_proj_trig(1)
            for b in kvb:
                b.wait_reads("tensor")
            for tch in range(16):
                if 1 <= tch < 15:
                    emit_vaug_proj_trig(tch + 1)
                emit_kv(tch)
            for k in range(4):
                kvb[k].written("M", em.cnt["M"])
            pending_drain = [drain_a, drain_b]
            if pi == 1:
                em.wait("gpsimd", "M", em.cnt["M"])
                load_w(wq_d, A_WL_Q)
        pending_drain[0]()
        pending_drain[1]()

        if stop_after == "phaseA":
            return
        barrier("pre-q")
        load_and_transpose(wq_d, A_WL_Q, A_WT_Q, skip_load=True)

        def qchunk(ic):
            return A_Q[:, ic * T:(ic + 1) * T]
        if stop_after == "q":
            proj_fm(A_WT_Q, qchunk, B_q)
            emit_norms(qchunk, B_q, sq_tm, B_sqt)
            return
        barrier("pre-sweep1")

        def qfu_ap(par, dc, lo, width):
            return A_QFU[:, par * 4096 + dc * 1024 + lo: par * 4096 + dc * 1024 + lo + width]
        B_qf2 = [Buf(em, "qf0"), Buf(em, "qf1")]
        steps = [(h, th2) for h in range(16) for th2 in range(2)]
        B_qc = [Buf(em, f"qc{ic}") for ic in range(NCH)]
        psn_q = pb(1, 1)[:, 0:256]

        def emit_qproj_tile(ic, tt):
            pB = pk(1, 0)
            pA = pb(1, 0)
            pB.wait_reads("tensor")
            B_wt.wait_write("tensor")
            B_ub.wait_write("tensor")
            mq = None
            for dc in range(NCH):
                lhs = A_WT_Q[:, dc * 1024 + ic * 128: dc * 1024 + (ic + 1) * 128]
                rhs = UB[:, dc * T + tt * 512: dc * T + (tt + 1) * 512]
                mq = mm(pA, lhs, rhs, dc == 0, dc == NCH - 1,
                        inc="M" if dc == NCH - 1 else None)
            pB.written("M", mq)
            B_wt.read_by("M", mq)
            B_ub.read_by("M", mq)
            pB.wait_write("scalar")
            sop = em.op("scalar", lambda s, ic=ic, tt=tt, pA=pA: s.copy(
                qchunk(ic)[:, tt * 512:(tt + 1) * 512], pA), inc="S")
            pB.read_by("S", sop)
            if tt == 3:
                B_qc[ic].written("S", sop)
                B_q.written("S", sop)

        def emit_qnorm_piece(c):
            B_qc[c].wait_write("vector")
            mq = None
            for hb in range(4):
                sl = trig_slot[0]
                trig_slot[0] = (sl + 1) % NTRIG
                nb = nscr[sl]
                Bt = B_trig[sl]
                Bt.wait_reads("vector")
                Bt.wait_reads("scalar")
                vn = em.op("vector", lambda v, c=c, hb=hb, nb=nb: v.tensor_mul(
                    nb[:, 0:512], qchunk(c)[:, hb * 512:(hb + 1) * 512],
                    qchunk(c)[:, hb * 512:(hb + 1) * 512]), inc="V")
                Bt.written("V", vn)
                B_qc[c].read_by("V", vn)
                em.wait("tensor", "V", vn)
                if c == 0 and hb == 0:
                    pk(1, 1).wait_reads("tensor")
                for tq in range(4):
                    tch = hb * 4 + tq
                    mq = mm(psn_q[:, tch * 16 + 2 * c: tch * 16 + 2 * c + 2],
                            nb[:, tq * 128:(tq + 1) * 128], pairsel[:],
                            True, True, inc="M" if tq == 3 else None)
                Bt.read_by("M", mq)
            if c == NCH - 1:
                pk(1, 1).written("M", mq)

        def emit_proj_trig(i):
            h, th2 = steps[i]
            hc, hr = h // 2, (h % 2) * 64
            par = th2
            pjq = 3 if i % 2 == 0 else 2
            B_qc[hc].wait_write("tensor")
            B_qf2[par].wait_reads("tensor")
            for mc in range(2):
                for nn in range(2):
                    tt = th2 * 2 + nn
                    pjb = pk(pjq, nn)
                    pja = pb(pjq, nn)
                    pjb.wait_reads("tensor")
                    lhs = rfT[:, (h % 2) * 256 + mc * 128:
                              (h % 2) * 256 + (mc + 1) * 128]
                    rhs = A_Q[:, hc * T + tt * 512: hc * T + (tt + 1) * 512]
                    mq = mm(pja, lhs, rhs, True, True, inc="M")
                    pjb.written("M", mq)
                    B_q.read_by("M", mq)
                    cosd = qfu_ap(par, mc, nn * 512, 512)
                    sind = qfu_ap(par, 2 + mc, nn * 512, 512)
                    st = trig_pipeline(pja, pjb, 512, sind, cosd)
                    B_qf2[par].written("S", st)

        def emit_qkv(i):
            h, th2 = steps[i]
            hc, hr = h // 2, (h % 2) * 64
            par = th2
            B_kv.wait_write("tensor")
            B_qf2[par].wait_write("tensor")
            qb0, qb1 = pk(0, 0), pk(0, 1)
            qb0.wait_reads("tensor")
            qb1.wait_reads("tensor")
            mq = None
            for tti in range(2):
                qa = pb(0, tti)[0:65, :]
                for dc in range(4):
                    lhs = A_KV[:, h * 260 + dc * 65: h * 260 + (dc + 1) * 65]
                    rhs = qfu_ap(par, dc, tti * 512, 512)
                    mq = mm(qa, lhs, rhs, dc == 0, dc == 3,
                            inc="M" if (dc == 3 and tti == 1) else None)
            qb0.written("M", mq)
            qb1.written("M", mq)
            B_kv.read_by("M", mq)
            B_qf2[par].read_by("M", mq)
            qb0.wait_write("vector")
            qb0.wait_write("scalar")
            if th2 == 0:
                B_zst.wait_reads("scalar")
                B_q.wait_reads("vector")
            dst = A_Q[:, hc * T + th2 * 1024:
                      hc * T + (th2 + 1) * 1024][hr:hr + 64, :]
            szv = em.op("vector", lambda v, d=dst: v.tensor_copy(
                d, PB[0][0:64, 0:1024]), inc="V")
            sz = em.op("scalar", lambda s, th2=th2: s.copy(
                A_ZST[64:65, th2 * 1024:(th2 + 1) * 1024],
                PB[0][64:65, 0:1024]), inc="S")
            qb0.read_by("V", szv)
            qb0.read_by("S", sz)
            qb1.read_by("V", szv)
            qb1.read_by("S", sz)
            if th2 == 1:
                B_zst.written("S", sz)
                B_q.written("V", szv)
                B_zfm.wait_reads("gpsimd")
                B_zst.wait_write("gpsimd")
                cz = em.op("gpsimd", lambda g, h=h: g.dma_start(
                    A_ZFM[h:h + 1, :], A_ZST[64:65, :]), inc="dG", incby=16)
                B_zst.read_by("dG", cz)
                if h == 15:
                    B_zfm.written("dG", cz)

        for ic in range(2):
            for tt in range(4):
                emit_qproj_tile(ic, tt)
            emit_qnorm_piece(ic)
        emit_proj_trig(0)
        for i in range(32):
            hc = i // 4
            if hc + 2 < NCH:
                emit_qproj_tile(hc + 2, i % 4)
                if i % 4 == 3:
                    emit_qnorm_piece(hc + 2)
            if i < 31:
                emit_proj_trig(i + 1)
            emit_qkv(i)
        pk(1, 1).wait_write("scalar")
        B_sqt.wait_reads("scalar")
        lastn = None
        for tch in range(16):
            lastn = em.op("scalar", lambda s, tch=tch: s.activation(
                sq_tm[:, tch * H:(tch + 1) * H],
                psn_q[:, tch * 16:tch * 16 + 16],
                AF.Exp, scale=f32(-0.5), bias=f32(-LNSM)), inc="S")
        pk(1, 1).read_by("S", lastn)
        B_sqt.written("S", lastn)

        if stop_after == "sweep1":
            return
        B_qfu.wait_reads("gpsimd")
        B_wl.wait_reads("gpsimd")
        src_wo = wo_d.rearrange("(c p) d -> p c d", p=128)
        dst_wo = A_WO.rearrange("p (c d) -> p c d", c=NCH)
        cwo = em.op("gpsimd", lambda g: g.dma_start(dst_wo, src_wo),
                    inc="dW", incby=16)
        B_wl.written("dW", cwo)
        B_wl.wait_write("tensor")
        B_wt.wait_reads("tensor")
        B_wt.wait_reads("scalar")
        wo_sq = [None]

        def emit_wo_piece(i):
            dc, icq = i // 2, i % 2
            pB = pk(1, icq)
            pA = pb(1, icq)
            pB.wait_reads("tensor")
            mq = None
            for ici in range(4):
                ic = icq * 4 + ici
                lhs = A_WO[:, ic * 1024 + dc * 128: ic * 1024 + (dc + 1) * 128]
                mq = mm(pA[:, ici * 128:(ici + 1) * 128], lhs, ident_bf[:],
                        True, True, inc="M" if ici == 3 else None)
            pB.written("M", mq)
            B_wl.read_by("M", mq)
            pB.wait_write("scalar")
            wo_sq[0] = em.op("scalar", lambda s, pA=pA, dc=dc, icq=icq: s.copy(
                A_WOT[:, dc * 1024 + icq * 512: dc * 1024 + (icq + 1) * 512],
                pA), inc="S")
            pB.read_by("S", wo_sq[0])

        B_zfm.wait_write("tensor")
        B_sqt.wait_write("vector")
        vw = None
        for tch in range(16):
            emit_wo_piece(tch)
            pB = pk(2, tch % 2)
            pA = pb(2, tch % 2)
            pB.wait_reads("tensor")
            mz = mm(pA[:, 0:16], A_ZFM[0:16, tch * 128:(tch + 1) * 128],
                    ident_bf[0:16, 0:16], True, True, inc="M")
            pB.written("M", mz)
            pB.wait_write("vector")
            B_wtm.wait_reads("vector")
            sqsl = sq_tm[:, tch * H:(tch + 1) * H]
            vh(lambda v, pA=pA, sq=sqsl: v.tensor_tensor(
                wsc_f[:, 0:16], sq, pA[:, 0:16], ALU.mult))
            vh(lambda v: v.tensor_scalar(
                wsc_f[:, 0:16], wsc_f[:, 0:16], 1e-6, None, ALU.max))
            vh(lambda v: v.reciprocal(wsc_f[:, 16:32], wsc_f[:, 0:16]))
            vw = vh(lambda v, sq=sqsl, tch=tch: v.tensor_tensor(
                w_tm[:, tch * H:(tch + 1) * H], sq, wsc_f[:, 16:32], ALU.mult))
            pB.read_by("V", vw)
        B_wt.written("S", wo_sq[0])
        B_wtm.written("V", vw)
        B_wtm.wait_write("tensor")
        vap = None
        for h in range(16):
            hc, hr = h // 2, (h % 2) * 64
            for tt in range(4):
                pB = pk(3, tt % 2)
                pA = pb(3, tt % 2)
                pB.wait_reads("tensor")
                mwb = None
                for tq in range(4):
                    tch = tt * 4 + tq
                    wcol = w_tm[:, tch * H + h: tch * H + h + 1]\
                        .to_broadcast([128, 64])
                    mwb = mm(pA[0:64, tq * 128:(tq + 1) * 128], wcol, ident_bf[:],
                             True, True, inc="M" if tq == 3 else None)
                pB.written("M", mwb)
                pB.wait_write("vector")
                ap_ = A_Q[:, hc * T + tt * 512:
                          hc * T + (tt + 1) * 512][hr:hr + 64, :]
                vap = em.op("vector", lambda v, a=ap_, pA=pA: v.tensor_tensor(
                    a, a, pA[0:64, :], ALU.mult), inc="V")
                pB.read_by("V", vap)
        B_q.written("V", vap)

        if stop_after == "sweep2":
            return
        B_wt.wait_write("tensor")
        B_q.wait_write("tensor")
        for jc in range(NCH):
            for tt in range(4):
                pB = pk(1, tt % 2)
                pA = pb(1, tt % 2)
                pB.wait_reads("tensor")
                mq = None
                for ic in range(NCH):
                    lhs = A_WOT[:, ic * 1024 + jc * 128: ic * 1024 + (jc + 1) * 128]
                    rhs = A_Q[:, ic * T + tt * 512: ic * T + (tt + 1) * 512]
                    mq = mm(pA, lhs, rhs, ic == 0, ic == NCH - 1,
                            inc="M" if ic == NCH - 1 else None)
                pB.written("M", mq)
                B_q.read_by("M", mq)
                B_wt.read_by("M", mq)
                pB.wait_write("vector")
                B_x[jc].wait_write("vector")
                x_ap = x_sb[:, jc * T + tt * 512: jc * T + (tt + 1) * 512]
                vx = em.op("vector", lambda v, x_ap=x_ap, pA=pA: v.tensor_add(
                    x_ap, x_ap, pA), inc="V")
                pB.read_by("V", vx)
                B_x[jc].written("V", vx)


    if mode == "ln1":
        emit_layernorm()
        emit_output_from_ub()
    elif mode == "ffn1":
        emit_layernorm()
        emit_ffn(0)
        barrier("out")
        for c in range(NCH):
            emit_out_chunk(c)
        finish_output()
    elif mode == "rt":
        emit_layernorm()
        emit_routing()
        barrier("out")
        for c in range(NCH):
            emit_out_chunk(c)
        finish_output()
    elif mode == "dbg_ineg":
        emit_layernorm()
        emit_routing()
        barrier("out")
        em.op("gpsimd", lambda g: g.dma_start(out_r[0][0:8, :], INEGB[0:8, :]),
              inc="dG", incby=16)
        em.op("gpsimd", lambda g: g.dma_start(out_r[1][0:8, :], LOGB[0:8, :]),
              inc="dG", incby=16)
        em.op("gpsimd", lambda g: g.dma_start(out_r[2][:, 0:128], lt_sb[:]),
              inc="dG", incby=16)
        em.op("gpsimd", lambda g: g.dma_start(out_r[3][:, 0:128], pe_bf[:]),
              inc="dG", incby=16)
        em.op("gpsimd", lambda g: g.dma_start(out_r[4][0:8, 0:1024], maskmat[:]),
              inc="dG", incby=16)
        em.wait("gpsimd", "dG", em.cnt["dG"])
    elif mode == "ffn_sh":
        emit_layernorm()
        emit_ffn(0, max_groups=1, max_th=2, skip_routing=True)
        barrier("out")
        for c in range(NCH):
            emit_out_chunk(c)
        finish_output()
    elif mode == "ffn_g1":
        emit_layernorm()
        emit_ffn(0, max_groups=1, max_th=2)
        barrier("out")
        for c in range(NCH):
            emit_out_chunk(c)
        finish_output()
    elif mode.startswith("attn_"):
        emit_layernorm()
        emit_ffn(0)
        emit_attention(stop_after=mode[5:])
        barrier("out")
        for c in range(NCH):
            emit_out_chunk(c)
        finish_output()
    elif mode == "attn":
        emit_layernorm()
        emit_ffn(0)
        emit_attention()
        barrier("out")
        for c in range(NCH):
            emit_out_chunk(c)
        finish_output()
    else:
        emit_layernorm()
        emit_ffn(0)
        emit_attention()
        emit_layernorm(home=LN_HOME_B)
        barrier("pre-ffn2")
        emit_ffn(1, emit_out_tail=True)
        finish_output()

    with nc.Block() as block:
        @block.tensor
        def _(t):
            for fn in em.ops["tensor"]:
                fn(t)

        @block.vector
        def _(v):
            for fn in em.ops["vector"]:
                fn(v)

        @block.scalar
        def _(s):
            for fn in em.ops["scalar"]:
                fn(s)

        @block.gpsimd
        def _(g):
            for fn in em.ops["gpsimd"]:
                fn(g)

        @block.sync
        def _(sy):
            for fn in em.ops["sync"]:
                fn(sy)

    ctx.close()
    return nc


_NC_CACHE = {}


def get_nc(mode="full"):
    if mode not in _NC_CACHE:
        _NC_CACHE[mode] = build_program(mode)
    return _NC_CACHE[mode]


def make_in_maps(L0_attn_x, L0_attn_wq, L0_attn_wk, L0_attn_wv, L0_attn_wo,
                 L0_attn_rf, L0_ffn_Ws1, L0_ffn_Ws2, L0_ffn_W1, L0_ffn_W2,
                 L0_ffn_Wr):
    cst = np.zeros((128, 1152), dtype=np.float32)
    cst[:, 0:128] = np.eye(128, dtype=np.float32)
    for e in range(8):
        cst[e, 128 + e * 128:128 + (e + 1) * 128] = -BIG
    shared = {
        "wq": np.ascontiguousarray(L0_attn_wq, dtype=np.float32),
        "wk": np.ascontiguousarray(L0_attn_wk, dtype=np.float32),
        "wv": np.ascontiguousarray(L0_attn_wv, dtype=np.float32),
        "wo": np.ascontiguousarray(L0_attn_wo, dtype=np.float32),
        "rf": np.ascontiguousarray(L0_attn_rf, dtype=np.float32),
        "ws1": np.ascontiguousarray(L0_ffn_Ws1, dtype=np.float32),
        "ws2": np.ascontiguousarray(L0_ffn_Ws2, dtype=np.float32),
        "w1": np.ascontiguousarray(L0_ffn_W1, dtype=np.float32),
        "w2": np.ascontiguousarray(L0_ffn_W2, dtype=np.float32),
        "wr": np.ascontiguousarray(L0_ffn_Wr, dtype=np.float32),
        "cst": cst,
    }
    in_maps = []
    for b in range(NCORES):
        m = dict(shared)
        m["x"] = np.ascontiguousarray(L0_attn_x[b].T, dtype=np.float32)
        in_maps.append(m)
    return in_maps


def kernel(**inputs):
    nc = get_nc("full")
    in_maps = make_in_maps(**inputs)
    res = run_bass_kernel_spmd(nc, in_maps, core_ids=list(range(NCORES)))
    out = np.stack([np.ascontiguousarray(res.results[b]["out"].T)
                    for b in range(NCORES)], axis=0)
    return out
